# version 01
# baseline (speedup 1.0000x reference)
import contextlib
import numpy as np
import ml_dtypes as md

import concourse.bass as bass
import concourse.mybir as mybir
from concourse.bass_utils import run_bass_kernel_spmd

F32 = mybir.dt.float32
BF16 = mybir.dt.bfloat16
I32 = mybir.dt.int32
F16 = mybir.dt.float16
AF = mybir.ActivationFunctionType
ALU = mybir.AluOpType
AX = mybir.AxisListType

D = 2048
T = 2048
BLK = 512
NB = T // BLK
DT = D // 128
N_CORES = 8

D_INNER = 4096
D_STATE = 16
D_HEAD = 256
PROJ_PAD = 8832
HID = 1024
LAT = 512
NE = 16
MOE_SCALE = 2.5
I_FF = 5632
GU = 2 * I_FF
NHEADS = 16
HDIM = 128

LN_EPS = 1e-5
RMS_EPS_MAMBA = 1.1920929e-07
RMS_EPS_L1FFN = 1e-6

CONSTV_VALUES = [-4.0, -12.0, 0.5, -6.0]



def _mmtile(w):
    out_f, in_f = w.shape
    assert out_f % 128 == 0 and in_f % 128 == 0, (out_f, in_f)
    ot, kc = out_f // 128, in_f // 128
    h = w.reshape(ot, 128, kc, 128).transpose(0, 3, 2, 1)
    return np.ascontiguousarray(h.reshape(ot * 128, in_f))


def _sbimg(w, part=128):
    r, c = w.shape
    k = r // part
    return np.ascontiguousarray(
        w.reshape(k, part, c).transpose(1, 0, 2).reshape(part, k * c))


def mkap(ap, newdims):
    return bass.AP(ap.tensor, ap.offset, newdims)


def _host_consts():
    f32 = np.float32
    c = {}
    c["idn_bf"] = np.eye(128, dtype=f32).astype(md.bfloat16)
    c["idn_f32"] = np.eye(128, dtype=f32)
    c["ones_col_bf"] = np.ones((128, 1), f32).astype(md.bfloat16)
    c["ones_row_bf"] = np.ones((1, 128), f32).astype(md.bfloat16)
    c["ones_row_f32"] = np.ones((1, 128), f32)
    pat = np.zeros((32, 128, 16), f32)
    for i in range(32):
        pat[i, :, i >> 1] = 1.0 / D_HEAD
    c["xmean_pat_bf"] = np.ascontiguousarray(
        pat.transpose(1, 0, 2).reshape(128, 512)).astype(md.bfloat16)
    rep = np.zeros((16, 256), f32)
    for h in range(16):
        rep[h, 16 * h:16 * h + 16] = 1.0
    c["hrep_pat"] = rep
    col = np.zeros((256, 16), f32)
    for h in range(16):
        col[16 * h:16 * h + 16, h] = 1.0
    c["hcol_pat"] = _sbimg(col)
    sel = np.zeros((16, 16, 128), f32)
    for e in range(16):
        sel[e, e, :] = 1.0
    c["sel16_bf"] = np.ascontiguousarray(
        sel.transpose(1, 0, 2).reshape(16, 2048)).astype(md.bfloat16)
    inv = 1.0 / (10000.0 ** (np.arange(0, D, 2, dtype=np.float64) / D))
    tpos = np.arange(T, dtype=np.float64)
    emb = np.concatenate([np.outer(inv, tpos)] * 2, axis=0)
    c["rope_cos"] = np.cos(emb).astype(f32)
    c["rope_sin"] = np.sin(emb).astype(f32)
    return c


def prep_weights(i):
    w = {}
    f32 = np.float32
    W_in = np.asarray(i["L0_attn_W_in"], f32)
    pad = np.zeros((112, D), f32)
    W_in_pad = np.concatenate([W_in[0:4096], W_in[4096:8192], W_in[8192:8208],
                               pad, W_in[8208:8464], W_in[8464:8720]], 0)
    w["w_inT"] = _mmtile(W_in_pad)
    w["w_outT"] = _mmtile(np.asarray(i["L0_attn_W_out"], f32))
    w["w_h2iT"] = np.ascontiguousarray(np.asarray(i["L0_attn_W_h2i"], f32).T)
    w["w_norm_cols"] = np.ascontiguousarray(
        np.asarray(i["L0_attn_w_norm"], f32).reshape(DT, 128).T)
    w["w_conv_cols"] = np.ascontiguousarray(
        np.asarray(i["L0_attn_w_conv"], f32)[:, 0, :].reshape(32, 128, 4)
        .transpose(1, 0, 2).reshape(128, 128))
    w["dt_bias"] = np.asarray(i["L0_attn_dt_bias"], f32).reshape(16, 1)
    w["a_log"] = np.asarray(i["L0_attn_A_log"], f32).reshape(16, 1)

    sw2t = _mmtile(np.asarray(i["L0_ffn_sw2"], f32))
    wupt = _mmtile(np.asarray(i["L0_ffn_Wup"], f32))
    w["comb_img"] = np.ascontiguousarray(np.concatenate([sw2t, wupt], 1))
    w["sw1T"] = _mmtile(np.asarray(i["L0_ffn_sw1"], f32))
    w["swgT"] = _mmtile(np.asarray(i["L0_ffn_swg"], f32))
    w["wdownT"] = _mmtile(np.asarray(i["L0_ffn_Wdown"], f32))
    w["wg_sb"] = _sbimg(np.ascontiguousarray(np.asarray(i["L0_ffn_Wg"], f32).T))
    rw1 = np.asarray(i["L0_ffn_rw1"], f32)
    rwg = np.asarray(i["L0_ffn_rwg"], f32)
    rw2 = np.asarray(i["L0_ffn_rw2"], f32)
    imgs = []
    for e in range(NE):
        imgs.append(np.concatenate([_sbimg(_mmtile(rw1[e])),
                                    _sbimg(_mmtile(rwg[e])),
                                    _sbimg(_mmtile(rw2[e]))], 1))
    w["rw_img"] = np.ascontiguousarray(np.stack(imgs).reshape(NE * 128, 12288))
    w["moe_bias"] = np.asarray(i["L0_ffn_biases"], f32).reshape(1, 16)

    for nm, key in (("wqT", "L1_attn_Wq"), ("wkT", "L1_attn_Wk"),
                    ("wvT", "L1_attn_Wv"), ("woT", "L1_attn_Wo")):
        w[nm] = _mmtile(np.asarray(i[key], f32))
    w["attn_gate"] = np.asarray(i["L1_attn_gate"], f32).reshape(1, 16)

    w["gu_packed"] = np.ascontiguousarray(np.asarray(i["L1_ffn_gate_up_packed"]))
    w["gu_scales_t"] = np.ascontiguousarray(
        np.asarray(i["L1_ffn_gate_up_scales"]).T)
    w["dn_packed"] = np.ascontiguousarray(np.asarray(i["L1_ffn_down_packed"]))
    w["dn_scales_t"] = np.ascontiguousarray(
        np.asarray(i["L1_ffn_down_scales"]).T)
    w.update(_host_consts())
    cv = np.zeros((128, 16), np.float32)
    cv[:, :len(CONSTV_VALUES)] = np.asarray(CONSTV_VALUES, np.float32)[None, :]
    w["constv"] = cv
    return w


WEIGHT_SPECS = {
    "w_inT": ((PROJ_PAD, D), F32), "w_outT": ((D, D_INNER), F32),
    "w_h2iT": ((16, D_INNER), F32), "w_norm_cols": ((128, 16), F32),
    "w_conv_cols": ((128, 128), F32), "dt_bias": ((16, 1), F32),
    "a_log": ((16, 1), F32),
    "comb_img": ((D, 1536), F32), "sw1T": ((HID, D), F32),
    "swgT": ((HID, D), F32), "wdownT": ((LAT, D), F32),
    "wg_sb": ((128, 256), F32), "rw_img": ((NE * 128, 12288), F32),
    "moe_bias": ((1, 16), F32),
    "wqT": ((D, D), F32), "wkT": ((D, D), F32), "wvT": ((D, D), F32),
    "woT": ((D, D), F32), "attn_gate": ((1, 16), F32),
    "gu_packed": ((GU, 256), I32), "gu_scales_t": ((GU, 16), F16),
    "dn_packed": ((D, 704), I32), "dn_scales_t": ((D, 44), F16),
    "idn_bf": ((128, 128), BF16), "idn_f32": ((128, 128), F32),
    "ones_col_bf": ((128, 1), BF16), "ones_row_bf": ((1, 128), BF16),
    "ones_row_f32": ((1, 128), F32), "xmean_pat_bf": ((128, 512), BF16),
    "hrep_pat": ((16, 256), F32), "hcol_pat": ((128, 32), F32),
    "sel16_bf": ((16, 2048), BF16),
    "rope_cos": ((D, T), F32), "rope_sin": ((D, T), F32),
    "constv": ((128, 16), F32),
}

SCRATCH_SPECS = {
    "resid": ((D, T), F32),
    "zbuf": ((D_INNER, T), F32),
    "r1buf": ((D, T), F32),
    "vbuf": ((D, T), F32),
    "dtbc": ((5 * 128, T), F32),
    "wgu_scr": ((GU, D), BF16),
    "wdn_scr": ((D, I_FF), BF16),
    "hbuf": ((I_FF, T), BF16),
    "qspill": ((D, T), BF16),
    "kspill": ((D, T), BF16),
    "qfspill": ((D, T), BF16),
    "kfspill": ((D, T), BF16),
    "vtok": ((T, D), BF16),
    "combspill": ((D, T), BF16),
}



class Sem:
    __slots__ = ("h", "val", "name")

    def __init__(self, handle, name):
        self.h = handle
        self.val = 0
        self.name = name

    def inc(self, inst, n=1):
        inst.then_inc(self.h, n)
        self.val += n
        return self.val


ENG_SEM = {"sync": "ld", "gpsimd": "cast", "tensor": "pe", "scalar": "act",
           "vector": "dve"}


class Plan:
    def __init__(self, nc, ctx):
        self.nc = nc
        self.ctx = ctx
        self.blk = ctx.enter_context(nc.Block())
        for nm in ["ld", "cast", "st", "pe", "dve", "act", "gp"]:
            setattr(self, nm, Sem(ctx.enter_context(nc.semaphore(nm)), nm))
        self.pb = [ctx.enter_context(nc.psum_tensor(f"pb{k}", [128, BLK], F32))
                   for k in range(7)]
        self.pbt = ctx.enter_context(nc.psum_tensor("pbt", [128, 1024], BF16))
        self.arb1 = ctx.enter_context(nc.sbuf_tensor("arb1", [128, 32768], BF16))
        self.arb2 = ctx.enter_context(nc.sbuf_tensor("arb2", [128, 16384], BF16))
        self.arf = ctx.enter_context(nc.sbuf_tensor("arf", [128, 12288], F32))
        self.wbuf = ctx.enter_context(nc.sbuf_tensor("wbuf", [128, 12288], BF16))
        self.xbuf = ctx.enter_context(nc.sbuf_tensor("xbuf", [128, 1536], F32))
        self.smf = ctx.enter_context(nc.sbuf_tensor("smf", [128, 1024], F32))
        self.smf2 = ctx.enter_context(nc.sbuf_tensor("smf2", [128, 1024], F32))
        self.smb = ctx.enter_context(nc.sbuf_tensor("smb", [128, 2048], BF16))
        self.cbuf = ctx.enter_context(nc.sbuf_tensor("cbuf", [128, 784], BF16))
        self.cbuf_f = ctx.enter_context(nc.sbuf_tensor("cbuf_f", [128, 896], F32))
        self.cbuf2 = ctx.enter_context(nc.sbuf_tensor("cbuf2", [16, 2048], BF16))
        self.rows = self.arf[0:1, 10496:12288]
        self.fence_v = {}

    def fence(self):
        for nm in ["ld", "cast", "st", "pe", "dve", "act", "gp"]:
            self.fence_v[nm] = getattr(self, nm).val

    def fwait(self, e, *names):
        for nm in names:
            v = self.fence_v.get(nm, 0)
            if v > 0:
                e.wait_ge(getattr(self, nm).h, v)

    @staticmethod
    def aw(e, sem, val):
        if val > 0:
            e.wait_ge(sem.h, val)

    @staticmethod
    def dwait(e, sem, base, k, total):
        Plan.aw(e, sem, base + 16 * min(k, total))

    def chain(self, steps, first_wait=("pe", "dve", "act")):
        prev = None
        for k, (engname, fn) in enumerate(steps):
            sem = getattr(self, ENG_SEM[engname])

            def body(e, fn=fn, prev=prev, k=k, sem=sem):
                if k == 0:
                    self.fwait(e, *first_wait)
                if prev is not None:
                    Plan.aw(e, prev[0], prev[1])
                last = fn(e)
                if isinstance(last, (list, tuple)):
                    last = [x for x in last if x is not None][-1]
                sem.inc(last)
            getattr(self.blk, engname)(body)
            prev = (sem, sem.val)
        self.fence()


def emit_dump(p, dst, region, rows=128, tiles=DT, tile_cols=2048):
    gp0 = p.gp.val

    def body(e):
        p.fwait(e, "pe", "dve", "act")
        Plan.aw(e, p.st, p.st.val)
        for t in range(tiles):
            p.gp.inc(e.dma_start(
                dst.ap()[128 * t:128 * t + rows, 0:tile_cols], region(t)), 16)
        e.wait_ge(p.gp.h, gp0 + 16 * tiles)
    p.blk.gpsimd(body)
    p.fence()


def make_plan(nc, ctx):
    p = Plan(nc, ctx)
    for j, v in enumerate(CONSTV_VALUES):
        nc.const_aps.register(F32, v, p.cbuf_f[:, CF_CONSTV + j:CF_CONSTV + j + 1])
    return p


CB_IDN = 0
CB_ONES_COL = 128
CB_XMEAN = 136
CB_ONES_ROW = 648
CF_IDN = 0
CF_ONES_ROW = 128
CF_HREP = 256
CF_HCOL = 512
CF_WNORM = 544
CF_WCONV = 560
CF_DTBIAS = 688
CF_ALOG = 689
CF_MOEB = 690
CF_GATE = 706
CF_GREP = 722
CF_MOEB_REP = 738
CF_NALOGE = 754
CF_CONSTV = 770


def emit_load_consts(p, dr):
    cb, cf = p.cbuf, p.cbuf_f

    def body(e):
        pairs = [
            (cb[:, CB_IDN:CB_IDN + 128], dr["idn_bf"].ap()),
            (cb[:, CB_ONES_COL:CB_ONES_COL + 1], dr["ones_col_bf"].ap()),
            (cb[0:1, CB_ONES_ROW:CB_ONES_ROW + 128], dr["ones_row_bf"].ap()),
            (cb[:, CB_XMEAN:CB_XMEAN + 512], dr["xmean_pat_bf"].ap()),
            (cf[:, CF_IDN:CF_IDN + 128], dr["idn_f32"].ap()),
            (cf[0:1, CF_ONES_ROW:CF_ONES_ROW + 128], dr["ones_row_f32"].ap()),
            (cf[0:16, CF_HREP:CF_HREP + 256], dr["hrep_pat"].ap()),
            (cf[:, CF_HCOL:CF_HCOL + 32], dr["hcol_pat"].ap()),
            (cf[:, CF_WNORM:CF_WNORM + 16], dr["w_norm_cols"].ap()),
            (cf[:, CF_WCONV:CF_WCONV + 128], dr["w_conv_cols"].ap()),
            (cf[0:16, CF_DTBIAS:CF_DTBIAS + 1], dr["dt_bias"].ap()),
            (cf[0:16, CF_ALOG:CF_ALOG + 1], dr["a_log"].ap()),
            (cf[0:1, CF_MOEB:CF_MOEB + 16], dr["moe_bias"].ap()),
            (cf[0:1, CF_GATE:CF_GATE + 16], dr["attn_gate"].ap()),
            (p.cbuf2[:, :], dr["sel16_bf"].ap()),
            (cf[:, CF_CONSTV:CF_CONSTV + 16], dr["constv"].ap()),
        ]
        for dst, src in pairs:
            p.ld.inc(e.dma_start(dst, src), 16)
    p.blk.sync(body)

    p.fence()


def emit_prep_small(p):
    cf = p.cbuf_f
    p.chain([
        ("scalar", lambda e: [
            p.fwait(e, "ld", "cast"),
            e.activation(cf[0:1, CF_GATE:CF_GATE + 16],
                         cf[0:1, CF_GATE:CF_GATE + 16], AF.Sigmoid),
            e.activation(cf[0:16, CF_NALOGE:CF_NALOGE + 1],
                         cf[0:16, CF_ALOG:CF_ALOG + 1], AF.Exp)][-1]),
        ("tensor", lambda e: [
            e.matmul(p.pb[6][:, 0:16], cf[0:1, CF_ONES_ROW:CF_ONES_ROW + 128],
                     cf[0:1, CF_GATE:CF_GATE + 16], start=True, stop=True),
            e.matmul(p.pb[6][:, 16:32], cf[0:1, CF_ONES_ROW:CF_ONES_ROW + 128],
                     cf[0:1, CF_MOEB:CF_MOEB + 16], start=True, stop=True)][-1]),
        ("vector", lambda e: [
            e.tensor_copy(cf[:, CF_GREP:CF_GREP + 16], p.pb[6][:, 0:16]),
            e.tensor_copy(cf[:, CF_MOEB_REP:CF_MOEB_REP + 16], p.pb[6][:, 16:32]),
            e.tensor_scalar_mul(cf[0:16, CF_NALOGE:CF_NALOGE + 1],
                                cf[0:16, CF_NALOGE:CF_NALOGE + 1], -1.0)][-1]),
    ], first_wait=("ld", "cast"))



def emit_ln(p, x_src, arena, col0, router=None):
    xb, srh = p.xbuf, p.smb
    mrep = p.smf[:, 0:512]
    rrep = p.smf[:, 512:1024]
    r_mean = p.rows[0:1, 0:512]
    r_rstd = p.rows[0:1, 512:1024]
    r_sq = p.rows[0:1, 1024:1536]
    r_tmp = p.smf2[0:1, 0:512]

    for b in range(NB):
        ld0, act0, pe0, st0 = p.ld.val, p.act.val, p.pe.val, p.st.val

        def sync1(e, ld0=ld0, act0=act0, b=b, st0=st0):
            p.fwait(e, "dve", "act")
            Plan.aw(e, p.ld, ld0)
            Plan.aw(e, p.st, st0)
            for t in range(DT):
                Plan.aw(e, p.act, act0 + t - 2)
                p.ld.inc(e.dma_start(
                    xb[:, 512 * (t % 3):512 * (t % 3 + 1)],
                    x_src.ap()[128 * t:128 * (t + 1), BLK * b:BLK * (b + 1)]), 16)
        p.blk.sync(sync1)
        assert p.ld.val == ld0 + 16 * DT, ("cnt", "ld", p.ld.val, ld0 + 16 * DT)

        def act1(e, ld0=ld0, pe0=pe0, b=b):
            p.fwait(e, "dve", "pe")
            for t in range(DT):
                Plan.dwait(e, p.ld, ld0, t + 3, DT)
                Plan.aw(e, p.pe, pe0 + t - 1)
                src = xb[:, 512 * (t % 3):512 * (t % 3 + 1)]
                dst = srh[:, 1024 * (t % 2):1024 * (t % 2 + 1)]
                e.copy(dst[:, 0:512], src)
                p.act.inc(e.activation(dst[:, 512:1024], src, AF.Square))
        p.blk.scalar(act1)
        assert p.act.val == act0 + DT, ("cnt", "act", p.act.val, act0 + DT)

        def pe1(e, act0=act0, b=b):
            p.fwait(e, "dve")
            for t in range(DT):
                Plan.aw(e, p.act, act0 + t + 1)
                dst = srh[:, 1024 * (t % 2):1024 * (t % 2 + 1)]
                first, last = (t == 0), (t == DT - 1)
                e.matmul(p.pb[4][0:1, :], p.cbuf[:, CB_ONES_COL:CB_ONES_COL + 1],
                         dst[:, 0:512], start=first, stop=last)
                p.pe.inc(e.matmul(p.pb[5][0:1, :],
                                  p.cbuf[:, CB_ONES_COL:CB_ONES_COL + 1],
                                  dst[:, 512:1024], start=first, stop=last))
        p.blk.tensor(pe1)
        assert p.pe.val == pe0 + DT, ("cnt", "pe", p.pe.val, pe0 + DT)
        p.fence()

        p.chain([
            ("vector", lambda e: [
                e.tensor_scalar_mul(r_mean, p.pb[4][0:1, :], 1.0 / D),
                e.tensor_scalar_mul(r_sq, p.pb[5][0:1, :], 1.0 / D),
                e.tensor_tensor(r_tmp, r_mean, r_mean, ALU.mult),
                e.tensor_sub(r_rstd, r_sq, r_tmp),
                e.tensor_scalar_add(r_rstd, r_rstd, float(LN_EPS))][-1]),
            ("scalar", lambda e: [
                e.activation(r_rstd, r_rstd, AF.Ln),
                e.activation(r_rstd, r_rstd, AF.Exp, scale=-0.5)][-1]),
            ("tensor", lambda e: [
                e.matmul(p.pb[4][:, :], p.cbuf_f[0:1, CF_ONES_ROW:CF_ONES_ROW + 128],
                         r_mean, start=True, stop=True),
                e.matmul(p.pb[5][:, :], p.cbuf_f[0:1, CF_ONES_ROW:CF_ONES_ROW + 128],
                         r_rstd, start=True, stop=True)][-1]),
            ("vector", lambda e: [
                e.tensor_copy(mrep, p.pb[4][:, :]),
                e.tensor_copy(rrep, p.pb[5][:, :])][-1]),
        ])

        ld2, dve2 = p.ld.val, p.dve.val

        def sync2(e, ld2=ld2, dve2=dve2, b=b, st0=st0):
            Plan.aw(e, p.ld, ld2)
            Plan.aw(e, p.act, p.act.val)
            Plan.aw(e, p.st, st0)
            for t in range(DT):
                Plan.aw(e, p.dve, dve2 + t - 2)
                p.ld.inc(e.dma_start(
                    xb[:, 512 * (t % 3):512 * (t % 3 + 1)],
                    x_src.ap()[128 * t:128 * (t + 1), BLK * b:BLK * (b + 1)]), 16)
        p.blk.sync(sync2)
        assert p.ld.val == ld2 + 16 * DT, ("cnt", "ld", p.ld.val, ld2 + 16 * DT)

        per0 = p.pe.val

        def dve3(e, ld2=ld2, b=b, per0=per0):
            for t in range(DT):
                Plan.dwait(e, p.ld, ld2, t + 3, DT)
                src = xb[:, 512 * (t % 3):512 * (t % 3 + 1)]
                dst = arena[:, col0 + 2048 * t + 512 * b:
                            col0 + 2048 * t + 512 * (b + 1)]
                if router is None:
                    tmp = p.smf2[:, 0:512]
                    e.tensor_sub(tmp, src, mrep)
                    p.dve.inc(e.tensor_mul(dst, tmp, rrep))
                else:
                    xf = p.smf2[:, 512 * (t % 2):512 * (t % 2 + 1)]
                    if t >= 2:
                        Plan.aw(e, p.pe, per0 + t - 1)
                    e.tensor_sub(xf, src, mrep)
                    e.tensor_mul(xf, xf, rrep)
                    p.dve.inc(e.tensor_copy(dst, xf))
        p.blk.vector(dve3)
        assert p.dve.val == dve2 + DT, ("cnt", "dve", p.dve.val, dve2 + DT)

        if router is not None:
            dve3_0 = dve2

            def pe_rt(e, b=b, dve3_0=dve3_0):
                if b > 0:
                    Plan.aw(e, p.act, p.act.val)
                for t in range(DT):
                    Plan.aw(e, p.dve, dve3_0 + t + 1)
                    xf = p.smf2[:, 512 * (t % 2):512 * (t % 2 + 1)]
                    p.pe.inc(e.matmul(p.pb[6][0:16, :],
                                      router["wg"][:, 16 * t:16 * (t + 1)],
                                      xf, start=(t == 0), stop=(t == DT - 1)))
            p.blk.tensor(pe_rt)
            rt_pe_end = p.pe.val

            def act_rt(e, b=b, rt_pe_end=rt_pe_end):
                Plan.aw(e, p.pe, rt_pe_end)
                p.act.inc(e.activation(router["gl"][:, 512 * b:512 * (b + 1)],
                                       p.pb[6][0:16, :], AF.Sigmoid))
            p.blk.scalar(act_rt)
        p.fence()



def emit_ffn_l0(p, dr, src, dst, upto=None):
    cb, cf = p.cbuf, p.cbuf_f
    arb1, arb2, arf, wbuf = p.arb1, p.arb2, p.arf, p.wbuf
    xn = lambda c, b: arb1[:, 2048 * c + 512 * b:2048 * c + 512 * (b + 1)]

    gl = arf[0:16, 8192:10240]
    wgf = arf[:, 10240:10496]
    wn_tok = p.smf2[:, 0:256]
    wT16 = p.smb[0:16, 0:2048]

    ldw0 = p.ld.val

    def wg_ld(e):
        p.fwait(e, "pe", "dve")
        p.ld.inc(e.dma_start(wgf, dr["wg_sb"].ap()), 16)
    p.blk.sync(wg_ld)

    def wg_gate(e):
        Plan.aw(e, p.ld, ldw0 + 16)
        p.pe.inc(e.nop())
    p.blk.tensor(wg_gate)
    p.fence()

    emit_ln(p, src, arb1, 0, router={"wg": wgf, "gl": gl})
    if upto == "ln":
        emit_dump(p, dst, lambda t: arb1[:, 2048 * t:2048 * (t + 1)])
        return
    if upto == "router_gl":
        emit_dump(p, dst, lambda t: gl[0:16, 0:2048], rows=16, tiles=1)
        return

    def tr_fwd(e):
        for c in range(16):
            mm = e.transpose(p.pb[4][:, 16 * c:16 * (c + 1)],
                             gl[:, 128 * c:128 * (c + 1)],
                             cf[0:16, CF_IDN:CF_IDN + 16])
        return mm

    def rank(e):
        dn0 = p.dve.val

        def step(inst):
            p.dve.inc(inst)
            e.wait_ge(p.dve.h, p.dve.val)

        pbi = p.smf[:, 0:16]
        cnt = p.smf[:, 16:32]
        msk = p.smf[:, 32:48]
        wv = p.smf[:, 48:64]
        ws = p.smf[:, 64:66]
        scrap = p.smf[:, 256:512]
        for c in range(16):
            ptk = p.pb[4][:, 16 * c:16 * (c + 1)]
            step(e.tensor_add(pbi, ptk, cf[:, CF_MOEB_REP:CF_MOEB_REP + 16]))
            pstep = pbi.ap[0][0]
            gt0 = mkap(pbi, [[pstep, 128], [1, 16], [0, 16]])
            gt1 = mkap(pbi, [[pstep, 128], [0, 16], [1, 16]])
            step(e.tensor_tensor(scrap.rearrange("p (a b) -> p a b", b=16),
                                 gt0, gt1, ALU.is_gt))
            step(e.reduce_sum(cnt, scrap.rearrange("p (a b) -> p b a", b=16),
                              axis=AX.X))
            step(e.tensor_scalar(msk, cnt, 7.5, None, ALU.is_le))
            step(e.tensor_mul(wv, ptk, msk))
            step(e.reduce_sum(ws[:, 0:1], wv, axis=AX.X))
            step(e.reciprocal(ws[:, 1:2], ws[:, 0:1]))
            wib = mkap(ws[:, 1:2], [[ws.ap[0][0], 128], [0, 16]])
            last = e.scalar_tensor_tensor(wn_tok[:, 16 * c:16 * (c + 1)],
                                          wv, MOE_SCALE, wib,
                                          ALU.mult, ALU.mult)
            if c < 15:
                step(last)
        return last

    TRB = [4, 5, 6, 0]

    def tr_bwd(e):
        for g in range(4):
            for j in range(4):
                c = 4 * g + j
                mm = e.transpose(p.pb[TRB[g]][0:16, 128 * j:128 * (j + 1)],
                                 wn_tok[:, 16 * c:16 * (c + 1)],
                                 cf[:, CF_IDN:CF_IDN + 128])
        return mm

    def tr_ev(e):
        for g in range(4):
            last = e.copy(wT16[:, 512 * g:512 * (g + 1)],
                          p.pb[TRB[g]][0:16, 0:512])
        return last

    p.chain([("tensor", tr_fwd), ("vector", rank), ("tensor", tr_bwd),
             ("scalar", tr_ev)])
    if upto == "router":
        emit_dump(p, dst, lambda t: wT16[0:16, 0:2048], rows=16, tiles=1)
        return

    cast0, pe0, act0, dve0 = p.cast.val, p.pe.val, p.act.val, p.dve.val

    def sh_cast(e):
        Plan.aw(e, p.cast, cast0)
        Plan.aw(e, p.pe, pe0)
        for ht in range(8):
            Plan.aw(e, p.pe, pe0 + 4 * (ht - 1))
            p.cast.inc(e.dma_start(wbuf[:, 2048 * (ht % 2):2048 * (ht % 2 + 1)],
                                   dr["sw1T"].ap()[128 * ht:128 * (ht + 1), :]), 16)
            p.cast.inc(e.dma_start(
                wbuf[:, 4096 + 2048 * (ht % 2):4096 + 2048 * (ht % 2 + 1)],
                dr["swgT"].ap()[128 * ht:128 * (ht + 1), :]), 16)
    p.blk.gpsimd(sh_cast)

    def sh_pe(e):
        p.fwait(e, "act", "dve")
        for ht in range(8):
            Plan.aw(e, p.cast, cast0 + 32 * min(ht + 2, 8))
            for b in range(NB):
                i = ht * NB + b
                Plan.aw(e, p.dve, dve0 + max(i - 1, 0) if i >= 2 else 0)
                pa = p.pb[2 * (i % 2)]
                pg = p.pb[2 * (i % 2) + 1]
                for c in range(DT):
                    e.matmul(pa[:, :], wbuf[:, 2048 * (ht % 2) + 128 * c:
                                            2048 * (ht % 2) + 128 * (c + 1)],
                             xn(c, b), start=(c == 0), stop=(c == DT - 1))
                for c in range(DT):
                    mm = e.matmul(pg[:, :], wbuf[:, 4096 + 2048 * (ht % 2) + 128 * c:
                                                 4096 + 2048 * (ht % 2) + 128 * (c + 1)],
                                  xn(c, b), start=(c == 0), stop=(c == DT - 1))
                p.pe.inc(mm)
    p.blk.tensor(sh_pe)

    def sh_act(e):
        for i in range(32):
            Plan.aw(e, p.pe, pe0 + i + 1)
            if i >= 2:
                Plan.aw(e, p.dve, dve0 + i - 1)
            pg = p.pb[2 * (i % 2) + 1]
            tmp = p.smf[:, 512 * (i % 2):512 * (i % 2 + 1)]
            e.activation(tmp, pg[:, :], AF.Relu)
            p.act.inc(e.activation(tmp, tmp, AF.Square))
    p.blk.scalar(sh_act)

    def sh_dve(e):
        for i in range(32):
            ht, b = divmod(i, NB)
            Plan.aw(e, p.act, act0 + i + 1)
            pa = p.pb[2 * (i % 2)]
            tmp = p.smf[:, 512 * (i % 2):512 * (i % 2 + 1)]
            dsth = arb2[:, 2048 * ht + 512 * b:2048 * ht + 512 * (b + 1)]
            p.dve.inc(e.tensor_mul(dsth, pa[:, :], tmp))
    p.blk.vector(sh_dve)
    p.fence()
    if upto == "shared":
        emit_dump(p, dst, lambda t: arb2[:, 2048 * t:2048 * (t + 1)], tiles=8)
        return

    cast0, pe0, dve0 = p.cast.val, p.pe.val, p.dve.val

    def xl_cast(e):
        Plan.aw(e, p.cast, cast0)
        Plan.aw(e, p.pe, pe0)
        for lt in range(4):
            Plan.aw(e, p.pe, pe0 + 4 * (lt - 1))
            p.cast.inc(e.dma_start(wbuf[:, 2048 * (lt % 2):2048 * (lt % 2 + 1)],
                                   dr["wdownT"].ap()[128 * lt:128 * (lt + 1), :]), 16)
    p.blk.gpsimd(xl_cast)

    def xl_pe(e):
        p.fwait(e, "dve", "act")
        for lt in range(4):
            Plan.aw(e, p.cast, cast0 + 16 * min(lt + 2, 4))
            for b in range(NB):
                i = lt * NB + b
                if i >= 2:
                    Plan.aw(e, p.dve, dve0 + i - 1)
                pbk = p.pb[i % 2]
                for c in range(DT):
                    mm = e.matmul(pbk[:, :], wbuf[:, 2048 * (lt % 2) + 128 * c:
                                                  2048 * (lt % 2) + 128 * (c + 1)],
                                  xn(c, b), start=(c == 0), stop=(c == DT - 1))
                p.pe.inc(mm)
    p.blk.tensor(xl_pe)

    def xl_dve(e):
        for i in range(16):
            lt, b = divmod(i, NB)
            Plan.aw(e, p.pe, pe0 + i + 1)
            dxl = wbuf[:, 4096 + 2048 * lt + 512 * b:4096 + 2048 * lt + 512 * (b + 1)]
            p.dve.inc(e.tensor_copy(dxl, p.pb[i % 2][:, :]))
    p.blk.vector(xl_dve)
    if upto == "xl":
        p.fence()
        emit_dump(p, dst, lambda t: wbuf[:, 4096 + 2048 * t:4096 + 2048 * (t + 1)],
                  tiles=4)
        return
    p.fence()
    xl = lambda c, b: wbuf[:, 4096 + 2048 * c + 512 * b:4096 + 2048 * c + 512 * (b + 1)]

    h_eb = arf.bitcast(BF16)
    H0 = 20480
    acc_t = lambda c, b: arf[:, 2048 * c + 512 * b:2048 * c + 512 * (b + 1)]
    wrep_b = lambda b: arf[:, 8192 + 512 * b:8192 + 512 * (b + 1)]

    cast0, pe0, act0, dve0 = p.cast.val, p.pe.val, p.act.val, p.dve.val
    PE_E, DVE_E, ACT_E = 40, 52, 32

    def d_wrep(ee, b):
        return dve0 + DVE_E * ee + b + 1

    def d_h(ee, b, ht):
        return dve0 + DVE_E * ee + 4 + 12 * b + ht + 1

    def d_acc(ee, b):
        return dve0 + DVE_E * ee + 4 + 12 * b + 12

    def g_pe(ee, b, ht):
        return pe0 + PE_E * ee + 4 + 9 * b + ht + 1

    def ex_cast(e):
        Plan.aw(e, p.cast, cast0)
        Plan.aw(e, p.pe, pe0)
        for ee in range(NE):
            if ee >= 2:
                Plan.aw(e, p.pe, pe0 + (ee - 1) * PE_E)
            p.cast.inc(e.dma_start(
                arb1[:, 12288 * (ee % 2):12288 * (ee % 2 + 1)],
                dr["rw_img"].ap()[128 * ee:128 * (ee + 1), :]), 16)
    p.blk.gpsimd(ex_cast)

    def ex_pe(e):
        p.fwait(e, "dve", "act")
        for ee in range(NE):
            s = 12288 * (ee % 2)
            Plan.aw(e, p.cast, cast0 + 16 * min(ee + 2, NE))
            if ee > 0:
                Plan.aw(e, p.dve, dve0 + DVE_E * ee)
            for b in range(NB):
                if b > 0 or ee > 0:
                    pass
                Plan.aw(e, p.dve, d_wrep(ee, b) - 1 if (ee, b) != (0, 0) else 0)
                p.pe.inc(e.matmul(p.pb[6][:, :],
                                  p.cbuf2[:, 128 * ee:128 * (ee + 1)],
                                  wT16[:, 512 * b:512 * (b + 1)],
                                  start=True, stop=True))
            for b in range(NB):
                if b > 0:
                    Plan.aw(e, p.dve, d_acc(ee, b - 1))
                for ht in range(8):
                    g = 8 * b + ht
                    pr = g % 3
                    if ht >= 3:
                        Plan.aw(e, p.dve, d_h(ee, b, ht - 3))
                    pa, pg = p.pb[2 * pr], p.pb[2 * pr + 1]
                    for c in range(4):
                        e.matmul(pa[:, :], arb1[:, s + 512 * ht + 128 * c:
                                                s + 512 * ht + 128 * (c + 1)],
                                 xl(c, b), start=(c == 0), stop=(c == 3))
                    for c in range(4):
                        mm = e.matmul(pg[:, :], arb1[:, s + 4096 + 512 * ht + 128 * c:
                                                     s + 4096 + 512 * ht + 128 * (c + 1)],
                                      xl(c, b), start=(c == 0), stop=(c == 3))
                    p.pe.inc(mm)
                Plan.aw(e, p.dve, d_h(ee, b, 7))
                for lt in range(4):
                    for ht in range(8):
                        hsrc = h_eb[:, H0 + 512 * ht:H0 + 512 * (ht + 1)]
                        mm = e.matmul(p.pb[lt][:, :],
                                      arb1[:, s + 8192 + 1024 * lt + 128 * ht:
                                           s + 8192 + 1024 * lt + 128 * (ht + 1)],
                                      hsrc, start=(ht == 0), stop=(ht == 7))
                p.pe.inc(mm)
    p.blk.tensor(ex_pe)

    def ex_act(e):
        for ee in range(NE):
            for b in range(NB):
                for ht in range(8):
                    g = 8 * b + ht
                    Plan.aw(e, p.pe, g_pe(ee, b, ht))
                    gb2 = g - 2
                    if gb2 >= 0:
                        Plan.aw(e, p.dve, d_h(ee, gb2 // 8, gb2 % 8))
                    elif ee > 0:
                        Plan.aw(e, p.dve, dve0 + DVE_E * ee)
                    pg = p.pb[2 * (g % 3) + 1]
                    tmp = p.smf[:, 512 * (g % 2):512 * (g % 2 + 1)]
                    e.activation(tmp, pg[:, :], AF.Relu)
                    p.act.inc(e.activation(tmp, tmp, AF.Square))
    p.blk.scalar(ex_act)

    def ex_dve(e):
        for ee in range(NE):
            for b in range(NB):
                Plan.aw(e, p.pe, pe0 + PE_E * ee + b + 1)
                p.dve.inc(e.tensor_copy(wrep_b(b), p.pb[6][:, :]))
            for b in range(NB):
                for ht in range(8):
                    g = 8 * b + ht
                    Plan.aw(e, p.act, act0 + ACT_E * ee + g + 1)
                    pa = p.pb[2 * (g % 3)]
                    tmp = p.smf[:, 512 * (g % 2):512 * (g % 2 + 1)]
                    hdst = h_eb[:, H0 + 512 * ht:H0 + 512 * (ht + 1)]
                    p.dve.inc(e.tensor_mul(hdst, pa[:, :], tmp))
                Plan.aw(e, p.pe, pe0 + PE_E * ee + 4 + 9 * b + 9)
                for lt in range(4):
                    t2 = p.smf2[:, 512:1024]
                    if ee == 0:
                        p.dve.inc(e.tensor_mul(acc_t(lt, b), p.pb[lt][:, :],
                                               wrep_b(b)))
                    else:
                        e.tensor_mul(t2, p.pb[lt][:, :], wrep_b(b))
                        p.dve.inc(e.tensor_add(acc_t(lt, b), acc_t(lt, b), t2))
    p.blk.vector(ex_dve)
    p.fence()
    if upto == "experts":
        emit_dump(p, dst, lambda t: arf[:, 2048 * t:2048 * (t + 1)], tiles=4)
        return

    def acc_cvt(e):
        for c in range(4):
            last = e.tensor_copy(wbuf[:, 4096 + 2048 * c:4096 + 2048 * (c + 1)],
                                 arf[:, 2048 * c:2048 * (c + 1)])
        return last
    p.chain([("vector", acc_cvt)])
    accb = lambda c, b: wbuf[:, 4096 + 2048 * c + 512 * b:
                             4096 + 2048 * c + 512 * (b + 1)]

    cast0, pe0, dve0, ld0, st0 = (p.cast.val, p.pe.val, p.dve.val,
                                  p.ld.val, p.st.val)

    def cm_cast(e):
        Plan.aw(e, p.cast, cast0)
        Plan.aw(e, p.pe, pe0)
        for dt_ in range(DT):
            if dt_ >= 2:
                Plan.aw(e, p.pe, pe0 + 4 * (dt_ - 1))
            p.cast.inc(e.dma_start(
                wbuf[:, 1536 * (dt_ % 2):1536 * (dt_ % 2 + 1)],
                dr["comb_img"].ap()[128 * dt_:128 * (dt_ + 1), :]), 16)
    p.blk.gpsimd(cm_cast)

    def cm_sync(e):
        p.fwait(e, "dve")
        Plan.aw(e, p.ld, ld0)
        Plan.aw(e, p.st, st0)
        for i in range(DT * NB):
            if i >= 3:
                Plan.aw(e, p.dve, dve0 + i - 2)
            dt_, b = divmod(i, NB)
            p.ld.inc(e.dma_start(
                p.xbuf[:, 512 * (i % 3):512 * (i % 3 + 1)],
                src.ap()[128 * dt_:128 * (dt_ + 1), BLK * b:BLK * (b + 1)]), 16)
    p.blk.sync(cm_sync)

    def cm_pe(e):
        p.fwait(e, "dve", "act")
        for dt_ in range(DT):
            Plan.aw(e, p.cast, cast0 + 16 * min(dt_ + 2, DT))
            for b in range(NB):
                i = dt_ * NB + b
                if i >= 2:
                    Plan.aw(e, p.dve, dve0 + i - 1)
                pbk = p.pb[i % 2]
                w0 = 1536 * (dt_ % 2)
                for c in range(8):
                    e.matmul(pbk[:, :], wbuf[:, w0 + 128 * c:w0 + 128 * (c + 1)],
                             arb2[:, 2048 * c + 512 * b:2048 * c + 512 * (b + 1)],
                             start=(c == 0), stop=False)
                for c in range(4):
                    mm = e.matmul(pbk[:, :],
                                  wbuf[:, w0 + 1024 + 128 * c:w0 + 1024 + 128 * (c + 1)],
                                  accb(c, b), start=False, stop=(c == 3))
                p.pe.inc(mm)
    p.blk.tensor(cm_pe)

    def cm_dve(e):
        for i in range(DT * NB):
            Plan.aw(e, p.pe, pe0 + i + 1)
            Plan.dwait(e, p.ld, ld0, i + 3, DT * NB)
            if i >= 2:
                Plan.aw(e, p.st, st0 + 16 * (i - 1))
            xsrc = p.xbuf[:, 512 * (i % 3):512 * (i % 3 + 1)]
            out = p.smf2[:, 512 * (i % 2):512 * (i % 2 + 1)]
            p.dve.inc(e.scalar_tensor_tensor(out, p.pb[i % 2][:, :], 0.5, xsrc,
                                             ALU.mult, ALU.add))
    p.blk.vector(cm_dve)

    def cm_st(e):
        for i in range(DT * NB):
            dt_, b = divmod(i, NB)
            Plan.aw(e, p.dve, dve0 + i + 1)
            out = p.smf2[:, 512 * (i % 2):512 * (i % 2 + 1)]
            p.st.inc(e.dma_start(
                dst.ap()[128 * dt_:128 * (dt_ + 1), BLK * b:BLK * (b + 1)], out), 16)
    p.blk.scalar(cm_st)
    p.fence()



def build_program(sublayers, final_dst="out", upto=None):
    nc = bass.Bass("TRN2", target_bir_lowering=False)
    dr = {}
    dr["x_in"] = nc.dram_tensor("x_in", [D, T], F32, kind="ExternalInput")
    dr["out"] = nc.dram_tensor("out", [D, T], F32, kind="ExternalOutput")
    for nm, (shape, dt_) in WEIGHT_SPECS.items():
        dr[nm] = nc.dram_tensor(nm, list(shape), dt_, kind="ExternalInput")
    for nm, (shape, dt_) in SCRATCH_SPECS.items():
        dr[nm] = nc.dram_tensor(nm, list(shape), dt_)

    with contextlib.ExitStack() as ctx:
        p = make_plan(nc, ctx)
        emit_load_consts(p, dr)
        emit_prep_small(p)
        srcs = ["x_in"] + ["resid"] * (len(sublayers) - 1)
        dsts = ["resid"] * (len(sublayers) - 1) + [final_dst]
        for sl, s, d in zip(sublayers, srcs, dsts):
            if sl == "ffn0a" or sl == "ffn0b":
                emit_ffn_l0(p, dr, dr[s], dr[d], upto=upto)
            elif sl == "mamba":
                emit_mamba(p, dr, dr[s], dr[d], upto=upto)
            elif sl == "attn":
                emit_attn_l1(p, dr, dr[s], dr[d], upto=upto)
            elif sl == "ffn1a":
                emit_ffn_l1(p, dr, dr[s], dr[d], dequant=True)
            elif sl == "ffn1b":
                emit_ffn_l1(p, dr, dr[s], dr[d], dequant=False)
            else:
                raise ValueError(sl)
        stv = p.st.val

        def fin(e):
            e.wait_ge(p.st.h, stv)
        p.blk.sync(fin)
    return nc


def emit_mamba(p, dr, src, dst, upto=None):
    cb, cf = p.cbuf, p.cbuf_f
    arb1, arb2, arf, wbuf = p.arb1, p.arb2, p.arf, p.wbuf
    xb, srh = p.xbuf, p.smb
    mrep = p.smf[:, 0:512]
    rrep = p.smf[:, 512:1024]
    r_mean = p.rows[0:1, 0:512]
    r_rstd = p.rows[0:1, 512:1024]
    r_sq = p.rows[0:1, 1024:1536]
    r_tmp = p.smf2[0:1, 0:512]
    rstd2_all = arf[0:1, 8192:10240]

    for b in range(NB):
        ld0, act0, pe0, dve0, st0 = (p.ld.val, p.act.val, p.pe.val,
                                     p.dve.val, p.st.val)

        def sync1(e, ld0=ld0, act0=act0, b=b, st0=st0):
            p.fwait(e, "dve", "act")
            Plan.aw(e, p.ld, ld0)
            Plan.aw(e, p.st, st0)
            for t in range(DT):
                Plan.aw(e, p.act, act0 + t - 2)
                p.ld.inc(e.dma_start(
                    xb[:, 512 * (t % 3):512 * (t % 3 + 1)],
                    src.ap()[128 * t:128 * (t + 1), BLK * b:BLK * (b + 1)]), 16)
        p.blk.sync(sync1)
        assert p.ld.val == ld0 + 16 * DT, ("cnt", "ld", p.ld.val, ld0 + 16 * DT)

        def act1(e, ld0=ld0, pe0=pe0, b=b):
            p.fwait(e, "dve", "pe")
            for t in range(DT):
                Plan.dwait(e, p.ld, ld0, t + 3, DT)
                Plan.aw(e, p.pe, pe0 + t - 1)
                s2 = xb[:, 512 * (t % 3):512 * (t % 3 + 1)]
                dstp = srh[:, 1024 * (t % 2):1024 * (t % 2 + 1)]
                e.copy(dstp[:, 0:512], s2)
                p.act.inc(e.activation(dstp[:, 512:1024], s2, AF.Square))
        p.blk.scalar(act1)
        assert p.act.val == act0 + DT, ("cnt", "act", p.act.val, act0 + DT)

        def pe1(e, act0=act0, b=b):
            p.fwait(e, "dve")
            for t in range(DT):
                Plan.aw(e, p.act, act0 + t + 1)
                dstp = srh[:, 1024 * (t % 2):1024 * (t % 2 + 1)]
                first, last = (t == 0), (t == DT - 1)
                e.matmul(p.pb[4][0:1, :], cb[:, CB_ONES_COL:CB_ONES_COL + 1],
                         dstp[:, 0:512], start=first, stop=last)
                p.pe.inc(e.matmul(p.pb[5][0:1, :],
                                  cb[:, CB_ONES_COL:CB_ONES_COL + 1],
                                  dstp[:, 512:1024], start=first, stop=last))
        p.blk.tensor(pe1)
        assert p.pe.val == pe0 + DT, ("cnt", "pe", p.pe.val, pe0 + DT)
        p.fence()

        p.chain([
            ("vector", lambda e: [
                e.tensor_scalar_mul(r_mean, p.pb[4][0:1, :], 1.0 / D),
                e.tensor_scalar_mul(r_sq, p.pb[5][0:1, :], 1.0 / D),
                e.tensor_tensor(r_tmp, r_mean, r_mean, ALU.mult),
                e.tensor_sub(r_rstd, r_sq, r_tmp),
                e.tensor_scalar_add(r_rstd, r_rstd, float(LN_EPS))][-1]),
            ("scalar", lambda e: [
                e.activation(r_rstd, r_rstd, AF.Ln),
                e.activation(r_rstd, r_rstd, AF.Exp, scale=-0.5)][-1]),
            ("tensor", lambda e: [
                e.matmul(p.pb[4][:, :], cf[0:1, CF_ONES_ROW:CF_ONES_ROW + 128],
                         r_mean, start=True, stop=True),
                e.matmul(p.pb[5][:, :], cf[0:1, CF_ONES_ROW:CF_ONES_ROW + 128],
                         r_rstd, start=True, stop=True)][-1]),
            ("vector", lambda e: [
                e.tensor_copy(mrep, p.pb[4][:, :]),
                e.tensor_copy(rrep, p.pb[5][:, :])][-1]),
        ])

        ld2, dve2, act2, pe2, st2 = (p.ld.val, p.dve.val, p.act.val,
                                     p.pe.val, p.st.val)

        def sync2(e, ld2=ld2, dve2=dve2, b=b, st2=st2):
            Plan.aw(e, p.ld, ld2)
            Plan.aw(e, p.act, p.act.val)
            Plan.aw(e, p.st, st2)
            for t in range(DT):
                if t >= 3:
                    Plan.aw(e, p.dve, dve2 + 2 * (t - 2))
                p.ld.inc(e.dma_start(
                    xb[:, 512 * (t % 3):512 * (t % 3 + 1)],
                    src.ap()[128 * t:128 * (t + 1), BLK * b:BLK * (b + 1)]), 16)
        p.blk.sync(sync2)
        assert p.ld.val == ld2 + 16 * DT, ("cnt", "ld", p.ld.val, ld2 + 16 * DT)

        def dve3(e, ld2=ld2, st2=st2, b=b):
            for t in range(DT):
                Plan.dwait(e, p.ld, ld2, t + 3, DT)
                if t >= 2:
                    Plan.aw(e, p.st, st2 + 16 * 2 * (t - 1))
                s2 = xb[:, 512 * (t % 3):512 * (t % 3 + 1)]
                v = p.smf2[:, 512 * (t % 2):512 * (t % 2 + 1)]
                rp = arf[:, 512 * (t % 2):512 * (t % 2 + 1)]
                e.tensor_sub(v, s2, mrep)
                p.dve.inc(e.tensor_mul(v, v, rrep))
                p.dve.inc(e.tensor_add(rp, s2, v))
        p.blk.vector(dve3)
        assert p.dve.val == dve2 + 2 * DT, ("cnt", "dve", p.dve.val, dve2 + 2 * DT)

        def act3(e, dve2=dve2, pe2=pe2, st2=st2, b=b):
            for t in range(DT):
                Plan.aw(e, p.dve, dve2 + 2 * t + 1)
                if t >= 2:
                    Plan.aw(e, p.pe, pe2 + t - 1)
                v = p.smf2[:, 512 * (t % 2):512 * (t % 2 + 1)]
                sq = srh[:, 1024 * (t % 2):1024 * (t % 2) + 512]
                p.act.inc(e.activation(sq, v, AF.Square))
                rp = arf[:, 512 * (t % 2):512 * (t % 2 + 1)]
                Plan.aw(e, p.dve, dve2 + 2 * t + 2)
                p.st.inc(e.dma_start(
                    dr["vbuf"].ap()[128 * t:128 * (t + 1), BLK * b:BLK * (b + 1)],
                    v), 16)
                p.st.inc(e.dma_start(
                    dr["r1buf"].ap()[128 * t:128 * (t + 1), BLK * b:BLK * (b + 1)],
                    rp), 16)
        p.blk.scalar(act3)
        assert p.act.val == act2 + DT, ("cnt", "act", p.act.val, act2 + DT)
        assert p.st.val == st2 + 32 * DT, ("cnt", "st", p.st.val, st2 + 32 * DT)

        def pe3(e, act2=act2, b=b):
            for t in range(DT):
                Plan.aw(e, p.act, act2 + t + 1)
                sq = srh[:, 1024 * (t % 2):1024 * (t % 2) + 512]
                p.pe.inc(e.matmul(p.pb[6][0:1, :],
                                  cb[:, CB_ONES_COL:CB_ONES_COL + 1], sq,
                                  start=(t == 0), stop=(t == DT - 1)))
        p.blk.tensor(pe3)
        assert p.pe.val == pe2 + DT, ("cnt", "pe", p.pe.val, pe2 + DT)
        p.fence()

        rs = rstd2_all[0:1, 512 * b:512 * (b + 1)]

        def vrms(e, rs=rs):
            e.tensor_scalar_mul(r_tmp, p.pb[6][0:1, :], 1.0 / D)
            return e.tensor_scalar_add(rs, r_tmp, float(RMS_EPS_MAMBA))

        def srms(e, rs=rs):
            e.activation(rs, rs, AF.Ln)
            return e.activation(rs, rs, AF.Exp, scale=-0.5)
        p.chain([("vector", vrms), ("scalar", srms)])

    for b in range(NB):
        def reprs(e, b=b):
            return e.matmul(p.pb[5][:, :], cf[0:1, CF_ONES_ROW:CF_ONES_ROW + 128],
                            rstd2_all[0:1, 512 * b:512 * (b + 1)],
                            start=True, stop=True)
        p.chain([("tensor", reprs),
                 ("vector", lambda e: e.tensor_copy(rrep, p.pb[5][:, :]))])
        ld0, dve0 = p.ld.val, p.dve.val

        def syncv(e, ld0=ld0, dve0=dve0, b=b):
            Plan.aw(e, p.ld, ld0)
            Plan.aw(e, p.dve, dve0)
            Plan.aw(e, p.st, p.st.val)
            for t in range(DT):
                if t >= 3:
                    Plan.aw(e, p.dve, dve0 + t - 2)
                p.ld.inc(e.dma_start(
                    xb[:, 512 * (t % 3):512 * (t % 3 + 1)],
                    dr["vbuf"].ap()[128 * t:128 * (t + 1), BLK * b:BLK * (b + 1)]), 16)
        p.blk.sync(syncv)
        assert p.ld.val == ld0 + 16 * DT, ("cnt", "ld", p.ld.val, ld0 + 16 * DT)

        def dvev(e, ld0=ld0, b=b):
            for t in range(DT):
                Plan.dwait(e, p.ld, ld0, t + 3, DT)
                s2 = xb[:, 512 * (t % 3):512 * (t % 3 + 1)]
                dstx = arb1[:, 2048 * t + 512 * b:2048 * t + 512 * (b + 1)]
                p.dve.inc(e.scalar_tensor_tensor(
                    dstx, s2, cf[:, CF_WNORM + t:CF_WNORM + t + 1], rrep,
                    ALU.mult, ALU.mult))
        p.blk.vector(dvev)
        assert p.dve.val == dve0 + DT, ("cnt", "dve", p.dve.val, dve0 + DT)
        p.fence()
    xn = lambda c, b: arb1[:, 2048 * c + 512 * b:2048 * c + 512 * (b + 1)]
    if upto == "xn":
        emit_dump(p, dst, lambda t: arb1[:, 2048 * t:2048 * (t + 1)])
        return

    NT = 69
    ld0, cast0, pe0, act0, dve0, st0 = (p.ld.val, p.cast.val, p.pe.val,
                                        p.act.val, p.dve.val, p.st.val)

    def is_xi(tt):
        return 32 <= tt < 64

    def n_xi(tt):
        return max(0, min(tt, 64) - 32)

    def dve_base(tt):
        return dve0 + 5 * n_xi(tt) + 4 * (tt - n_xi(tt))

    def act_base(tt):
        return act0 + n_xi(tt)

    def pe_base(tt):
        return pe0 + 4 * tt + n_xi(tt)

    def st_jz(tt):
        return 4 * (min(tt, 32) + max(0, tt - 64))

    def pj_cast(e):
        Plan.aw(e, p.cast, cast0)
        Plan.aw(e, p.pe, pe0)
        for tt in range(NT):
            if tt >= 2:
                Plan.aw(e, p.pe, pe_base(tt - 1))
            p.cast.inc(e.dma_start(
                wbuf[:, 2048 * (tt % 2):2048 * (tt % 2 + 1)],
                dr["w_inT"].ap()[128 * tt:128 * (tt + 1), :]), 16)
    p.blk.gpsimd(pj_cast)
    assert p.cast.val == cast0 + 16 * NT, ("cnt", "cast", p.cast.val, cast0 + 16 * NT)

    def pj_pe(e):
        p.fwait(e, "dve", "act")
        for tt in range(NT):
            if tt >= 33 and is_xi(tt - 1):
                i = tt - 1 - 32
                Plan.aw(e, p.act, act_base(tt - 1) + 1)
                sl = arb2[:, 2048 * ((tt - 1) % 2):2048 * ((tt - 1) % 2 + 1)]
                for b in range(NB):
                    mm = e.matmul(p.pb[2 + b][0:16, 0:512],
                                  cb[:, CB_XMEAN + 16 * i:CB_XMEAN + 16 * (i + 1)],
                                  sl[:, 512 * b:512 * (b + 1)],
                                  start=(tt - 1 == 32), stop=(tt - 1 == 63))
                p.pe.inc(mm)
            Plan.aw(e, p.cast, cast0 + 16 * min(tt + 2, NT))
            for b in range(NB):
                i = tt * NB + b
                if i >= 2:
                    pb_, bb_ = divmod(i - 2, NB)
                    Plan.aw(e, p.dve, dve_base(pb_) + bb_ + 1)
                for c in range(DT):
                    mm = e.matmul(p.pb[i % 2][:, :],
                                  wbuf[:, 2048 * (tt % 2) + 128 * c:
                                       2048 * (tt % 2) + 128 * (c + 1)],
                                  xn(c, b), start=(c == 0), stop=(c == DT - 1))
                p.pe.inc(mm)
    p.blk.tensor(pj_pe)
    assert p.pe.val == pe_base(NT), ("cnt", "pe", p.pe.val, pe_base(NT))

    def pj_dve(e):
        jz = 0
        for tt in range(NT):
            base = dve_base(tt)
            for b in range(NB):
                i = tt * NB + b
                Plan.aw(e, p.pe, pe_base(tt) + b + 1)
                if is_xi(tt):
                    if b == 0 and tt >= 34:
                        Plan.aw(e, p.act, act_base(tt - 1))
                    dstt = arf[:, 2048 * (tt % 2) + 512 * b:
                               2048 * (tt % 2) + 512 * (b + 1)]
                    p.dve.inc(e.tensor_copy(dstt, p.pb[i % 2][:, :]))
                else:
                    sl = p.smf[:, 512 * (jz % 2):512 * (jz % 2 + 1)]
                    Plan.aw(e, p.st, st0 + 16 * (st_jz(tt) + b - 1))
                    p.dve.inc(e.tensor_copy(sl, p.pb[i % 2][:, :]))
                    jz += 1
            if is_xi(tt):
                asm = arf[:, 2048 * (tt % 2):2048 * (tt % 2 + 1)]
                acc = arf[:, 4096:6144]
                it = tt - 32
                if tt > 32:
                    Plan.aw(e, p.act, act_base(tt))
                e.tensor_scalar(acc, asm, cf[:, CF_WCONV + 4 * it + 3:
                                             CF_WCONV + 4 * it + 4],
                                None, ALU.mult)
                last = None
                for tau in range(3):
                    sh = 3 - tau
                    last = e.scalar_tensor_tensor(
                        acc[:, sh:2048], asm[:, 0:2048 - sh],
                        cf[:, CF_WCONV + 4 * it + tau:CF_WCONV + 4 * it + tau + 1],
                        acc[:, sh:2048], ALU.mult, ALU.add)
                p.dve.inc(last)
    p.blk.vector(pj_dve)
    assert p.dve.val == dve_base(NT), ("cnt", "dve", p.dve.val, dve_base(NT))

    def pj_act(e):
        for tt in range(NT):
            base = dve_base(tt)
            if is_xi(tt):
                Plan.aw(e, p.dve, base + 5)
                if tt >= 34:
                    Plan.aw(e, p.pe, pe_base(tt - 1) + 1)
                acc = arf[:, 4096:6144]
                sl = arb2[:, 2048 * (tt % 2):2048 * (tt % 2 + 1)]
                p.act.inc(e.activation(sl, acc, AF.Silu))
            else:
                for b in range(NB):
                    Plan.aw(e, p.dve, base + b + 1)
                    jz2 = st_jz(tt) + b
                    sl = p.smf[:, 512 * (jz2 % 2):512 * (jz2 % 2 + 1)]
                    if tt < 32:
                        dd = dr["zbuf"].ap()[128 * tt:128 * (tt + 1),
                                             BLK * b:BLK * (b + 1)]
                    else:
                        dd = dr["dtbc"].ap()[128 * (tt - 64):128 * (tt - 63),
                                             BLK * b:BLK * (b + 1)]
                    p.st.inc(e.dma_start(dd, sl), 16)
    p.blk.scalar(pj_act)
    assert p.act.val == act_base(NT), ("cnt", "act", p.act.val, act_base(NT))
    assert p.st.val == st0 + 16 * st_jz(NT), ("cnt", "st", p.st.val, st0 + 16 * st_jz(NT))
    p.fence()

    def xm_ev(e):
        for b in range(NB):
            last = e.tensor_copy(arf[0:16, 6144 + 512 * b:6144 + 512 * (b + 1)],
                                 p.pb[2 + b][0:16, 0:512])
        return last
    p.chain([("vector", xm_ev)])
    x_mean = arf[0:16, 6144:8192]
    if upto == "proj":
        emit_dump(p, dst, lambda t: arf[0:16, 6144:8192], rows=16, tiles=1)
        return

    dtS = arf[0:16, 0:2048]
    v_s = arf[0:16, 2048:4096]
    y_all = arf[0:16, 4096:6144]
    absb = arb2.bitcast(F32)[0:16, 2048:4096]
    ld00 = p.ld.val

    def b4_sync0(e):
        p.fwait(e, "dve", "act", "pe")
        Plan.aw(e, p.st, p.st.val)
        p.ld.inc(e.dma_start(dtS, dr["dtbc"].ap()[0:16, :]), 16)
    p.blk.sync(b4_sync0)
    assert p.ld.val == ld00 + 16, ("cnt", "ld", p.ld.val, ld00 + 16)

    def v1(e):
        Plan.aw(e, p.ld, ld00 + 16)
        e.tensor_scalar(dtS, dtS, cf[0:16, CF_DTBIAS:CF_DTBIAS + 1],
                        None, ALU.add)
        e.tensor_scalar_max(v_s, dtS, 0.0)
        e.tensor_scalar_mul(absb, dtS, -1.0)
        return e.tensor_tensor(absb, absb, dtS, ALU.max)

    def s1(e):
        e.activation(absb, absb, AF.Exp, scale=-1.0)
        return e.activation(absb, absb, AF.Ln, bias=1.0)

    def v2(e):
        e.tensor_add(dtS, v_s, absb)
        e.tensor_mul(v_s, dtS, x_mean)
        return e.tensor_scalar(dtS, dtS, cf[0:16, CF_NALOGE:CF_NALOGE + 1],
                               None, ALU.mult)

    def s2(e):
        return e.activation(dtS, dtS, AF.Exp)

    p.chain([("vector", v1), ("scalar", s1), ("vector", v2), ("scalar", s2)])

    Bq = lambda k: arf[:, 6144 + 512 * k:6144 + 512 * (k + 1)]
    Cq = lambda k: arf[:, 7168 + 512 * k:7168 + 512 * (k + 1)]
    vrep = lambda k: arf[:, 8192 + 512 * k:8192 + 512 * (k + 1)]
    darep = lambda k: arf[:, 9216 + 512 * k:9216 + 512 * (k + 1)]
    hq = lambda k: arf[:, 10240 + 512 * k:10240 + 512 * (k + 1)]
    hstate = p.smf2[:, 512:514]

    p.chain([("vector", lambda e: e.memset(hstate, 0.0))])

    for q in range(4):
        ldq = p.ld.val

        def b4_sync(e, q=q, ldq=ldq):
            Plan.aw(e, p.dve, p.dve.val)
            for k in range(2):
                p.ld.inc(e.dma_start(
                    Bq(k), dr["dtbc"].ap()[128 + 128 * k:256 + 128 * k,
                                           512 * q:512 * (q + 1)]), 16)
                p.ld.inc(e.dma_start(
                    Cq(k), dr["dtbc"].ap()[384 + 128 * k:512 + 128 * k,
                                           512 * q:512 * (q + 1)]), 16)
        p.blk.sync(b4_sync)
        assert p.ld.val == ldq + 64, ("cnt", "ld", p.ld.val, ldq + 64)

        def mkrep(e, q=q, ldq=ldq):
            Plan.aw(e, p.ld, ldq + 64)
            for k in range(2):
                e.matmul(p.pb[2 + k][:, :],
                         cf[0:16, CF_HREP + 128 * k:CF_HREP + 128 * (k + 1)],
                         v_s[0:16, 512 * q:512 * (q + 1)],
                         start=True, stop=True)
                mm = e.matmul(p.pb[4 + k][:, :],
                              cf[0:16, CF_HREP + 128 * k:CF_HREP + 128 * (k + 1)],
                              dtS[0:16, 512 * q:512 * (q + 1)],
                              start=True, stop=True)
            return mm

        def evrep(e, q=q):
            for k in range(2):
                e.tensor_copy(vrep(k), p.pb[2 + k][:, :])
                e.tensor_copy(darep(k), p.pb[4 + k][:, :])
            for k in range(2):
                last = e.tensor_mul(Bq(k), Bq(k), vrep(k))
            return last

        def scan(e, q=q):
            e.tensor_tensor_scan(hq(0), darep(0), Bq(0), hstate[:, 0:1],
                                 ALU.mult, ALU.add)
            return e.tensor_tensor_scan(hq(1), darep(1), Bq(1), hstate[:, 1:2],
                                        ALU.mult, ALU.add)

        def post(e, q=q):
            e.tensor_copy(hstate[:, 0:1], hq(0)[:, 511:512])
            e.tensor_copy(hstate[:, 1:2], hq(1)[:, 511:512])
            e.tensor_mul(Cq(0), Cq(0), hq(0))
            return e.tensor_mul(Cq(1), Cq(1), hq(1))

        def ymm(e, q=q):
            e.matmul(p.pb[6][0:16, 0:512], cf[:, CF_HCOL:CF_HCOL + 16],
                     Cq(0), start=True, stop=False)
            return e.matmul(p.pb[6][0:16, 0:512],
                            cf[:, CF_HCOL + 16:CF_HCOL + 32],
                            Cq(1), start=False, stop=True)

        def yev(e, q=q):
            return e.tensor_copy(y_all[0:16, 512 * q:512 * (q + 1)],
                                 p.pb[6][0:16, 0:512])

        p.chain([("tensor", mkrep), ("vector", evrep), ("vector", scan),
                 ("vector", post), ("tensor", ymm), ("vector", yev)])
    if upto == "scan":
        emit_dump(p, dst, lambda t: y_all, rows=16, tiles=1)
        return

    wh2 = arf[0:16, 6144:10240]
    ldw = p.ld.val

    def b5_sync0(e):
        Plan.aw(e, p.dve, p.dve.val)
        p.ld.inc(e.dma_start(wh2, dr["w_h2iT"].ap()), 16)
    p.blk.sync(b5_sync0)
    assert p.ld.val == ldw + 16, ("cnt", "ld", p.ld.val, ldw + 16)

    for half in range(2):
        cast0, pe0, act0, dve0, ld0, st0 = (p.cast.val, p.pe.val, p.act.val,
                                            p.dve.val, p.ld.val, p.st.val)
        PE_B5, DVE_B5, ACT_B5, LD_B5 = 40, 40, 32, 40

        def h_cast(e, half=half, cast0=cast0, pe0=pe0):
            Plan.aw(e, p.cast, cast0)
            if half == 0:
                p.fwait(e, "pe")
            else:
                Plan.aw(e, p.pe, pe0)
            for k in range(8):
                p.cast.inc(e.dma_start(
                    arb1[:, 4096 * k:4096 * (k + 1)],
                    dr["w_outT"].ap()[128 * (8 * half + k):
                                      128 * (8 * half + k + 1), :]), 16)
        p.blk.gpsimd(h_cast)
        assert p.cast.val == cast0 + 128, ("cnt", "cast", p.cast.val, cast0 + 128)

        def b5_sync(e, half=half, ld0=ld0, dve0=dve0):
            Plan.aw(e, p.ld, ld0)
            if half == 1:
                Plan.aw(e, p.dve, dve0)
            for b in range(NB):
                for it in range(32):
                    if it >= 3:
                        Plan.aw(e, p.dve, dve0 + DVE_B5 * b + it - 2)
                    elif b > 0:
                        Plan.aw(e, p.dve, dve0 + DVE_B5 * b)
                    p.ld.inc(e.dma_start(
                        xb[:, 512 * (it % 3):512 * (it % 3 + 1)],
                        dr["zbuf"].ap()[128 * it:128 * (it + 1),
                                        BLK * b:BLK * (b + 1)]), 16)
                for dt_ in range(8):
                    j = 32 + dt_
                    Plan.aw(e, p.dve, dve0 + DVE_B5 * b + 30 + min(dt_, 2)
                            if dt_ < 3 else dve0 + DVE_B5 * b + 32 + dt_ - 2)
                    p.ld.inc(e.dma_start(
                        xb[:, 512 * (j % 3):512 * (j % 3 + 1)],
                        dr["r1buf"].ap()[128 * (8 * half + dt_):
                                         128 * (8 * half + dt_ + 1),
                                         BLK * b:BLK * (b + 1)]), 16)
        p.blk.sync(b5_sync)
        assert p.ld.val == ld0 + 16 * LD_B5 * NB, ("cnt", "ld", p.ld.val, ld0 + 16 * LD_B5 * NB)

        def b5_pe(e, half=half, cast0=cast0, pe0=pe0, act0=act0, dve0=dve0,
                  ld0=ld0):
            if half == 0:
                p.fwait(e, "dve", "act")
                Plan.aw(e, p.ld, ldw + 16)
            Plan.aw(e, p.cast, cast0 + 128)
            for b in range(NB):
                for it in range(32):
                    if it >= 2:
                        Plan.aw(e, p.dve, dve0 + DVE_B5 * b + it - 1)
                    elif b + half > 0:
                        Plan.aw(e, p.dve, dve0 + DVE_B5 * b)
                    p.pe.inc(e.matmul(
                        p.pb[it % 2][:, :], wh2[0:16, 128 * it:128 * (it + 1)],
                        y_all[0:16, 512 * b:512 * (b + 1)],
                        start=True, stop=True))
                Plan.aw(e, p.dve, dve0 + DVE_B5 * b + 32)
                for dt_ in range(8):
                    if dt_ >= 2:
                        Plan.aw(e, p.dve, dve0 + DVE_B5 * b + 32 + dt_ - 1)
                    for c in range(32):
                        mm = e.matmul(p.pb[2 + dt_ % 2][:, :],
                                      arb1[:, 4096 * dt_ + 128 * c:
                                           4096 * dt_ + 128 * (c + 1)],
                                      arb2[:, 512 * c:512 * (c + 1)],
                                      start=(c == 0), stop=(c == 31))
                    p.pe.inc(mm)
        p.blk.tensor(b5_pe)
        assert p.pe.val == pe0 + PE_B5 * NB, ("cnt", "pe", p.pe.val, pe0 + PE_B5 * NB)

        def b5_act(e, half=half, pe0=pe0, dve0=dve0, ld0=ld0, st0=st0):
            for b in range(NB):
                for it in range(32):
                    Plan.aw(e, p.pe, pe0 + PE_B5 * b + it + 1)
                    Plan.dwait(e, p.ld, ld0, min(LD_B5 * b + it + 3, LD_B5 * (b + 1)), LD_B5 * NB)
                    if it >= 2:
                        Plan.aw(e, p.dve, dve0 + DVE_B5 * b + it - 1)
                    sg = p.smf[:, 512 * (it % 2):512 * (it % 2 + 1)]
                    z = xb[:, 512 * (it % 3):512 * (it % 3 + 1)]
                    p.act.inc(e.activation(sg, z, AF.Silu))
                for dt_ in range(8):
                    Plan.aw(e, p.dve, dve0 + DVE_B5 * b + 32 + dt_ + 1)
                    out = p.smf2[:, 512 * (dt_ % 2):512 * (dt_ % 2 + 1)]
                    p.st.inc(e.dma_start(
                        dst.ap()[128 * (8 * half + dt_):
                                 128 * (8 * half + dt_ + 1),
                                 BLK * b:BLK * (b + 1)], out), 16)
        p.blk.scalar(b5_act)
        assert p.act.val == act0 + ACT_B5 * NB, ("cnt", "act", p.act.val, act0 + ACT_B5 * NB)
        assert p.st.val == st0 + 16 * 8 * NB

        def b5_dve(e, half=half, pe0=pe0, act0=act0, dve0=dve0, ld0=ld0,
                   st0=st0):
            for b in range(NB):
                for it in range(32):
                    Plan.aw(e, p.act, act0 + ACT_B5 * b + it + 1)
                    Plan.aw(e, p.pe, pe0 + PE_B5 * b + it + 1)
                    sg = p.smf[:, 512 * (it % 2):512 * (it % 2 + 1)]
                    p.dve.inc(e.tensor_mul(arb2[:, 512 * it:512 * (it + 1)],
                                           p.pb[it % 2][:, :], sg))
                for dt_ in range(8):
                    Plan.aw(e, p.pe, pe0 + PE_B5 * b + 32 + dt_ + 1)
                    Plan.dwait(e, p.ld, ld0, min(LD_B5 * b + 32 + dt_ + 3,
                               LD_B5 * (b + 1)), LD_B5 * NB)
                    if dt_ >= 2:
                        Plan.aw(e, p.st, st0 + 16 * (8 * b + dt_ - 1))
                    j = 32 + dt_
                    rp = xb[:, 512 * (j % 3):512 * (j % 3 + 1)]
                    out = p.smf2[:, 512 * (dt_ % 2):512 * (dt_ % 2 + 1)]
                    p.dve.inc(e.tensor_add(out, p.pb[2 + dt_ % 2][:, :], rp))
        p.blk.vector(b5_dve)
        assert p.dve.val == dve0 + DVE_B5 * NB, ("cnt", "dve", p.dve.val, dve0 + DVE_B5 * NB)

        p.fence()


def emit_attn_l1(p, dr, src, dst, upto=None):
    cb, cf = p.cbuf, p.cbuf_f
    arb1, arb2, arf, wbuf = p.arb1, p.arb2, p.arf, p.wbuf
    xb, smb, smf, smf2 = p.xbuf, p.smb, p.smf, p.smf2
    SC = float(1.0 / np.sqrt(HDIM))

    def tm_store(name, ot, b):
        base = dr[name].ap()[512 * b:512 * (b + 1), 128 * ot:128 * (ot + 1)]
        return mkap(base, [[T, 128], [128 * T, 4], [1, 128]])

    def tm_load(name, h):
        base = dr[name].ap()[0:T, 128 * h:128 * (h + 1)]
        return mkap(base, [[T, 128], [128 * T, 16], [1, 128]])

    def dump_dram(name):
        g0 = p.gp.val
        p.fence()

        def body(e):
            p.fwait(e, "pe", "dve", "act")
            Plan.aw(e, p.st, p.st.val)
            p.gp.inc(e.dma_start(dst.ap()[0:D, 0:T], dr[name].ap()), 16)
            e.wait_ge(p.gp.h, g0 + 16)
        p.blk.gpsimd(body)
        p.fence()

    emit_ln(p, src, arb1, 0)
    xn = lambda c, b: arb1[:, 2048 * c + 512 * b:2048 * c + 512 * (b + 1)]
    if upto == "attn_ln":
        emit_dump(p, dst, lambda t: arb1[:, 2048 * t:2048 * (t + 1)])
        return

    for wname, xsp, fsp in (("wqT", "qspill", "qfspill"),
                            ("wkT", "kspill", None)):
        isk = fsp is None
        PEb = 4 if isk else 2
        ACTb = 4 if isk else 2
        peW0 = p.pe.val
        for pr in range(8):
            cast0, pe0, act0, dve0, st0, ld0 = (p.cast.val, p.pe.val,
                                                p.act.val, p.dve.val,
                                                p.st.val, p.ld.val)
            ws = 4096 * (pr % 2)

            def c2_cast(e, pr=pr, wname=wname, cast0=cast0, ws=ws, peW0=peW0,
                        PEb=PEb):
                Plan.aw(e, p.cast, cast0)
                Plan.aw(e, p.pe, peW0 + PEb * NB * max(pr - 1, 0))
                p.cast.inc(e.dma_start(
                    wbuf[:, ws:ws + 2048],
                    dr[wname].ap()[128 * pr:128 * (pr + 1), :]), 16)
                p.cast.inc(e.dma_start(
                    wbuf[:, ws + 2048:ws + 4096],
                    dr[wname].ap()[128 * (pr + 8):128 * (pr + 9), :]), 16)
            p.blk.gpsimd(c2_cast)
            assert p.cast.val == cast0 + 32

            def c2_sync(e, pr=pr, ld0=ld0, dve0=dve0, pe0=pe0, act0=act0):
                Plan.aw(e, p.ld, ld0)
                Plan.aw(e, p.dve, dve0)
                Plan.aw(e, p.pe, pe0)
                Plan.aw(e, p.act, act0)
                for j, nm in enumerate(("rope_cos", "rope_sin")):
                    p.ld.inc(e.dma_start(
                        arf[:, 2048 * j:2048 * (j + 1)],
                        dr[nm].ap()[128 * pr:128 * (pr + 1), :]), 16)
                for j, nm in enumerate(("rope_cos", "rope_sin")):
                    p.ld.inc(e.dma_start(
                        arf[:, 4096 + 2048 * j:4096 + 2048 * (j + 1)],
                        dr[nm].ap()[128 * (pr + 8):128 * (pr + 9), :]), 16)
            p.blk.sync(c2_sync)
            assert p.ld.val == ld0 + 64

            def c2_pe(e, cast0=cast0, dve0=dve0, act0=act0, ws=ws, isk=isk):
                p.fwait(e, "dve", "act")
                Plan.aw(e, p.cast, cast0 + 32)
                for b in range(NB):
                    if b > 0:
                        Plan.aw(e, p.dve, dve0 + 14 * b)
                    else:
                        Plan.aw(e, p.dve, max(dve0 - 9, 0))
                    for c in range(DT):
                        mm = e.matmul(p.pb[0][:, :],
                                      wbuf[:, ws + 128 * c:ws + 128 * (c + 1)],
                                      xn(c, b), start=(c == 0), stop=(c == DT - 1))
                    p.pe.inc(mm)
                    for c in range(DT):
                        mm = e.matmul(
                            p.pb[1][:, :],
                            wbuf[:, ws + 2048 + 128 * c:ws + 2048 + 128 * (c + 1)],
                            xn(c, b), start=(c == 0), stop=(c == DT - 1))
                    p.pe.inc(mm)
                    if isk:
                        Plan.aw(e, p.dve, dve0 + 14 * b + 10)
                        Plan.aw(e, p.act, max(act0 + 4 * b, 0))
                        for j in range(4):
                            mm = e.transpose(
                                p.pbt[:, 128 * j:128 * (j + 1)],
                                smb[:, 1024 + 128 * j:1024 + 128 * (j + 1)],
                                cb[:, CB_IDN:CB_IDN + 128])
                        p.pe.inc(mm)
                        Plan.aw(e, p.dve, dve0 + 14 * b + 14)
                        Plan.aw(e, p.act, max(act0 + 4 * b + 3, 0))
                        for j in range(4):
                            mm = e.transpose(
                                p.pbt[:, 512 + 128 * j:512 + 128 * (j + 1)],
                                smb[:, 1536 + 128 * j:1536 + 128 * (j + 1)],
                                cb[:, CB_IDN:CB_IDN + 128])
                        p.pe.inc(mm)
            p.blk.tensor(c2_pe)
            assert p.pe.val == pe0 + PEb * NB

            def c2_dve(e, pe0=pe0, act0=act0, dve0=dve0, st0=st0, ld0=ld0,
                       PEb=PEb):
                for b in range(NB):
                    Plan.aw(e, p.pe, pe0 + PEb * b + 2)
                    Plan.aw(e, p.act, act0 + ACTb * b)
                    Plan.aw(e, p.st, st0 + 64 * b)
                    if b == 0:
                        Plan.aw(e, p.ld, ld0 + 64)
                    cA = arf[:, 512 * b:512 * (b + 1)]
                    sA = arf[:, 2048 + 512 * b:2048 + 512 * (b + 1)]
                    cB = arf[:, 4096 + 512 * b:4096 + 512 * (b + 1)]
                    sB = arf[:, 6144 + 512 * b:6144 + 512 * (b + 1)]
                    t1, t2 = smf[:, 0:512], smf[:, 512:1024]
                    xA, xB = smf2[:, 0:512], smf2[:, 512:1024]
                    p.dve.inc(e.tensor_mul(t1, p.pb[0][:, :], cA))
                    p.dve.inc(e.tensor_mul(t2, p.pb[1][:, :], sA))
                    p.dve.inc(e.tensor_sub(xA, t1, t2))
                    p.dve.inc(e.tensor_mul(t1, p.pb[1][:, :], cB))
                    p.dve.inc(e.tensor_mul(t2, p.pb[0][:, :], sB))
                    p.dve.inc(e.tensor_add(xB, t1, t2))
                    p.dve.inc(e.tensor_copy(smb[:, 0:512], xA))
                    p.dve.inc(e.tensor_scalar(t1, xA, 0.0, None, ALU.min))
                    Plan.aw(e, p.act, act0 + ACTb * b + 1)
                    p.dve.inc(e.tensor_scalar(t1, xA, 0.0, None, ALU.max))
                    p.dve.inc(e.tensor_add(smb[:, 1024:1536], t2, t1))
                    p.dve.inc(e.tensor_copy(smb[:, 512:1024], xB))
                    p.dve.inc(e.tensor_scalar(t1, xB, 0.0, None, ALU.min))
                    Plan.aw(e, p.act, act0 + ACTb * b + 2)
                    p.dve.inc(e.tensor_scalar(t1, xB, 0.0, None, ALU.max))
                    p.dve.inc(e.tensor_add(smb[:, 1536:2048], t2, t1))
            p.blk.vector(c2_dve)
            assert p.dve.val == dve0 + 14 * NB

            def c2_act(e, dve0=dve0, act0=act0, st0=st0, pe0=pe0, pr=pr,
                       isk=isk, xsp=xsp, fsp=fsp, PEb=PEb):
                for b in range(NB):
                    Plan.aw(e, p.st, st0 + 64 * b)
                    Plan.aw(e, p.dve, dve0 + 14 * b + 8)
                    p.act.inc(e.activation(smf[:, 512:1024], smf[:, 0:512],
                                           AF.Exp))
                    p.st.inc(e.dma_start(
                        dr[xsp].ap()[128 * pr:128 * (pr + 1),
                                     512 * b:512 * (b + 1)],
                        smb[:, 0:512]), 16)
                    Plan.aw(e, p.dve, dve0 + 14 * b + 12)
                    p.act.inc(e.activation(smf[:, 512:1024], smf[:, 0:512],
                                           AF.Exp))
                    p.st.inc(e.dma_start(
                        dr[xsp].ap()[128 * (pr + 8):128 * (pr + 9),
                                     512 * b:512 * (b + 1)],
                        smb[:, 512:1024]), 16)
                    if isk:
                        Plan.aw(e, p.pe, pe0 + PEb * b + 3)
                        p.act.inc(e.copy(arb2[:, 0:512], p.pbt[:, 0:512]))
                        p.st.inc(e.dma_start(tm_store("kfspill", pr, b),
                                             arb2[:, 0:512]), 16)
                        Plan.aw(e, p.pe, pe0 + PEb * b + 4)
                        p.act.inc(e.copy(arb2[:, 512:1024], p.pbt[:, 512:1024]))
                        p.st.inc(e.dma_start(tm_store("kfspill", pr + 8, b),
                                             arb2[:, 512:1024]), 16)
                    else:
                        p.st.inc(e.dma_start(
                            dr[fsp].ap()[128 * pr:128 * (pr + 1),
                                         512 * b:512 * (b + 1)],
                            smb[:, 1024:1536]), 16)
                        Plan.aw(e, p.dve, dve0 + 14 * b + 14)
                        p.st.inc(e.dma_start(
                            dr[fsp].ap()[128 * (pr + 8):128 * (pr + 9),
                                         512 * b:512 * (b + 1)],
                            smb[:, 1536:2048]), 16)
            p.blk.scalar(c2_act)
            assert p.act.val == act0 + ACTb * NB
            assert p.st.val == st0 + 16 * 4 * NB

    if upto in ("attn_v_slow", "attn_v_slowb"):
        pe_s, dve_s, act_s, st_s = p.pe.val, p.dve.val, p.act.val, p.st.val
        for ot in range(16):
            c0 = p.cast.val

            def vs_cast(e, ot=ot, c0=c0):
                p.fwait(e, "pe", "dve", "act")
                Plan.aw(e, p.pe, pe_s)
                Plan.aw(e, p.dve, dve_s)
                Plan.aw(e, p.act, act_s)
                Plan.aw(e, p.st, st_s)
                Plan.aw(e, p.cast, c0)
                p.cast.inc(e.dma_start(
                    wbuf[:, 0:2048],
                    dr["wvT"].ap()[128 * ot:128 * (ot + 1), :]), 16)
                e.wait_ge(p.cast.h, c0 + 16)
            p.blk.gpsimd(vs_cast)
            p.fence()
            for b in range(NB):
                def s_mm(e, ot=ot, b=b, c0=c0):
                    Plan.aw(e, p.cast, c0 + 16)
                    for c in range(DT):
                        mm = e.matmul(p.pb[0][:, :],
                                      wbuf[:, 128 * c:128 * (c + 1)],
                                      xn(c, b), start=(c == 0),
                                      stop=(c == DT - 1))
                    return mm

                def s_vbf(e):
                    return e.tensor_scalar_add(smb[:, 0:512], p.pb[0][:, :],
                                               0.0)

                def s_tr(e):
                    for j in range(4):
                        mm = e.transpose(p.pbt[:, 128 * j:128 * (j + 1)],
                                         smb[:, 128 * j:128 * (j + 1)],
                                         cb[:, CB_IDN:CB_IDN + 128])
                    return mm

                st_pre = p.st.val

                def s_ev(e, st_pre=st_pre):
                    Plan.aw(e, p.st, st_pre)
                    return e.tensor_copy(smb[:, 1024:1536], p.pbt[:, 0:512])

                def s_st(e, ot=ot, b=b):
                    p.st.inc(e.dma_start(tm_store("vtok", ot, b),
                                         smb[:, 1024:1536]), 16)
                    p.st.inc(e.dma_start(
                        dr["combspill"].ap()[128 * ot:128 * (ot + 1),
                                             512 * b:512 * (b + 1)],
                        smb[:, 0:512]), 16)
                    return e.activation(p.smf[0:1, 0:1], p.smf[0:1, 0:1],
                                        AF.Copy)
                p.chain([("tensor", s_mm), ("vector", s_vbf), ("tensor", s_tr),
                         ("vector", s_ev), ("scalar", s_st)])
        dump_dram("vtok" if upto == "attn_v_slow" else "combspill")
        return

    castV0, peV0, stV0 = p.cast.val, p.pe.val, p.st.val
    for ot in range(16):
        cast0, pe0, dve0, act0, st0 = (p.cast.val, p.pe.val, p.dve.val,
                                       p.act.val, p.st.val)

        def v_cast(e, ot=ot, cast0=cast0):
            Plan.aw(e, p.cast, cast0)
            Plan.aw(e, p.pe, peV0 + 8 * max(ot - 1, 0))
            p.cast.inc(e.dma_start(
                wbuf[:, 2048 * (ot % 2):2048 * (ot % 2 + 1)],
                dr["wvT"].ap()[128 * ot:128 * (ot + 1), :]), 16)
        p.blk.gpsimd(v_cast)

        def v_pe(e, cast0=cast0, dve0=dve0, ot=ot):
            Plan.aw(e, p.cast, cast0 + 16)
            for b in range(NB):
                Plan.aw(e, p.dve, max(dve0 + 2 * b - 3, 0))
                for c in range(DT):
                    mm = e.matmul(
                        p.pb[b % 2][:, :],
                        wbuf[:, 2048 * (ot % 2) + 128 * c:
                             2048 * (ot % 2) + 128 * (c + 1)],
                        xn(c, b), start=(c == 0), stop=(c == DT - 1))
                p.pe.inc(mm)
                Plan.aw(e, p.dve, dve0 + 2 * b + 1)
                for j in range(4):
                    mm = e.transpose(
                        p.pbt[:, 512 * (b % 2) + 128 * j:
                              512 * (b % 2) + 128 * (j + 1)],
                        smb[:, 512 * (b % 2) + 128 * j:
                            512 * (b % 2) + 128 * (j + 1)],
                        cb[:, CB_IDN:CB_IDN + 128])
                p.pe.inc(mm)
        p.blk.tensor(v_pe)
        assert p.pe.val == pe0 + 2 * NB

        def v_dve(e, pe0=pe0, dve0=dve0, ot=ot):
            if ot == 0:
                Plan.aw(e, p.st, stV0)
            for b in range(NB):
                Plan.aw(e, p.pe, pe0 + 2 * b + 1)
                p.dve.inc(e.tensor_copy(smb[:, 512 * (b % 2):512 * (b % 2 + 1)],
                                        p.pb[b % 2][:, :]))
                Plan.aw(e, p.pe, pe0 + 2 * b + 2)
                Plan.aw(e, p.st, stV0 + 16 * max(4 * ot + b - 1, 0))
                p.dve.inc(e.tensor_copy(smb[:, 1024 + 512 * (b % 2):
                                            1024 + 512 * (b % 2 + 1)],
                                        p.pbt[:, 512 * (b % 2):
                                              512 * (b % 2 + 1)]))
        p.blk.vector(v_dve)
        assert p.dve.val == dve0 + 2 * NB

        def v_act(e, dve0=dve0, ot=ot):
            for b in range(NB):
                Plan.aw(e, p.dve, dve0 + 2 * b + 2)
                p.st.inc(e.dma_start(tm_store("vtok", ot, b),
                                     smb[:, 1024 + 512 * (b % 2):
                                         1024 + 512 * (b % 2 + 1)]), 16)
        p.blk.scalar(v_act)
        assert p.st.val == st0 + 16 * NB

    pe_c2_end = p.pe.val
    st_c2 = p.st.val
    if upto in ("attn_q", "attn_k", "attn_qf", "attn_kft", "attn_v"):
        dump_dram({"attn_q": "qspill", "attn_k": "kspill",
                   "attn_qf": "qfspill", "attn_kft": "kfspill",
                   "attn_v": "vtok"}[upto])
        return

    QH, KH, QFH, KFT, VH = 0, 2048, 4096, 6144, 8192
    MEMB, NORMB = 10240, 10368
    for h in range(16):
        ld0, pe0, dve0, act0, st0 = (p.ld.val, p.pe.val, p.dve.val,
                                     p.act.val, p.st.val)

        def h_sync(e, h=h, ld0=ld0, pe0=pe0):
            Plan.aw(e, p.ld, ld0)
            Plan.aw(e, p.pe, pe0)
            Plan.aw(e, p.st, st_c2)
            p.ld.inc(e.dma_start(arb2[:, QH:QH + 2048],
                                 dr["qspill"].ap()[128 * h:128 * (h + 1), :]), 16)
            p.ld.inc(e.dma_start(arb2[:, KH:KH + 2048],
                                 dr["kspill"].ap()[128 * h:128 * (h + 1), :]), 16)
            p.ld.inc(e.dma_start(arb2[:, QFH:QFH + 2048],
                                 dr["qfspill"].ap()[128 * h:128 * (h + 1), :]), 16)
            p.ld.inc(e.dma_start(arb2[:, KFT:KFT + 2048],
                                 tm_load("kfspill", h)), 16)
            p.ld.inc(e.dma_start(arb2[:, VH:VH + 2048],
                                 tm_load("vtok", h)), 16)
        p.blk.sync(h_sync)
        assert p.ld.val == ld0 + 80

        def h_pe(e, ld0=ld0, pe0=pe0, dve0=dve0, act0=act0):
            Plan.aw(e, p.ld, ld0 + 80)
            Plan.aw(e, p.dve, dve0)
            Plan.aw(e, p.act, act0)
            for c in range(DT):
                mm = e.matmul(p.pb[2][:, 0:128],
                              arb2[:, KFT + 128 * c:KFT + 128 * (c + 1)],
                              arb2[:, VH + 128 * c:VH + 128 * (c + 1)],
                              start=(c == 0), stop=(c == DT - 1))
            p.pe.inc(mm)
            for c in range(DT):
                mm = e.matmul(p.pb[3][:, 0:1],
                              arb2[:, KFT + 128 * c:KFT + 128 * (c + 1)],
                              cb[:, CB_ONES_COL:CB_ONES_COL + 1],
                              start=(c == 0), stop=(c == DT - 1))
            p.pe.inc(mm)
            for qb in range(4):
                dve_q0 = dve0 + 2 + 9 * qb
                if qb == 0:
                    Plan.aw(e, p.dve, dve0 + 2)
                else:
                    Plan.aw(e, p.dve, dve_q0 - 9 + 6)
                    Plan.aw(e, p.act, act0 + 16 * qb)
                p.pe.inc(e.matmul(p.pb[4][:, :], arb2[:, MEMB:MEMB + 128],
                                  arb2[:, QFH + 512 * qb:QFH + 512 * (qb + 1)],
                                  start=True, stop=True))
                p.pe.inc(e.matmul(p.pb[5][0:1, :], arb2[:, NORMB:NORMB + 1],
                                  arb2[:, QFH + 512 * qb:QFH + 512 * (qb + 1)],
                                  start=True, stop=True))
                for kc in range(16):
                    Plan.aw(e, p.act, act0 + 16 * qb + max(kc - 1, 0))
                    p.pe.inc(e.matmul(
                        p.pb[kc % 2][:, :],
                        arb2[:, KH + 128 * kc:KH + 128 * (kc + 1)],
                        arb2[:, QH + 512 * qb:QH + 512 * (qb + 1)],
                        start=True, stop=True))
                for kc in range(16):
                    Plan.aw(e, p.act, act0 + 16 * qb + kc + 1)
                    pt = arb1[:, 8192 * (qb % 2) + 512 * kc:
                              8192 * (qb % 2) + 512 * (kc + 1)]
                    mmv = e.matmul(p.pb[6][:, :],
                                   arb2[:, VH + 128 * kc:VH + 128 * (kc + 1)],
                                   pt, start=(kc == 0), stop=(kc == 15))
                    mmd = e.matmul(p.pb[3][0:1, :],
                                   cb[:, CB_ONES_COL:CB_ONES_COL + 1],
                                   pt, start=(kc == 0), stop=(kc == 15))
                p.pe.inc(mmv)
                p.pe.inc(mmd)
                Plan.aw(e, p.act, act0 + 16 * (qb + 1))
                Plan.aw(e, p.dve, dve_q0 + 1)
                p.pe.inc(e.matmul(p.pb[0][:, :],
                                  cf[0:1, CF_ONES_ROW:CF_ONES_ROW + 128],
                                  p.rows[0:1, 0:512], start=True, stop=True))
                Plan.aw(e, p.dve, dve_q0 + 2)
                p.pe.inc(e.matmul(p.pb[1][:, :],
                                  cf[0:1, CF_ONES_ROW:CF_ONES_ROW + 128],
                                  p.rows[0:1, 512:1024], start=True, stop=True))
        p.blk.tensor(h_pe)
        assert p.pe.val == pe0 + 90

        def h_act(e, pe0=pe0, dve0=dve0, h=h):
            for qb in range(4):
                pe_q0 = pe0 + 2 + 22 * qb
                for kc in range(16):
                    Plan.aw(e, p.pe, pe_q0 + 3 + kc)
                    p.act.inc(e.activation(
                        arb1[:, 8192 * (qb % 2) + 512 * kc:
                             8192 * (qb % 2) + 512 * (kc + 1)],
                        p.pb[kc % 2][:, :], AF.Exp, scale=SC))
                Plan.aw(e, p.dve, dve0 + 2 + 9 * qb + 9)
                p.st.inc(e.dma_start(
                    dr["combspill"].ap()[128 * h:128 * (h + 1),
                                         512 * qb:512 * (qb + 1)],
                    smb[:, 512 * (qb % 2):512 * (qb % 2 + 1)]), 16)
        p.blk.scalar(h_act)
        assert p.act.val == act0 + 64
        assert p.st.val == st0 + 64

        def h_dve(e, pe0=pe0, dve0=dve0, st0=st0, act0=act0, h=h):
            Plan.aw(e, p.act, act0)
            Plan.aw(e, p.st, st_c2)
            Plan.aw(e, p.pe, pe0 + 1)
            p.dve.inc(e.tensor_copy(arb2[:, MEMB:MEMB + 128], p.pb[2][:, 0:128]))
            Plan.aw(e, p.pe, pe0 + 2)
            p.dve.inc(e.tensor_copy(arb2[:, NORMB:NORMB + 1], p.pb[3][:, 0:1]))
            for qb in range(4):
                pe_q0 = pe0 + 2 + 22 * qb
                Plan.aw(e, p.pe, pe_q0 + 20)
                p.dve.inc(e.reciprocal(p.rows[0:1, 0:512], p.pb[3][0:1, :]))
                p.dve.inc(e.reciprocal(p.rows[0:1, 512:1024], p.pb[5][0:1, :]))
                Plan.aw(e, p.pe, pe_q0 + 21)
                p.dve.inc(e.tensor_copy(smf2[:, 0:512], p.pb[0][:, :]))
                Plan.aw(e, p.pe, pe_q0 + 22)
                p.dve.inc(e.tensor_copy(smf2[:, 512:1024], p.pb[1][:, :]))
                p.dve.inc(e.tensor_mul(smf[:, 0:512], p.pb[4][:, :],
                                       smf2[:, 512:1024]))
                p.dve.inc(e.tensor_mul(smf[:, 512:1024], p.pb[6][:, :],
                                       smf2[:, 0:512]))
                p.dve.inc(e.tensor_sub(smf2[:, 0:512], smf[:, 0:512],
                                       smf[:, 512:1024]))
                p.dve.inc(e.tensor_scalar(
                    smf2[:, 512:1024], smf2[:, 0:512],
                    cf[:, CF_GREP + h:CF_GREP + h + 1], None, ALU.mult))
                if qb >= 2:
                    Plan.aw(e, p.st, st0 + 16 * (qb - 1))
                p.dve.inc(e.tensor_add(smb[:, 512 * (qb % 2):512 * (qb % 2 + 1)],
                                       smf2[:, 512:1024], smf[:, 512:1024]))
        p.blk.vector(h_dve)
        assert p.dve.val == dve0 + 38

    st_c3 = p.st.val
    pe_c3 = p.pe.val
    if upto == "attn_comb":
        dump_dram("combspill")
        return

    cast0, ld0, pe0, dve0, act0, st0 = (p.cast.val, p.ld.val, p.pe.val,
                                        p.dve.val, p.act.val, p.st.val)

    woW = lambda dt_: (16384 + 2048 * dt_) if dt_ < 8 else 2048 * (dt_ - 8)

    def wo_cast(e):
        Plan.aw(e, p.cast, cast0)
        Plan.aw(e, p.pe, pe_c2_end)
        for dt_ in range(16):
            if dt_ == 8:
                Plan.aw(e, p.pe, pe0)
            p.cast.inc(e.dma_start(
                arb1[:, woW(dt_):woW(dt_) + 2048],
                dr["woT"].ap()[128 * dt_:128 * (dt_ + 1), :]), 16)
    p.blk.gpsimd(wo_cast)
    assert p.cast.val == cast0 + 256

    def wo_sync(e):
        Plan.aw(e, p.ld, ld0)
        Plan.aw(e, p.act, act0)
        for b in range(NB):
            if b == 0:
                Plan.aw(e, p.st, st_c3)
                Plan.aw(e, p.pe, pe0)
            elif b >= 2:
                Plan.aw(e, p.pe, pe0 + 16 * (b - 1))
            for c in range(16):
                p.ld.inc(e.dma_start(
                    arb2[:, 8192 * (b % 2) + 512 * c:
                         8192 * (b % 2) + 512 * (c + 1)],
                    dr["combspill"].ap()[128 * c:128 * (c + 1),
                                         512 * b:512 * (b + 1)]), 16)
            for dt_ in range(16):
                i = 16 * b + dt_
                if i >= 3:
                    Plan.aw(e, p.dve, dve0 + i - 2)
                p.ld.inc(e.dma_start(
                    xb[:, 512 * (i % 3):512 * (i % 3 + 1)],
                    src.ap()[128 * dt_:128 * (dt_ + 1),
                             512 * b:512 * (b + 1)]), 16)
    p.blk.sync(wo_sync)
    assert p.ld.val == ld0 + 16 * 32 * NB

    def wo_pe(e):
        for b in range(NB):
            Plan.dwait(e, p.ld, ld0, min(32 * b + 16, 32 * (b + 1)), 32 * NB)
            for dt_ in range(16):
                i = 16 * b + dt_
                Plan.dwait(e, p.cast, cast0, dt_ + 1, 16)
                if i >= 2:
                    Plan.aw(e, p.dve, dve0 + i - 1)
                else:
                    Plan.aw(e, p.dve, dve0)
                for c in range(16):
                    mm = e.matmul(
                        p.pb[i % 2][:, :],
                        arb1[:, woW(dt_) + 128 * c:woW(dt_) + 128 * (c + 1)],
                        arb2[:, 8192 * (b % 2) + 512 * c:
                             8192 * (b % 2) + 512 * (c + 1)],
                        start=(c == 0), stop=(c == 15))
                p.pe.inc(mm)
    p.blk.tensor(wo_pe)
    assert p.pe.val == pe0 + 16 * NB

    def wo_dve(e):
        for b in range(NB):
            for dt_ in range(16):
                i = 16 * b + dt_
                Plan.aw(e, p.pe, pe0 + i + 1)
                Plan.dwait(e, p.ld, ld0, min(32 * b + 16 + dt_ + 1,
                                             32 * (b + 1)), 32 * NB)
                if i >= 2:
                    Plan.aw(e, p.st, st0 + 16 * (i - 1))
                p.dve.inc(e.tensor_add(smf2[:, 512 * (i % 2):512 * (i % 2 + 1)],
                                       p.pb[i % 2][:, :],
                                       xb[:, 512 * (i % 3):512 * (i % 3 + 1)]))
    p.blk.vector(wo_dve)
    assert p.dve.val == dve0 + 16 * NB

    def wo_act(e):
        for b in range(NB):
            for dt_ in range(16):
                i = 16 * b + dt_
                Plan.aw(e, p.dve, dve0 + i + 1)
                p.st.inc(e.dma_start(
                    dst.ap()[128 * dt_:128 * (dt_ + 1),
                             512 * b:512 * (b + 1)],
                    smf2[:, 512 * (i % 2):512 * (i % 2 + 1)]), 16)
    p.blk.scalar(wo_act)
    assert p.st.val == st0 + 16 * 16 * NB
    p.fence()



def emit_dequant(p, dr):
    dqb = p.wbuf
    dqi = p.wbuf.bitcast(I32)
    sfb = p.wbuf.bitcast(F16)
    SF32 = p.wbuf.bitcast(F32)[:, 5888:5932]
    X = lambda W: dqb[:, 4096:4096 + W]
    B1 = lambda W: dqb[:, 5120:5120 + W]
    B2 = lambda W: dqb[:, 6144:6144 + W]
    B3 = lambda W: dqb[:, 7168:7168 + W]
    OUT = lambda W: dqb[:, 6144:6144 + W]
    ASM = lambda s, W: dqb[:, 8192:8192 + W]

    slabs = []
    for o in range(88):
        for q in range(2):
            slabs.append(("gu", o, q, 1024, 128, 8))
    for o in range(16):
        for q in range(11):
            slabs.append(("dn", o, q, 512, 64, 4))
    N = len(slabs)

    ld0, act0, dve0, pe0, st0 = (p.ld.val, p.act.val, p.dve.val, p.pe.val,
                                 p.st.val)
    pstep_i = dqi.ap().ap[0][0]
    pstep_b = dqb.ap().ap[0][0]

    def d_sync(e):
        p.fwait(e, "pe", "dve", "act")
        Plan.aw(e, p.ld, ld0)
        for s, (kind, o, q, W, nw, ng) in enumerate(slabs):
            Plan.aw(e, p.dve, dve0 + 4 * (s - 2) + 1)
            if kind == "gu":
                psrc = dr["gu_packed"].ap()[128 * o:128 * (o + 1),
                                            128 * q:128 * q + nw]
                ssrc = dr["gu_scales_t"].ap()[128 * o:128 * (o + 1), :]
                nsc = 16
            else:
                psrc = dr["dn_packed"].ap()[128 * o:128 * (o + 1),
                                            64 * q:64 * q + nw]
                ssrc = dr["dn_scales_t"].ap()[128 * o:128 * (o + 1), :]
                nsc = 44
            p.ld.inc(e.dma_start(dqi[:, 4608 + 128 * (s % 2):4608 + 128 * (s % 2) + nw], psrc), 16)
            Plan.aw(e, p.act, act0 + 3 * (s - 1) + 1)
            p.ld.inc(e.dma_start(sfb[:, 11864:11864 + nsc], ssrc), 16)
    p.blk.sync(d_sync)
    assert p.ld.val == ld0 + 32 * N, ("cnt", "ld", p.ld.val, ld0 + 32 * N)

    def d_dve(e):
        for s, (kind, o, q, W, nw, ng) in enumerate(slabs):
            Plan.dwait(e, p.ld, ld0, 2 * s + 3, 2 * N)
            Plan.aw(e, p.act, act0 + 3 * (s - 1) + 1)
            for k in range(8):
                last = e.tensor_scalar(dqi[:, 4864 + k * nw:4864 + (k + 1) * nw],
                                       dqi[:, 4608 + 128 * (s % 2):
                                            4608 + 128 * (s % 2) + nw],
                                       4 * k, 15, ALU.logical_shift_right,
                                       ALU.bitwise_and)
            p.dve.inc(last)
            Plan.aw(e, p.act, act0 + 3 * s + 1)
            e.tensor_scalar(B1(W), X(W), 8.0, None, ALU.is_ge)
            e.scalar_tensor_tensor(X(W), B1(W), -8.0, X(W), ALU.mult, ALU.add)
            e.tensor_scalar_mul(B1(W), B1(W), -1.0)
            p.dve.inc(e.tensor_scalar_add(B1(W), B1(W), 0.5))
            Plan.aw(e, p.act, act0 + 3 * s + 2)
            e.tensor_add(X(W), X(W), B2(W))
            e.tensor_add(X(W), X(W), B3(W))
            p.dve.inc(e.tensor_mul(X(W), X(W), B1(W)))
            Plan.aw(e, p.pe, pe0 + s)
            for g in range(ng):
                sidx = (8 * q + g) if kind == "gu" else (4 * q + g)
                last = e.tensor_scalar(OUT(W)[:, 128 * g:128 * (g + 1)],
                                       X(W)[:, 128 * g:128 * (g + 1)],
                                       SF32[:, sidx:sidx + 1], None, ALU.mult)
            p.dve.inc(last)
    p.blk.vector(d_dve)
    assert p.dve.val == dve0 + 4 * N, ("cnt", "dve", p.dve.val, dve0 + 4 * N)

    nibsrc = lambda nw: bass.AP(dqi.ap().tensor, dqi.ap().offset + 4864,
                                [[pstep_i, 128], [nw, 8], [1, nw]])
    xdst = lambda nw: bass.AP(dqb.ap().tensor, dqb.ap().offset + 4096,
                              [[pstep_b, 128], [1, 8], [8, nw]])

    def d_act(e):
        for s, (kind, o, q, W, nw, ng) in enumerate(slabs):
            nsc = 16 if kind == "gu" else 44
            Plan.dwait(e, p.ld, ld0, 2 * s + 3, 2 * N)
            Plan.aw(e, p.dve, dve0 + 4 * (s - 1) + 4)
            e.copy(SF32[:, 0:nsc], sfb[:, 11864:11864 + nsc])
            Plan.aw(e, p.dve, dve0 + 4 * s + 1)
            p.act.inc(e.copy(xdst(nw), nibsrc(nw)))
            Plan.aw(e, p.dve, dve0 + 4 * s + 2)
            Plan.aw(e, p.pe, pe0 + s)
            e.activation(B2(W), X(W), AF.Relu, bias=-4.0)
            p.act.inc(e.activation(B3(W), X(W), AF.Relu, scale=2.0,
                                   bias=-12.0))
            Plan.aw(e, p.pe, pe0 + s + 1)
            Plan.aw(e, p.st, st0 + 16 * s)
            p.act.inc(e.copy(ASM(s, W), p.pbt[:, 0:W]))
            if kind == "gu":
                dstp = dr["wgu_scr"].ap()[128 * o:128 * (o + 1),
                                          1024 * q:1024 * q + W]
            else:
                dstp = dr["wdn_scr"].ap()[128 * o:128 * (o + 1),
                                          512 * q:512 * q + W]
            p.st.inc(e.dma_start(dstp, ASM(s, W)), 16)
    p.blk.scalar(d_act)
    assert p.act.val == act0 + 3 * N, ("cnt", "act", p.act.val, act0 + 3 * N)
    assert p.st.val == st0 + 16 * N, ("cnt", "st", p.st.val, st0 + 16 * N)

    def d_pe(e):
        p.fwait(e, "dve", "act")
        for s, (kind, o, q, W, nw, ng) in enumerate(slabs):
            Plan.aw(e, p.dve, dve0 + 4 * s + 4)
            Plan.aw(e, p.act, act0 + 3 * (s - 1) + 3)
            for c in range(ng):
                mm = e.transpose(p.pbt[:, 128 * c:128 * (c + 1)],
                                 OUT(W)[:, 128 * c:128 * (c + 1)],
                                 p.cbuf[:, CB_IDN:CB_IDN + 128])
            p.pe.inc(mm)
    p.blk.tensor(d_pe)
    assert p.pe.val == pe0 + N, ("cnt", "pe", p.pe.val, pe0 + N)
    p.fence()



def emit_ffn_l1(p, dr, src, dst, dequant):
    arb1, arf, wbuf = p.arb1, p.arf, p.wbuf
    if dequant:
        emit_dequant(p, dr)

    emit_ln(p, src, arb1, 0)
    xn = lambda c, b: arb1[:, 2048 * c + 512 * b:2048 * c + 512 * (b + 1)]

    NP = 44
    ld0, pe0, act0, dve0, st0 = (p.ld.val, p.pe.val, p.act.val, p.dve.val,
                                 p.st.val)

    def da_sync(e):
        p.fwait(e, "pe", "dve")
        Plan.aw(e, p.ld, ld0)
        Plan.aw(e, p.st, st0)
        for pp in range(NP):
            Plan.aw(e, p.pe, pe0 + 4 * (pp - 1))
            w0 = 4096 * (pp % 2)
            p.ld.inc(e.dma_start(
                wbuf[:, w0:w0 + 2048],
                dr["wgu_scr"].ap()[128 * pp:128 * (pp + 1), :]), 16)
            p.ld.inc(e.dma_start(
                wbuf[:, w0 + 2048:w0 + 4096],
                dr["wgu_scr"].ap()[128 * (pp + NP):128 * (pp + NP + 1), :]), 16)
    p.blk.sync(da_sync)
    assert p.ld.val == ld0 + 32 * NP, ("cnt", "ld", p.ld.val, ld0 + 32 * NP)

    def da_pe(e):
        p.fwait(e, "dve", "act")
        for pp in range(NP):
            Plan.dwait(e, p.ld, ld0, 2 * pp + 4, 2 * NP)
            w0 = 4096 * (pp % 2)
            for b in range(NB):
                i = pp * NB + b
                Plan.aw(e, p.dve, dve0 + i - 1)
                pg = p.pb[2 * (i % 2)]
                pu = p.pb[2 * (i % 2) + 1]
                for c in range(DT):
                    e.matmul(pg[:, :], wbuf[:, w0 + 128 * c:w0 + 128 * (c + 1)],
                             xn(c, b), start=(c == 0), stop=(c == DT - 1))
                for c in range(DT):
                    mm = e.matmul(pu[:, :],
                                  wbuf[:, w0 + 2048 + 128 * c:w0 + 2048 + 128 * (c + 1)],
                                  xn(c, b), start=(c == 0), stop=(c == DT - 1))
                p.pe.inc(mm)
    p.blk.tensor(da_pe)
    assert p.pe.val == pe0 + 4 * NP, ("cnt", "pe", p.pe.val, pe0 + 4 * NP)

    def da_act(e):
        for i in range(4 * NP):
            Plan.aw(e, p.pe, pe0 + i + 1)
            Plan.aw(e, p.dve, dve0 + i - 1)
            sg = p.smf[:, 512 * (i % 2):512 * (i % 2 + 1)]
            p.act.inc(e.activation(sg, p.pb[2 * (i % 2)][:, :], AF.Silu))
            if i > 0:
                pp1, b1 = divmod(i - 1, NB)
                Plan.aw(e, p.dve, dve0 + i)
                p.st.inc(e.dma_start(
                    dr["hbuf"].ap()[128 * pp1:128 * (pp1 + 1),
                                    512 * b1:512 * (b1 + 1)],
                    p.smb[:, 512 * ((i - 1) % 4):512 * ((i - 1) % 4 + 1)]), 16)
        pp1, b1 = divmod(4 * NP - 1, NB)
        Plan.aw(e, p.dve, dve0 + 4 * NP)
        p.st.inc(e.dma_start(
            dr["hbuf"].ap()[128 * pp1:128 * (pp1 + 1), 512 * b1:512 * (b1 + 1)],
            p.smb[:, 512 * ((4 * NP - 1) % 4):512 * ((4 * NP - 1) % 4 + 1)]), 16)
    p.blk.scalar(da_act)
    assert p.act.val == act0 + 4 * NP, ("cnt", "act", p.act.val, act0 + 4 * NP)
    assert p.st.val == st0 + 16 * 4 * NP, ("cnt", "st", p.st.val, st0 + 16 * 4 * NP)

    def da_dve(e):
        for i in range(4 * NP):
            Plan.aw(e, p.act, act0 + i + 1)
            Plan.aw(e, p.st, st0 + 16 * (i - 3))
            sg = p.smf[:, 512 * (i % 2):512 * (i % 2 + 1)]
            hh = p.smb[:, 512 * (i % 4):512 * (i % 4 + 1)]
            p.dve.inc(e.tensor_mul(hh, p.pb[2 * (i % 2) + 1][:, :], sg))
    p.blk.vector(da_dve)
    assert p.dve.val == dve0 + 4 * NP, ("cnt", "dve", p.dve.val, dve0 + 4 * NP)
    p.fence()

    NI = 44
    ld0, pe0, act0, dve0, st0 = (p.ld.val, p.pe.val, p.act.val, p.dve.val,
                                 p.st.val)
    r_rms = p.rows[0:1, 0:512]
    rrep = p.smf[:, 0:512]

    PE_B, DVE_B, ACT_B = 33, 34, 17

    def db_sync(e):
        p.fwait(e, "pe", "dve")
        Plan.aw(e, p.ld, ld0)
        Plan.aw(e, p.st, st0)
        for b in range(NB):
            Plan.aw(e, p.pe, pe0 + PE_B * b - 1 if b > 0 else 0)
            for pp in range(NI):
                p.ld.inc(e.dma_start(
                    arb1[:, 512 * pp:512 * (pp + 1)],
                    dr["hbuf"].ap()[128 * pp:128 * (pp + 1),
                                    512 * b:512 * (b + 1)]), 16)
            for ot in range(DT):
                if ot >= 3:
                    Plan.aw(e, p.dve, dve0 + DVE_B * b + 18 + ot - 2)
                elif b > 0:
                    Plan.aw(e, p.dve, dve0 + DVE_B * b)
                p.ld.inc(e.dma_start(
                    p.xbuf[:, 512 * (ot % 3):512 * (ot % 3 + 1)],
                    src.ap()[128 * ot:128 * (ot + 1), BLK * b:BLK * (b + 1)]), 16)
    p.blk.sync(db_sync)
    assert p.ld.val == ld0 + 16 * 60 * NB, ("cnt", "ld", p.ld.val, ld0 + 16 * 60 * NB)

    cast0 = p.cast.val

    def db_cast(e):
        Plan.aw(e, p.cast, cast0)
        Plan.aw(e, p.pe, pe0)
        for b in range(NB):
            for ot in range(DT):
                i = b * DT + ot
                if i >= 2:
                    b2, ot2 = divmod(i - 2, DT)
                    Plan.aw(e, p.pe, pe0 + PE_B * b2 +
                            (1 if ot2 == 0 else 2 * ot2))
                p.cast.inc(e.dma_start(
                    wbuf[:, 5632 * (i % 2):5632 * (i % 2) + 5632],
                    dr["wdn_scr"].ap()[128 * ot:128 * (ot + 1), :]), 16)
    p.blk.gpsimd(db_cast)
    assert p.cast.val == cast0 + 16 * 4 * DT, ("cnt", "cast", p.cast.val, cast0 + 16 * 4 * DT)

    def db_pe(e):
        p.fwait(e, "dve", "act")
        for b in range(NB):
            Plan.dwait(e, p.ld, ld0, 60 * b + 44, 60 * NB)
            for ot in range(DT):
                i = b * DT + ot
                Plan.aw(e, p.cast, cast0 + 16 * min(i + 2, 4 * DT))
                Plan.aw(e, p.dve, dve0 + DVE_B * b + ot - 1)
                pbk = p.pb[i % 2]
                for c in range(NI):
                    mm = e.matmul(pbk[:, :],
                                  wbuf[:, 5632 * (i % 2) + 128 * c:
                                       5632 * (i % 2) + 128 * (c + 1)],
                                  arb1[:, 512 * c:512 * (c + 1)],
                                  start=(c == 0), stop=(c == NI - 1))
                p.pe.inc(mm)
                if ot > 0:
                    Plan.aw(e, p.act, act0 + ACT_B * b + ot)
                    p.pe.inc(e.matmul(p.pb[6][0:1, :],
                                      p.cbuf[:, CB_ONES_COL:CB_ONES_COL + 1],
                                      p.smb[:, 512 * ((ot - 1) % 2):
                                            512 * ((ot - 1) % 2 + 1)],
                                      start=(ot - 1 == 0), stop=False))
            Plan.aw(e, p.act, act0 + ACT_B * b + 16)
            p.pe.inc(e.matmul(p.pb[6][0:1, :],
                              p.cbuf[:, CB_ONES_COL:CB_ONES_COL + 1],
                              p.smb[:, 512 * (15 % 2):512 * (15 % 2 + 1)],
                              start=False, stop=True))
            Plan.aw(e, p.act, act0 + ACT_B * b + 17)
            p.pe.inc(e.matmul(p.pb[5][:, :],
                              p.cbuf_f[0:1, CF_ONES_ROW:CF_ONES_ROW + 128],
                              r_rms, start=True, stop=True))
    p.blk.tensor(db_pe)
    assert p.pe.val == pe0 + PE_B * NB, ("cnt", "pe", p.pe.val, pe0 + PE_B * NB)

    def db_dve(e):
        for b in range(NB):
            for ot in range(DT):
                i = b * DT + ot
                Plan.aw(e, p.pe, pe0 + PE_B * b + 2 * ot + 1 - (1 if ot > 0 else 0))
                p.dve.inc(e.tensor_copy(arf[:, 512 * ot:512 * (ot + 1)],
                                        p.pb[i % 2][:, :]))
            Plan.aw(e, p.pe, pe0 + PE_B * b + 32)
            e.tensor_scalar_mul(r_rms, p.pb[6][0:1, :], 1.0 / D)
            p.dve.inc(e.tensor_scalar_add(r_rms, r_rms, float(RMS_EPS_L1FFN)))
            Plan.aw(e, p.pe, pe0 + PE_B * b + 33)
            p.dve.inc(e.tensor_copy(rrep, p.pb[5][:, :]))
            for ot in range(DT):
                Plan.dwait(e, p.ld, ld0, min(60 * b + 44 + ot + 3, 60 * (b + 1)), 60 * NB)
                Plan.aw(e, p.st, st0 + 16 * (b * DT + ot - 1))
                t2 = p.smf[:, 512:1024]
                out = p.smf2[:, 512 * (ot % 2):512 * (ot % 2 + 1)]
                e.tensor_mul(t2, arf[:, 512 * ot:512 * (ot + 1)], rrep)
                xsrc = p.xbuf[:, 512 * (ot % 3):512 * (ot % 3 + 1)]
                p.dve.inc(e.scalar_tensor_tensor(out, t2, 0.5, xsrc,
                                                 ALU.mult, ALU.add))
    p.blk.vector(db_dve)
    assert p.dve.val == dve0 + DVE_B * NB, ("cnt", "dve", p.dve.val, dve0 + DVE_B * NB)

    def db_act(e):
        for b in range(NB):
            for ot in range(DT):
                Plan.aw(e, p.dve, dve0 + DVE_B * b + ot + 1)
                ysq = p.smb[:, 512 * (ot % 2):512 * (ot % 2 + 1)]
                if ot >= 2:
                    Plan.aw(e, p.pe, pe0 + PE_B * b + 2 * ot - 1)
                p.act.inc(e.activation(ysq, arf[:, 512 * ot:512 * (ot + 1)],
                                       AF.Square))
            Plan.aw(e, p.dve, dve0 + DVE_B * b + 17)
            e.activation(r_rms, r_rms, AF.Ln)
            p.act.inc(e.activation(r_rms, r_rms, AF.Exp, scale=-0.5))
            for ot in range(DT):
                Plan.aw(e, p.dve, dve0 + DVE_B * b + 18 + ot + 1)
                out = p.smf2[:, 512 * (ot % 2):512 * (ot % 2 + 1)]
                p.st.inc(e.dma_start(
                    dst.ap()[128 * ot:128 * (ot + 1), BLK * b:BLK * (b + 1)],
                    out), 16)
    p.blk.scalar(db_act)
    assert p.act.val == act0 + ACT_B * NB, ("cnt", "act", p.act.val, act0 + ACT_B * NB)
    assert p.st.val == st0 + 16 * DT * NB, ("cnt", "st", p.st.val, st0 + 16 * DT * NB)
    p.fence()


ALL_SUBLAYERS = ["ffn0a", "mamba", "ffn0b", "ffn1a", "attn", "ffn1b"]


def _run_chained(x, weights, groups):
    cur = [np.ascontiguousarray(x[c].T) for c in range(N_CORES)]
    for gi, g in enumerate(groups):
        nc = build_program(g)
        in_maps = []
        for c in range(N_CORES):
            m = {"x_in": cur[c]}
            m.update(weights)
            in_maps.append(m)
        res = run_bass_kernel_spmd(nc, in_maps, core_ids=list(range(N_CORES)))
        cur = [np.asarray(res.results[c]["out"]) for c in range(N_CORES)]
    return np.stack([c.T for c in cur])


def kernel(**inputs):
    x = np.asarray(inputs["L0_attn_x"], np.float32)
    weights = prep_weights(inputs)
    out = _run_chained(x, weights, [ALL_SUBLAYERS])
    return out.astype(np.float32)


# version 02
# speedup vs baseline: 1.1399x; 1.1399x over previous
import contextlib
import numpy as np
import ml_dtypes as md

import concourse.bass as bass
import concourse.mybir as mybir
from concourse.bass_utils import run_bass_kernel_spmd

F32 = mybir.dt.float32
BF16 = mybir.dt.bfloat16
I32 = mybir.dt.int32
F16 = mybir.dt.float16
AF = mybir.ActivationFunctionType
ALU = mybir.AluOpType
AX = mybir.AxisListType

D = 2048
T = 2048
BLK = 512
NB = T // BLK
DT = D // 128
N_CORES = 8

D_INNER = 4096
D_STATE = 16
D_HEAD = 256
PROJ_PAD = 8832
HID = 1024
LAT = 512
NE = 16
MOE_SCALE = 2.5
I_FF = 5632
GU = 2 * I_FF
NHEADS = 16
HDIM = 128

LN_EPS = 1e-5
RMS_EPS_MAMBA = 1.1920929e-07
RMS_EPS_L1FFN = 1e-6

CONSTV_VALUES = [-4.0, -12.0, 0.5, -6.0]



def _mmtile(w):
    out_f, in_f = w.shape
    assert out_f % 128 == 0 and in_f % 128 == 0, (out_f, in_f)
    ot, kc = out_f // 128, in_f // 128
    h = w.reshape(ot, 128, kc, 128).transpose(0, 3, 2, 1)
    return np.ascontiguousarray(h.reshape(ot * 128, in_f))


def _sbimg(w, part=128):
    r, c = w.shape
    k = r // part
    return np.ascontiguousarray(
        w.reshape(k, part, c).transpose(1, 0, 2).reshape(part, k * c))


def mkap(ap, newdims):
    return bass.AP(ap.tensor, ap.offset, newdims)


def _host_consts():
    f32 = np.float32
    c = {}
    c["idn_bf"] = np.eye(128, dtype=f32).astype(md.bfloat16)
    c["idn_f32"] = np.eye(128, dtype=f32)
    c["ones_col_bf"] = np.ones((128, 1), f32).astype(md.bfloat16)
    c["ones_row_bf"] = np.ones((1, 128), f32).astype(md.bfloat16)
    c["ones_row_f32"] = np.ones((1, 128), f32)
    pat = np.zeros((32, 128, 16), f32)
    for i in range(32):
        pat[i, :, i >> 1] = 1.0 / D_HEAD
    c["xmean_pat_bf"] = np.ascontiguousarray(
        pat.transpose(1, 0, 2).reshape(128, 512)).astype(md.bfloat16)
    rep = np.zeros((16, 256), f32)
    for h in range(16):
        rep[h, 16 * h:16 * h + 16] = 1.0
    c["hrep_pat"] = rep
    col = np.zeros((256, 16), f32)
    for h in range(16):
        col[16 * h:16 * h + 16, h] = 1.0
    c["hcol_pat"] = _sbimg(col)
    sel = np.zeros((16, 16, 128), f32)
    for e in range(16):
        sel[e, e, :] = 1.0
    c["sel16_bf"] = np.ascontiguousarray(
        sel.transpose(1, 0, 2).reshape(16, 2048)).astype(md.bfloat16)
    inv = 1.0 / (10000.0 ** (np.arange(0, D, 2, dtype=np.float64) / D))
    tpos = np.arange(T, dtype=np.float64)
    emb = np.concatenate([np.outer(inv, tpos)] * 2, axis=0)
    c["rope_cos"] = np.cos(emb).astype(f32)
    c["rope_sin"] = np.sin(emb).astype(f32)
    return c


def prep_weights(i):
    w = {}
    f32 = np.float32
    W_in = np.asarray(i["L0_attn_W_in"], f32)
    pad = np.zeros((112, D), f32)
    W_in_pad = np.concatenate([W_in[0:4096], W_in[4096:8192], W_in[8192:8208],
                               pad, W_in[8208:8464], W_in[8464:8720]], 0)
    w["w_inT"] = _mmtile(W_in_pad)
    w["w_outT"] = _mmtile(np.asarray(i["L0_attn_W_out"], f32))
    w["w_h2iT"] = np.ascontiguousarray(np.asarray(i["L0_attn_W_h2i"], f32).T)
    w["w_norm_cols"] = np.ascontiguousarray(
        np.asarray(i["L0_attn_w_norm"], f32).reshape(DT, 128).T)
    w["w_conv_cols"] = np.ascontiguousarray(
        np.asarray(i["L0_attn_w_conv"], f32)[:, 0, :].reshape(32, 128, 4)
        .transpose(1, 0, 2).reshape(128, 128))
    w["dt_bias"] = np.asarray(i["L0_attn_dt_bias"], f32).reshape(16, 1)
    w["a_log"] = np.asarray(i["L0_attn_A_log"], f32).reshape(16, 1)

    sw2t = _mmtile(np.asarray(i["L0_ffn_sw2"], f32))
    wupt = _mmtile(np.asarray(i["L0_ffn_Wup"], f32))
    w["comb_img"] = np.ascontiguousarray(np.concatenate([sw2t, wupt], 1))
    w["sw1T"] = _mmtile(np.asarray(i["L0_ffn_sw1"], f32))
    w["swgT"] = _mmtile(np.asarray(i["L0_ffn_swg"], f32))
    w["wdownT"] = _mmtile(np.asarray(i["L0_ffn_Wdown"], f32))
    w["wg_sb"] = _sbimg(np.ascontiguousarray(np.asarray(i["L0_ffn_Wg"], f32).T))
    rw1 = np.asarray(i["L0_ffn_rw1"], f32)
    rwg = np.asarray(i["L0_ffn_rwg"], f32)
    rw2 = np.asarray(i["L0_ffn_rw2"], f32)
    imgs = []
    for e in range(NE):
        imgs.append(np.concatenate([_sbimg(_mmtile(rw1[e])),
                                    _sbimg(_mmtile(rwg[e])),
                                    _sbimg(_mmtile(rw2[e]))], 1))
    w["rw_img"] = np.ascontiguousarray(np.stack(imgs).reshape(NE * 128, 12288))
    w["moe_bias"] = np.asarray(i["L0_ffn_biases"], f32).reshape(1, 16)

    for nm, key in (("wqT", "L1_attn_Wq"), ("wkT", "L1_attn_Wk"),
                    ("wvT", "L1_attn_Wv"), ("woT", "L1_attn_Wo")):
        w[nm] = _mmtile(np.asarray(i[key], f32))
    w["attn_gate"] = np.asarray(i["L1_attn_gate"], f32).reshape(1, 16)

    w["gu_packed"] = np.ascontiguousarray(np.asarray(i["L1_ffn_gate_up_packed"]))
    w["gu_scales_t"] = np.ascontiguousarray(
        np.asarray(i["L1_ffn_gate_up_scales"]).T)
    w["dn_packed"] = np.ascontiguousarray(np.asarray(i["L1_ffn_down_packed"]))
    w["dn_scales_t"] = np.ascontiguousarray(
        np.asarray(i["L1_ffn_down_scales"]).T)
    w.update(_host_consts())
    cv = np.zeros((128, 16), np.float32)
    cv[:, :len(CONSTV_VALUES)] = np.asarray(CONSTV_VALUES, np.float32)[None, :]
    w["constv"] = cv
    return w


WEIGHT_SPECS = {
    "w_inT": ((PROJ_PAD, D), F32), "w_outT": ((D, D_INNER), F32),
    "w_h2iT": ((16, D_INNER), F32), "w_norm_cols": ((128, 16), F32),
    "w_conv_cols": ((128, 128), F32), "dt_bias": ((16, 1), F32),
    "a_log": ((16, 1), F32),
    "comb_img": ((D, 1536), F32), "sw1T": ((HID, D), F32),
    "swgT": ((HID, D), F32), "wdownT": ((LAT, D), F32),
    "wg_sb": ((128, 256), F32), "rw_img": ((NE * 128, 12288), F32),
    "moe_bias": ((1, 16), F32),
    "wqT": ((D, D), F32), "wkT": ((D, D), F32), "wvT": ((D, D), F32),
    "woT": ((D, D), F32), "attn_gate": ((1, 16), F32),
    "gu_packed": ((GU, 256), I32), "gu_scales_t": ((GU, 16), F16),
    "dn_packed": ((D, 704), I32), "dn_scales_t": ((D, 44), F16),
    "idn_bf": ((128, 128), BF16), "idn_f32": ((128, 128), F32),
    "ones_col_bf": ((128, 1), BF16), "ones_row_bf": ((1, 128), BF16),
    "ones_row_f32": ((1, 128), F32), "xmean_pat_bf": ((128, 512), BF16),
    "hrep_pat": ((16, 256), F32), "hcol_pat": ((128, 32), F32),
    "sel16_bf": ((16, 2048), BF16),
    "rope_cos": ((D, T), F32), "rope_sin": ((D, T), F32),
    "constv": ((128, 16), F32),
}

SCRATCH_SPECS = {
    "resid": ((D, T), F32),
    "zbuf": ((D_INNER, T), F32),
    "r1buf": ((D, T), F32),
    "vbuf": ((D, T), F32),
    "dtbc": ((5 * 128, T), F32),
    "wgu_scr": ((GU, D), BF16),
    "wdn_scr": ((D, I_FF), BF16),
    "hbuf": ((I_FF, T), BF16),
    "qspill": ((D, T), BF16),
    "kspill": ((D, T), BF16),
    "qfspill": ((D, T), BF16),
    "kfspill": ((D, T), BF16),
    "vtok": ((T, D), BF16),
    "combspill": ((D, T), BF16),
}



class Sem:
    __slots__ = ("h", "val", "name")

    def __init__(self, handle, name):
        self.h = handle
        self.val = 0
        self.name = name

    def inc(self, inst, n=1):
        inst.then_inc(self.h, n)
        self.val += n
        return self.val


ENG_SEM = {"sync": "ld", "gpsimd": "cast", "tensor": "pe", "scalar": "act",
           "vector": "dve"}


class Plan:
    def __init__(self, nc, ctx):
        self.nc = nc
        self.ctx = ctx
        self.blk = ctx.enter_context(nc.Block())
        for nm in ["ld", "cast", "st", "pe", "dve", "act", "gp"]:
            setattr(self, nm, Sem(ctx.enter_context(nc.semaphore(nm)), nm))
        self.pb = [ctx.enter_context(nc.psum_tensor(f"pb{k}", [128, BLK], F32))
                   for k in range(7)]
        self.pbt = ctx.enter_context(nc.psum_tensor("pbt", [128, 1024], BF16))
        self.arb1 = ctx.enter_context(nc.sbuf_tensor("arb1", [128, 32768], BF16))
        self.arb2 = ctx.enter_context(nc.sbuf_tensor("arb2", [128, 16384], BF16))
        self.arf = ctx.enter_context(nc.sbuf_tensor("arf", [128, 12288], F32))
        self.wbuf = ctx.enter_context(nc.sbuf_tensor("wbuf", [128, 12288], BF16))
        self.xbuf = ctx.enter_context(nc.sbuf_tensor("xbuf", [128, 1536], F32))
        self.smf = ctx.enter_context(nc.sbuf_tensor("smf", [128, 1024], F32))
        self.smf2 = ctx.enter_context(nc.sbuf_tensor("smf2", [128, 1024], F32))
        self.smb = ctx.enter_context(nc.sbuf_tensor("smb", [128, 2048], BF16))
        self.cbuf = ctx.enter_context(nc.sbuf_tensor("cbuf", [128, 784], BF16))
        self.cbuf_f = ctx.enter_context(nc.sbuf_tensor("cbuf_f", [128, 896], F32))
        self.cbuf2 = ctx.enter_context(nc.sbuf_tensor("cbuf2", [16, 2048], BF16))
        self.rows = self.arf[0:1, 10496:12288]
        self.fence_v = {}

    def fence(self):
        for nm in ["ld", "cast", "st", "pe", "dve", "act", "gp"]:
            self.fence_v[nm] = getattr(self, nm).val

    def fwait(self, e, *names):
        for nm in names:
            v = self.fence_v.get(nm, 0)
            if v > 0:
                e.wait_ge(getattr(self, nm).h, v)

    @staticmethod
    def aw(e, sem, val):
        if val > 0:
            e.wait_ge(sem.h, val)

    @staticmethod
    def dwait(e, sem, base, k, total):
        Plan.aw(e, sem, base + 16 * min(k, total))

    def chain(self, steps, first_wait=("pe", "dve", "act")):
        prev = None
        for k, (engname, fn) in enumerate(steps):
            sem = getattr(self, ENG_SEM[engname])

            def body(e, fn=fn, prev=prev, k=k, sem=sem):
                if k == 0:
                    self.fwait(e, *first_wait)
                if prev is not None:
                    Plan.aw(e, prev[0], prev[1])
                last = fn(e)
                if isinstance(last, (list, tuple)):
                    last = [x for x in last if x is not None][-1]
                sem.inc(last)
            getattr(self.blk, engname)(body)
            prev = (sem, sem.val)
        self.fence()


def selfstep(p, e, sem, inst):
    sem.inc(inst)
    e.wait_ge(sem.h, sem.val)
    return inst


def emit_dump(p, dst, region, rows=128, tiles=DT, tile_cols=2048):
    gp0 = p.gp.val

    def body(e):
        p.fwait(e, "pe", "dve", "act")
        Plan.aw(e, p.st, p.st.val)
        for t in range(tiles):
            p.gp.inc(e.dma_start(
                dst.ap()[128 * t:128 * t + rows, 0:tile_cols], region(t)), 16)
        e.wait_ge(p.gp.h, gp0 + 16 * tiles)
    p.blk.gpsimd(body)
    p.fence()


def make_plan(nc, ctx):
    p = Plan(nc, ctx)
    for j, v in enumerate(CONSTV_VALUES):
        nc.const_aps.register(F32, v, p.cbuf_f[:, CF_CONSTV + j:CF_CONSTV + j + 1])
    return p


CB_IDN = 0
CB_ONES_COL = 128
CB_XMEAN = 136
CB_ONES_ROW = 648
CF_IDN = 0
CF_ONES_ROW = 128
CF_HREP = 256
CF_HCOL = 512
CF_WNORM = 544
CF_WCONV = 560
CF_DTBIAS = 688
CF_ALOG = 689
CF_MOEB = 690
CF_GATE = 706
CF_GREP = 722
CF_MOEB_REP = 738
CF_NALOGE = 754
CF_CONSTV = 770


def emit_load_consts(p, dr):
    cb, cf = p.cbuf, p.cbuf_f

    def body(e):
        pairs = [
            (cb[:, CB_IDN:CB_IDN + 128], dr["idn_bf"].ap()),
            (cb[:, CB_ONES_COL:CB_ONES_COL + 1], dr["ones_col_bf"].ap()),
            (cb[0:1, CB_ONES_ROW:CB_ONES_ROW + 128], dr["ones_row_bf"].ap()),
            (cb[:, CB_XMEAN:CB_XMEAN + 512], dr["xmean_pat_bf"].ap()),
            (cf[:, CF_IDN:CF_IDN + 128], dr["idn_f32"].ap()),
            (cf[0:1, CF_ONES_ROW:CF_ONES_ROW + 128], dr["ones_row_f32"].ap()),
            (cf[0:16, CF_HREP:CF_HREP + 256], dr["hrep_pat"].ap()),
            (cf[:, CF_HCOL:CF_HCOL + 32], dr["hcol_pat"].ap()),
            (cf[:, CF_WNORM:CF_WNORM + 16], dr["w_norm_cols"].ap()),
            (cf[:, CF_WCONV:CF_WCONV + 128], dr["w_conv_cols"].ap()),
            (cf[0:16, CF_DTBIAS:CF_DTBIAS + 1], dr["dt_bias"].ap()),
            (cf[0:16, CF_ALOG:CF_ALOG + 1], dr["a_log"].ap()),
            (cf[0:1, CF_MOEB:CF_MOEB + 16], dr["moe_bias"].ap()),
            (cf[0:1, CF_GATE:CF_GATE + 16], dr["attn_gate"].ap()),
            (p.cbuf2[:, :], dr["sel16_bf"].ap()),
            (cf[:, CF_CONSTV:CF_CONSTV + 16], dr["constv"].ap()),
        ]
        for dst, src in pairs:
            p.ld.inc(e.dma_start(dst, src), 16)
    p.blk.sync(body)

    p.fence()


def emit_prep_small(p):
    cf = p.cbuf_f
    p.chain([
        ("scalar", lambda e: [
            p.fwait(e, "ld", "cast"),
            e.activation(cf[0:1, CF_GATE:CF_GATE + 16],
                         cf[0:1, CF_GATE:CF_GATE + 16], AF.Sigmoid),
            e.activation(cf[0:16, CF_NALOGE:CF_NALOGE + 1],
                         cf[0:16, CF_ALOG:CF_ALOG + 1], AF.Exp)][-1]),
        ("tensor", lambda e: [
            e.matmul(p.pb[6][:, 0:16], cf[0:1, CF_ONES_ROW:CF_ONES_ROW + 128],
                     cf[0:1, CF_GATE:CF_GATE + 16], start=True, stop=True),
            e.matmul(p.pb[6][:, 16:32], cf[0:1, CF_ONES_ROW:CF_ONES_ROW + 128],
                     cf[0:1, CF_MOEB:CF_MOEB + 16], start=True, stop=True)][-1]),
        ("vector", lambda e: [
            e.tensor_copy(cf[:, CF_GREP:CF_GREP + 16], p.pb[6][:, 0:16]),
            e.tensor_copy(cf[:, CF_MOEB_REP:CF_MOEB_REP + 16], p.pb[6][:, 16:32]),
            e.tensor_scalar_mul(cf[0:16, CF_NALOGE:CF_NALOGE + 1],
                                cf[0:16, CF_NALOGE:CF_NALOGE + 1], -1.0)][-1]),
    ], first_wait=("ld", "cast"))



def emit_ln(p, x_src, arena, col0, router=None):
    xb, srh = p.xbuf, p.smb
    mrep = p.smf[:, 0:512]
    rrep = p.smf[:, 512:1024]
    r_mean = p.rows[0:1, 0:512]
    r_rstd = p.rows[0:1, 512:1024]
    r_sq = p.rows[0:1, 1024:1536]
    r_tmp = p.smf2[0:1, 0:512]

    for b in range(NB):
        ld0, act0, pe0, st0 = p.ld.val, p.act.val, p.pe.val, p.st.val

        def sync1(e, ld0=ld0, act0=act0, b=b, st0=st0):
            p.fwait(e, "dve", "act")
            Plan.aw(e, p.ld, ld0)
            Plan.aw(e, p.st, st0)
            for t in range(DT):
                Plan.aw(e, p.act, act0 + t - 2)
                p.ld.inc(e.dma_start(
                    xb[:, 512 * (t % 3):512 * (t % 3 + 1)],
                    x_src.ap()[128 * t:128 * (t + 1), BLK * b:BLK * (b + 1)]), 16)
        p.blk.sync(sync1)
        assert p.ld.val == ld0 + 16 * DT, ("cnt", "ld", p.ld.val, ld0 + 16 * DT)

        def act1(e, ld0=ld0, pe0=pe0, b=b):
            p.fwait(e, "dve", "pe")
            for t in range(DT):
                Plan.dwait(e, p.ld, ld0, t + 3, DT)
                Plan.aw(e, p.pe, pe0 + t - 1)
                src = xb[:, 512 * (t % 3):512 * (t % 3 + 1)]
                dst = srh[:, 1024 * (t % 2):1024 * (t % 2 + 1)]
                e.copy(dst[:, 0:512], src)
                p.act.inc(e.activation(dst[:, 512:1024], src, AF.Square))
        p.blk.scalar(act1)
        assert p.act.val == act0 + DT, ("cnt", "act", p.act.val, act0 + DT)

        def pe1(e, act0=act0, b=b):
            p.fwait(e, "dve")
            for t in range(DT):
                Plan.aw(e, p.act, act0 + t + 1)
                dst = srh[:, 1024 * (t % 2):1024 * (t % 2 + 1)]
                first, last = (t == 0), (t == DT - 1)
                e.matmul(p.pb[4][0:1, :], p.cbuf[:, CB_ONES_COL:CB_ONES_COL + 1],
                         dst[:, 0:512], start=first, stop=last)
                p.pe.inc(e.matmul(p.pb[5][0:1, :],
                                  p.cbuf[:, CB_ONES_COL:CB_ONES_COL + 1],
                                  dst[:, 512:1024], start=first, stop=last))
        p.blk.tensor(pe1)
        assert p.pe.val == pe0 + DT, ("cnt", "pe", p.pe.val, pe0 + DT)
        p.fence()

        p.chain([
            ("vector", lambda e: [
                selfstep(p, e, p.dve,
                         e.tensor_scalar_mul(r_mean, p.pb[4][0:1, :], 1.0 / D)),
                selfstep(p, e, p.dve,
                         e.tensor_scalar_mul(r_sq, p.pb[5][0:1, :], 1.0 / D)),
                selfstep(p, e, p.dve,
                         e.tensor_tensor(r_tmp, r_mean, r_mean, ALU.mult)),
                selfstep(p, e, p.dve, e.tensor_sub(r_rstd, r_sq, r_tmp)),
                e.tensor_scalar_add(r_rstd, r_rstd, float(LN_EPS))][-1]),
            ("scalar", lambda e: [
                selfstep(p, e, p.act, e.activation(r_rstd, r_rstd, AF.Ln)),
                e.activation(r_rstd, r_rstd, AF.Exp, scale=-0.5)][-1]),
            ("tensor", lambda e: [
                e.matmul(p.pb[4][:, :], p.cbuf_f[0:1, CF_ONES_ROW:CF_ONES_ROW + 128],
                         r_mean, start=True, stop=True),
                e.matmul(p.pb[5][:, :], p.cbuf_f[0:1, CF_ONES_ROW:CF_ONES_ROW + 128],
                         r_rstd, start=True, stop=True)][-1]),
            ("vector", lambda e: [
                e.tensor_copy(mrep, p.pb[4][:, :]),
                e.tensor_copy(rrep, p.pb[5][:, :])][-1]),
        ])

        ld2, dve2 = p.ld.val, p.dve.val

        def sync2(e, ld2=ld2, dve2=dve2, b=b, st0=st0):
            Plan.aw(e, p.ld, ld2)
            Plan.aw(e, p.act, p.act.val)
            Plan.aw(e, p.st, st0)
            for t in range(DT):
                Plan.aw(e, p.dve, dve2 + t - 2)
                p.ld.inc(e.dma_start(
                    xb[:, 512 * (t % 3):512 * (t % 3 + 1)],
                    x_src.ap()[128 * t:128 * (t + 1), BLK * b:BLK * (b + 1)]), 16)
        p.blk.sync(sync2)
        assert p.ld.val == ld2 + 16 * DT, ("cnt", "ld", p.ld.val, ld2 + 16 * DT)

        per0 = p.pe.val

        def dve3(e, ld2=ld2, b=b, per0=per0):
            for t in range(DT):
                Plan.dwait(e, p.ld, ld2, t + 3, DT)
                src = xb[:, 512 * (t % 3):512 * (t % 3 + 1)]
                dst = arena[:, col0 + 2048 * t + 512 * b:
                            col0 + 2048 * t + 512 * (b + 1)]
                if router is None:
                    tmp = p.smf2[:, 0:512]
                    e.tensor_sub(tmp, src, mrep)
                    p.dve.inc(e.tensor_mul(dst, tmp, rrep))
                else:
                    xf = p.smf2[:, 512 * (t % 2):512 * (t % 2 + 1)]
                    if t >= 2:
                        Plan.aw(e, p.pe, per0 + t - 1)
                    e.tensor_sub(xf, src, mrep)
                    e.tensor_mul(xf, xf, rrep)
                    p.dve.inc(e.tensor_copy(dst, xf))
        p.blk.vector(dve3)
        assert p.dve.val == dve2 + DT, ("cnt", "dve", p.dve.val, dve2 + DT)

        if router is not None:
            dve3_0 = dve2

            def pe_rt(e, b=b, dve3_0=dve3_0):
                if b > 0:
                    Plan.aw(e, p.act, p.act.val)
                for t in range(DT):
                    Plan.aw(e, p.dve, dve3_0 + t + 1)
                    xf = p.smf2[:, 512 * (t % 2):512 * (t % 2 + 1)]
                    p.pe.inc(e.matmul(p.pb[6][0:16, :],
                                      router["wg"][:, 16 * t:16 * (t + 1)],
                                      xf, start=(t == 0), stop=(t == DT - 1)))
            p.blk.tensor(pe_rt)
            rt_pe_end = p.pe.val

            def act_rt(e, b=b, rt_pe_end=rt_pe_end):
                Plan.aw(e, p.pe, rt_pe_end)
                p.act.inc(e.activation(router["gl"][:, 512 * b:512 * (b + 1)],
                                       p.pb[6][0:16, :], AF.Sigmoid))
            p.blk.scalar(act_rt)
        p.fence()



def emit_ffn_l0(p, dr, src, dst, upto=None):
    cb, cf = p.cbuf, p.cbuf_f
    arb1, arb2, arf, wbuf = p.arb1, p.arb2, p.arf, p.wbuf
    xn = lambda c, b: arb1[:, 2048 * c + 512 * b:2048 * c + 512 * (b + 1)]

    gl = arf[0:16, 8192:10240]
    wgf = arf[:, 10240:10496]
    wn_tok = p.smf2[:, 0:256]
    wT16 = p.smb[0:16, 0:2048]

    ldw0 = p.ld.val

    def wg_ld(e):
        p.fwait(e, "pe", "dve")
        p.ld.inc(e.dma_start(wgf, dr["wg_sb"].ap()), 16)
    p.blk.sync(wg_ld)

    def wg_gate(e):
        Plan.aw(e, p.ld, ldw0 + 16)
        p.pe.inc(e.nop())
    p.blk.tensor(wg_gate)
    p.fence()

    emit_ln(p, src, arb1, 0, router={"wg": wgf, "gl": gl})
    if upto == "ln":
        emit_dump(p, dst, lambda t: arb1[:, 2048 * t:2048 * (t + 1)])
        return
    if upto == "router_gl":
        emit_dump(p, dst, lambda t: gl[0:16, 0:2048], rows=16, tiles=1)
        return

    def tr_fwd(e):
        for c in range(16):
            mm = e.transpose(p.pb[4][:, 16 * c:16 * (c + 1)],
                             gl[:, 128 * c:128 * (c + 1)],
                             cf[0:16, CF_IDN:CF_IDN + 16])
        return mm

    def rank(e):
        dn0 = p.dve.val

        def step(inst):
            p.dve.inc(inst)
            e.wait_ge(p.dve.h, p.dve.val)

        pbi = p.smf[:, 0:16]
        cnt = p.smf[:, 16:32]
        msk = p.smf[:, 32:48]
        wv = p.smf[:, 48:64]
        ws = p.smf[:, 64:66]
        scrap = p.smf[:, 256:512]
        for c in range(16):
            ptk = p.pb[4][:, 16 * c:16 * (c + 1)]
            step(e.tensor_add(pbi, ptk, cf[:, CF_MOEB_REP:CF_MOEB_REP + 16]))
            pstep = pbi.ap[0][0]
            gt0 = mkap(pbi, [[pstep, 128], [1, 16], [0, 16]])
            gt1 = mkap(pbi, [[pstep, 128], [0, 16], [1, 16]])
            step(e.tensor_tensor(scrap.rearrange("p (a b) -> p a b", b=16),
                                 gt0, gt1, ALU.is_gt))
            step(e.reduce_sum(cnt, scrap.rearrange("p (a b) -> p b a", b=16),
                              axis=AX.X))
            step(e.tensor_scalar(msk, cnt, 7.5, None, ALU.is_le))
            step(e.tensor_mul(wv, ptk, msk))
            step(e.reduce_sum(ws[:, 0:1], wv, axis=AX.X))
            step(e.reciprocal(ws[:, 1:2], ws[:, 0:1]))
            wib = mkap(ws[:, 1:2], [[ws.ap[0][0], 128], [0, 16]])
            last = e.scalar_tensor_tensor(wn_tok[:, 16 * c:16 * (c + 1)],
                                          wv, MOE_SCALE, wib,
                                          ALU.mult, ALU.mult)
            if c < 15:
                step(last)
        return last

    TRB = [4, 5, 6, 0]

    def tr_bwd(e):
        for g in range(4):
            for j in range(4):
                c = 4 * g + j
                mm = e.transpose(p.pb[TRB[g]][0:16, 128 * j:128 * (j + 1)],
                                 wn_tok[:, 16 * c:16 * (c + 1)],
                                 cf[:, CF_IDN:CF_IDN + 128])
        return mm

    def tr_ev(e):
        for g in range(4):
            last = e.copy(wT16[:, 512 * g:512 * (g + 1)],
                          p.pb[TRB[g]][0:16, 0:512])
        return last

    p.chain([("tensor", tr_fwd), ("vector", rank), ("tensor", tr_bwd),
             ("scalar", tr_ev)])
    if upto == "router":
        emit_dump(p, dst, lambda t: wT16[0:16, 0:2048], rows=16, tiles=1)
        return

    cast0, pe0, act0, dve0 = p.cast.val, p.pe.val, p.act.val, p.dve.val

    def sh_cast(e):
        Plan.aw(e, p.cast, cast0)
        Plan.aw(e, p.pe, pe0)
        for ht in range(8):
            Plan.aw(e, p.pe, pe0 + 4 * (ht - 1))
            p.cast.inc(e.dma_start(wbuf[:, 2048 * (ht % 2):2048 * (ht % 2 + 1)],
                                   dr["sw1T"].ap()[128 * ht:128 * (ht + 1), :]), 16)
            p.cast.inc(e.dma_start(
                wbuf[:, 4096 + 2048 * (ht % 2):4096 + 2048 * (ht % 2 + 1)],
                dr["swgT"].ap()[128 * ht:128 * (ht + 1), :]), 16)
    p.blk.gpsimd(sh_cast)

    def sh_pe(e):
        p.fwait(e, "act", "dve")
        for ht in range(8):
            Plan.aw(e, p.cast, cast0 + 32 * min(ht + 2, 8))
            for b in range(NB):
                i = ht * NB + b
                Plan.aw(e, p.dve, dve0 + max(i - 1, 0) if i >= 2 else 0)
                pa = p.pb[2 * (i % 2)]
                pg = p.pb[2 * (i % 2) + 1]
                for c in range(DT):
                    e.matmul(pa[:, :], wbuf[:, 2048 * (ht % 2) + 128 * c:
                                            2048 * (ht % 2) + 128 * (c + 1)],
                             xn(c, b), start=(c == 0), stop=(c == DT - 1))
                for c in range(DT):
                    mm = e.matmul(pg[:, :], wbuf[:, 4096 + 2048 * (ht % 2) + 128 * c:
                                                 4096 + 2048 * (ht % 2) + 128 * (c + 1)],
                                  xn(c, b), start=(c == 0), stop=(c == DT - 1))
                p.pe.inc(mm)
    p.blk.tensor(sh_pe)

    def sh_act(e):
        for i in range(32):
            Plan.aw(e, p.pe, pe0 + i + 1)
            if i >= 2:
                Plan.aw(e, p.dve, dve0 + i - 1)
            pg = p.pb[2 * (i % 2) + 1]
            tmp = p.smf[:, 512 * (i % 2):512 * (i % 2 + 1)]
            e.activation(tmp, pg[:, :], AF.Relu)
            p.act.inc(e.activation(tmp, tmp, AF.Square))
    p.blk.scalar(sh_act)

    def sh_dve(e):
        for i in range(32):
            ht, b = divmod(i, NB)
            Plan.aw(e, p.act, act0 + i + 1)
            pa = p.pb[2 * (i % 2)]
            tmp = p.smf[:, 512 * (i % 2):512 * (i % 2 + 1)]
            dsth = arb2[:, 2048 * ht + 512 * b:2048 * ht + 512 * (b + 1)]
            p.dve.inc(e.tensor_mul(dsth, pa[:, :], tmp))
    p.blk.vector(sh_dve)
    p.fence()
    if upto == "shared":
        emit_dump(p, dst, lambda t: arb2[:, 2048 * t:2048 * (t + 1)], tiles=8)
        return

    cast0, pe0, dve0 = p.cast.val, p.pe.val, p.dve.val

    def xl_cast(e):
        Plan.aw(e, p.cast, cast0)
        Plan.aw(e, p.pe, pe0)
        for lt in range(4):
            Plan.aw(e, p.pe, pe0 + 4 * (lt - 1))
            p.cast.inc(e.dma_start(wbuf[:, 2048 * (lt % 2):2048 * (lt % 2 + 1)],
                                   dr["wdownT"].ap()[128 * lt:128 * (lt + 1), :]), 16)
    p.blk.gpsimd(xl_cast)

    def xl_pe(e):
        p.fwait(e, "dve", "act")
        for lt in range(4):
            Plan.aw(e, p.cast, cast0 + 16 * min(lt + 2, 4))
            for b in range(NB):
                i = lt * NB + b
                if i >= 2:
                    Plan.aw(e, p.dve, dve0 + i - 1)
                pbk = p.pb[i % 2]
                for c in range(DT):
                    mm = e.matmul(pbk[:, :], wbuf[:, 2048 * (lt % 2) + 128 * c:
                                                  2048 * (lt % 2) + 128 * (c + 1)],
                                  xn(c, b), start=(c == 0), stop=(c == DT - 1))
                p.pe.inc(mm)
    p.blk.tensor(xl_pe)

    def xl_dve(e):
        for i in range(16):
            lt, b = divmod(i, NB)
            Plan.aw(e, p.pe, pe0 + i + 1)
            dxl = wbuf[:, 4096 + 2048 * lt + 512 * b:4096 + 2048 * lt + 512 * (b + 1)]
            p.dve.inc(e.tensor_copy(dxl, p.pb[i % 2][:, :]))
    p.blk.vector(xl_dve)
    if upto == "xl":
        p.fence()
        emit_dump(p, dst, lambda t: wbuf[:, 4096 + 2048 * t:4096 + 2048 * (t + 1)],
                  tiles=4)
        return
    p.fence()
    xl = lambda c, b: wbuf[:, 4096 + 2048 * c + 512 * b:4096 + 2048 * c + 512 * (b + 1)]

    h_eb = arf.bitcast(BF16)
    H0 = 20480
    acc_t = lambda c, b: arf[:, 2048 * c + 512 * b:2048 * c + 512 * (b + 1)]
    wrep_b = lambda b: arf[:, 8192 + 512 * b:8192 + 512 * (b + 1)]

    cast0, pe0, act0, dve0 = p.cast.val, p.pe.val, p.act.val, p.dve.val
    PE_E, DVE_E, ACT_E = 40, 52, 32

    def d_wrep(ee, b):
        return dve0 + DVE_E * ee + b + 1

    def d_h(ee, b, ht):
        return dve0 + DVE_E * ee + 4 + 12 * b + ht + 1

    def d_acc(ee, b):
        return dve0 + DVE_E * ee + 4 + 12 * b + 12

    def g_pe(ee, b, ht):
        return pe0 + PE_E * ee + 4 + 9 * b + ht + 1

    def ex_cast(e):
        Plan.aw(e, p.cast, cast0)
        Plan.aw(e, p.pe, pe0)
        for ee in range(NE):
            if ee >= 2:
                Plan.aw(e, p.pe, pe0 + (ee - 1) * PE_E)
            p.cast.inc(e.dma_start(
                arb1[:, 12288 * (ee % 2):12288 * (ee % 2 + 1)],
                dr["rw_img"].ap()[128 * ee:128 * (ee + 1), :]), 16)
    p.blk.gpsimd(ex_cast)

    def ex_pe(e):
        p.fwait(e, "dve", "act")
        for ee in range(NE):
            s = 12288 * (ee % 2)
            Plan.aw(e, p.cast, cast0 + 16 * min(ee + 2, NE))
            if ee > 0:
                Plan.aw(e, p.dve, dve0 + DVE_E * ee)
            for b in range(NB):
                if b > 0 or ee > 0:
                    pass
                Plan.aw(e, p.dve, d_wrep(ee, b) - 1 if (ee, b) != (0, 0) else 0)
                p.pe.inc(e.matmul(p.pb[6][:, :],
                                  p.cbuf2[:, 128 * ee:128 * (ee + 1)],
                                  wT16[:, 512 * b:512 * (b + 1)],
                                  start=True, stop=True))
            for b in range(NB):
                if b > 0:
                    Plan.aw(e, p.dve, d_acc(ee, b - 1))
                for ht in range(8):
                    g = 8 * b + ht
                    pr = g % 3
                    if ht >= 3:
                        Plan.aw(e, p.dve, d_h(ee, b, ht - 3))
                    pa, pg = p.pb[2 * pr], p.pb[2 * pr + 1]
                    for c in range(4):
                        e.matmul(pa[:, :], arb1[:, s + 512 * ht + 128 * c:
                                                s + 512 * ht + 128 * (c + 1)],
                                 xl(c, b), start=(c == 0), stop=(c == 3))
                    for c in range(4):
                        mm = e.matmul(pg[:, :], arb1[:, s + 4096 + 512 * ht + 128 * c:
                                                     s + 4096 + 512 * ht + 128 * (c + 1)],
                                      xl(c, b), start=(c == 0), stop=(c == 3))
                    p.pe.inc(mm)
                Plan.aw(e, p.dve, d_h(ee, b, 7))
                for lt in range(4):
                    for ht in range(8):
                        hsrc = h_eb[:, H0 + 512 * ht:H0 + 512 * (ht + 1)]
                        mm = e.matmul(p.pb[lt][:, :],
                                      arb1[:, s + 8192 + 1024 * lt + 128 * ht:
                                           s + 8192 + 1024 * lt + 128 * (ht + 1)],
                                      hsrc, start=(ht == 0), stop=(ht == 7))
                p.pe.inc(mm)
    p.blk.tensor(ex_pe)

    def ex_act(e):
        for ee in range(NE):
            for b in range(NB):
                for ht in range(8):
                    g = 8 * b + ht
                    Plan.aw(e, p.pe, g_pe(ee, b, ht))
                    gb2 = g - 2
                    if gb2 >= 0:
                        Plan.aw(e, p.dve, d_h(ee, gb2 // 8, gb2 % 8))
                    elif ee > 0:
                        Plan.aw(e, p.dve, dve0 + DVE_E * ee)
                    pg = p.pb[2 * (g % 3) + 1]
                    tmp = p.smf[:, 512 * (g % 2):512 * (g % 2 + 1)]
                    e.activation(tmp, pg[:, :], AF.Relu)
                    p.act.inc(e.activation(tmp, tmp, AF.Square))
    p.blk.scalar(ex_act)

    def ex_dve(e):
        for ee in range(NE):
            for b in range(NB):
                Plan.aw(e, p.pe, pe0 + PE_E * ee + b + 1)
                p.dve.inc(e.tensor_copy(wrep_b(b), p.pb[6][:, :]))
            for b in range(NB):
                for ht in range(8):
                    g = 8 * b + ht
                    Plan.aw(e, p.act, act0 + ACT_E * ee + g + 1)
                    pa = p.pb[2 * (g % 3)]
                    tmp = p.smf[:, 512 * (g % 2):512 * (g % 2 + 1)]
                    hdst = h_eb[:, H0 + 512 * ht:H0 + 512 * (ht + 1)]
                    p.dve.inc(e.tensor_mul(hdst, pa[:, :], tmp))
                Plan.aw(e, p.pe, pe0 + PE_E * ee + 4 + 9 * b + 9)
                for lt in range(4):
                    t2 = p.smf2[:, 512:1024]
                    if ee == 0:
                        p.dve.inc(e.tensor_mul(acc_t(lt, b), p.pb[lt][:, :],
                                               wrep_b(b)))
                    else:
                        e.tensor_mul(t2, p.pb[lt][:, :], wrep_b(b))
                        p.dve.inc(e.tensor_add(acc_t(lt, b), acc_t(lt, b), t2))
    p.blk.vector(ex_dve)
    p.fence()
    if upto == "experts":
        emit_dump(p, dst, lambda t: arf[:, 2048 * t:2048 * (t + 1)], tiles=4)
        return

    def acc_cvt(e):
        for c in range(4):
            last = e.tensor_copy(wbuf[:, 4096 + 2048 * c:4096 + 2048 * (c + 1)],
                                 arf[:, 2048 * c:2048 * (c + 1)])
        return last
    p.chain([("vector", acc_cvt)])
    accb = lambda c, b: wbuf[:, 4096 + 2048 * c + 512 * b:
                             4096 + 2048 * c + 512 * (b + 1)]

    cast0, pe0, dve0, ld0, st0 = (p.cast.val, p.pe.val, p.dve.val,
                                  p.ld.val, p.st.val)

    def cm_cast(e):
        Plan.aw(e, p.cast, cast0)
        Plan.aw(e, p.pe, pe0)
        for dt_ in range(DT):
            if dt_ >= 2:
                Plan.aw(e, p.pe, pe0 + 4 * (dt_ - 1))
            p.cast.inc(e.dma_start(
                wbuf[:, 1536 * (dt_ % 2):1536 * (dt_ % 2 + 1)],
                dr["comb_img"].ap()[128 * dt_:128 * (dt_ + 1), :]), 16)
    p.blk.gpsimd(cm_cast)

    def cm_sync(e):
        p.fwait(e, "dve")
        Plan.aw(e, p.ld, ld0)
        Plan.aw(e, p.st, st0)
        for i in range(DT * NB):
            if i >= 3:
                Plan.aw(e, p.dve, dve0 + i - 2)
            dt_, b = divmod(i, NB)
            p.ld.inc(e.dma_start(
                p.xbuf[:, 512 * (i % 3):512 * (i % 3 + 1)],
                src.ap()[128 * dt_:128 * (dt_ + 1), BLK * b:BLK * (b + 1)]), 16)
    p.blk.sync(cm_sync)

    def cm_pe(e):
        p.fwait(e, "dve", "act")
        for dt_ in range(DT):
            Plan.aw(e, p.cast, cast0 + 16 * min(dt_ + 2, DT))
            for b in range(NB):
                i = dt_ * NB + b
                if i >= 2:
                    Plan.aw(e, p.dve, dve0 + i - 1)
                pbk = p.pb[i % 2]
                w0 = 1536 * (dt_ % 2)
                for c in range(8):
                    e.matmul(pbk[:, :], wbuf[:, w0 + 128 * c:w0 + 128 * (c + 1)],
                             arb2[:, 2048 * c + 512 * b:2048 * c + 512 * (b + 1)],
                             start=(c == 0), stop=False)
                for c in range(4):
                    mm = e.matmul(pbk[:, :],
                                  wbuf[:, w0 + 1024 + 128 * c:w0 + 1024 + 128 * (c + 1)],
                                  accb(c, b), start=False, stop=(c == 3))
                p.pe.inc(mm)
    p.blk.tensor(cm_pe)

    def cm_dve(e):
        for i in range(DT * NB):
            Plan.aw(e, p.pe, pe0 + i + 1)
            Plan.dwait(e, p.ld, ld0, i + 3, DT * NB)
            if i >= 2:
                Plan.aw(e, p.st, st0 + 16 * (i - 1))
            xsrc = p.xbuf[:, 512 * (i % 3):512 * (i % 3 + 1)]
            out = p.smf2[:, 512 * (i % 2):512 * (i % 2 + 1)]
            p.dve.inc(e.scalar_tensor_tensor(out, p.pb[i % 2][:, :], 0.5, xsrc,
                                             ALU.mult, ALU.add))
    p.blk.vector(cm_dve)

    def cm_st(e):
        for i in range(DT * NB):
            dt_, b = divmod(i, NB)
            Plan.aw(e, p.dve, dve0 + i + 1)
            out = p.smf2[:, 512 * (i % 2):512 * (i % 2 + 1)]
            p.st.inc(e.dma_start(
                dst.ap()[128 * dt_:128 * (dt_ + 1), BLK * b:BLK * (b + 1)], out), 16)
    p.blk.scalar(cm_st)
    p.fence()



def build_program(sublayers, final_dst="out", upto=None):
    nc = bass.Bass("TRN2", target_bir_lowering=False)
    dr = {}
    dr["x_in"] = nc.dram_tensor("x_in", [D, T], F32, kind="ExternalInput")
    dr["out"] = nc.dram_tensor("out", [D, T], F32, kind="ExternalOutput")
    for nm, (shape, dt_) in WEIGHT_SPECS.items():
        dr[nm] = nc.dram_tensor(nm, list(shape), dt_, kind="ExternalInput")
    for nm, (shape, dt_) in SCRATCH_SPECS.items():
        dr[nm] = nc.dram_tensor(nm, list(shape), dt_)

    with contextlib.ExitStack() as ctx:
        p = make_plan(nc, ctx)
        emit_load_consts(p, dr)
        emit_prep_small(p)
        srcs = ["x_in"] + ["resid"] * (len(sublayers) - 1)
        dsts = ["resid"] * (len(sublayers) - 1) + [final_dst]
        for sl, s, d in zip(sublayers, srcs, dsts):
            if sl == "ffn0a" or sl == "ffn0b":
                emit_ffn_l0(p, dr, dr[s], dr[d], upto=upto)
            elif sl == "mamba":
                emit_mamba(p, dr, dr[s], dr[d], upto=upto)
            elif sl == "attn":
                emit_attn_l1(p, dr, dr[s], dr[d], upto=upto)
            elif sl == "ffn1a":
                emit_ffn_l1(p, dr, dr[s], dr[d], dequant=True)
            elif sl == "ffn1b":
                emit_ffn_l1(p, dr, dr[s], dr[d], dequant=False)
            else:
                raise ValueError(sl)
        stv = p.st.val

        def fin(e):
            e.wait_ge(p.st.h, stv)
        p.blk.sync(fin)
    return nc


def emit_mamba(p, dr, src, dst, upto=None):
    cb, cf = p.cbuf, p.cbuf_f
    arb1, arb2, arf, wbuf = p.arb1, p.arb2, p.arf, p.wbuf
    xb, srh = p.xbuf, p.smb
    mrep = p.smf[:, 0:512]
    rrep = p.smf[:, 512:1024]
    r_mean = p.rows[0:1, 0:512]
    r_rstd = p.rows[0:1, 512:1024]
    r_sq = p.rows[0:1, 1024:1536]
    r_tmp = p.smf2[0:1, 0:512]
    rstd2_all = arf[0:1, 8192:10240]

    for b in range(NB):
        ld0, act0, pe0, dve0, st0 = (p.ld.val, p.act.val, p.pe.val,
                                     p.dve.val, p.st.val)

        def sync1(e, ld0=ld0, act0=act0, b=b, st0=st0):
            p.fwait(e, "dve", "act")
            Plan.aw(e, p.ld, ld0)
            Plan.aw(e, p.st, st0)
            for t in range(DT):
                Plan.aw(e, p.act, act0 + t - 2)
                p.ld.inc(e.dma_start(
                    xb[:, 512 * (t % 3):512 * (t % 3 + 1)],
                    src.ap()[128 * t:128 * (t + 1), BLK * b:BLK * (b + 1)]), 16)
        p.blk.sync(sync1)
        assert p.ld.val == ld0 + 16 * DT, ("cnt", "ld", p.ld.val, ld0 + 16 * DT)

        def act1(e, ld0=ld0, pe0=pe0, b=b):
            p.fwait(e, "dve", "pe")
            for t in range(DT):
                Plan.dwait(e, p.ld, ld0, t + 3, DT)
                Plan.aw(e, p.pe, pe0 + t - 1)
                s2 = xb[:, 512 * (t % 3):512 * (t % 3 + 1)]
                dstp = srh[:, 1024 * (t % 2):1024 * (t % 2 + 1)]
                e.copy(dstp[:, 0:512], s2)
                p.act.inc(e.activation(dstp[:, 512:1024], s2, AF.Square))
        p.blk.scalar(act1)
        assert p.act.val == act0 + DT, ("cnt", "act", p.act.val, act0 + DT)

        def pe1(e, act0=act0, b=b):
            p.fwait(e, "dve")
            for t in range(DT):
                Plan.aw(e, p.act, act0 + t + 1)
                dstp = srh[:, 1024 * (t % 2):1024 * (t % 2 + 1)]
                first, last = (t == 0), (t == DT - 1)
                e.matmul(p.pb[4][0:1, :], cb[:, CB_ONES_COL:CB_ONES_COL + 1],
                         dstp[:, 0:512], start=first, stop=last)
                p.pe.inc(e.matmul(p.pb[5][0:1, :],
                                  cb[:, CB_ONES_COL:CB_ONES_COL + 1],
                                  dstp[:, 512:1024], start=first, stop=last))
        p.blk.tensor(pe1)
        assert p.pe.val == pe0 + DT, ("cnt", "pe", p.pe.val, pe0 + DT)
        p.fence()

        p.chain([
            ("vector", lambda e: [
                selfstep(p, e, p.dve,
                         e.tensor_scalar_mul(r_mean, p.pb[4][0:1, :], 1.0 / D)),
                selfstep(p, e, p.dve,
                         e.tensor_scalar_mul(r_sq, p.pb[5][0:1, :], 1.0 / D)),
                selfstep(p, e, p.dve,
                         e.tensor_tensor(r_tmp, r_mean, r_mean, ALU.mult)),
                selfstep(p, e, p.dve, e.tensor_sub(r_rstd, r_sq, r_tmp)),
                e.tensor_scalar_add(r_rstd, r_rstd, float(LN_EPS))][-1]),
            ("scalar", lambda e: [
                selfstep(p, e, p.act, e.activation(r_rstd, r_rstd, AF.Ln)),
                e.activation(r_rstd, r_rstd, AF.Exp, scale=-0.5)][-1]),
            ("tensor", lambda e: [
                e.matmul(p.pb[4][:, :], cf[0:1, CF_ONES_ROW:CF_ONES_ROW + 128],
                         r_mean, start=True, stop=True),
                e.matmul(p.pb[5][:, :], cf[0:1, CF_ONES_ROW:CF_ONES_ROW + 128],
                         r_rstd, start=True, stop=True)][-1]),
            ("vector", lambda e: [
                e.tensor_copy(mrep, p.pb[4][:, :]),
                e.tensor_copy(rrep, p.pb[5][:, :])][-1]),
        ])

        ld2, dve2, act2, pe2, st2 = (p.ld.val, p.dve.val, p.act.val,
                                     p.pe.val, p.st.val)

        def sync2(e, ld2=ld2, dve2=dve2, b=b, st2=st2):
            Plan.aw(e, p.ld, ld2)
            Plan.aw(e, p.act, p.act.val)
            Plan.aw(e, p.st, st2)
            for t in range(DT):
                if t >= 3:
                    Plan.aw(e, p.dve, dve2 + 2 * (t - 2))
                p.ld.inc(e.dma_start(
                    xb[:, 512 * (t % 3):512 * (t % 3 + 1)],
                    src.ap()[128 * t:128 * (t + 1), BLK * b:BLK * (b + 1)]), 16)
        p.blk.sync(sync2)
        assert p.ld.val == ld2 + 16 * DT, ("cnt", "ld", p.ld.val, ld2 + 16 * DT)

        def dve3(e, ld2=ld2, st2=st2, b=b):
            for t in range(DT):
                Plan.dwait(e, p.ld, ld2, t + 3, DT)
                if t >= 2:
                    Plan.aw(e, p.st, st2 + 16 * 2 * (t - 1))
                s2 = xb[:, 512 * (t % 3):512 * (t % 3 + 1)]
                v = p.smf2[:, 512 * (t % 2):512 * (t % 2 + 1)]
                rp = arf[:, 512 * (t % 2):512 * (t % 2 + 1)]
                e.tensor_sub(v, s2, mrep)
                p.dve.inc(e.tensor_mul(v, v, rrep))
                p.dve.inc(e.tensor_add(rp, s2, v))
        p.blk.vector(dve3)
        assert p.dve.val == dve2 + 2 * DT, ("cnt", "dve", p.dve.val, dve2 + 2 * DT)

        def act3(e, dve2=dve2, pe2=pe2, st2=st2, b=b):
            for t in range(DT):
                Plan.aw(e, p.dve, dve2 + 2 * t + 1)
                if t >= 2:
                    Plan.aw(e, p.pe, pe2 + t - 1)
                v = p.smf2[:, 512 * (t % 2):512 * (t % 2 + 1)]
                sq = srh[:, 1024 * (t % 2):1024 * (t % 2) + 512]
                p.act.inc(e.activation(sq, v, AF.Square))
                rp = arf[:, 512 * (t % 2):512 * (t % 2 + 1)]
                Plan.aw(e, p.dve, dve2 + 2 * t + 2)
                p.st.inc(e.dma_start(
                    dr["vbuf"].ap()[128 * t:128 * (t + 1), BLK * b:BLK * (b + 1)],
                    v), 16)
                p.st.inc(e.dma_start(
                    dr["r1buf"].ap()[128 * t:128 * (t + 1), BLK * b:BLK * (b + 1)],
                    rp), 16)
        p.blk.scalar(act3)
        assert p.act.val == act2 + DT, ("cnt", "act", p.act.val, act2 + DT)
        assert p.st.val == st2 + 32 * DT, ("cnt", "st", p.st.val, st2 + 32 * DT)

        def pe3(e, act2=act2, b=b):
            for t in range(DT):
                Plan.aw(e, p.act, act2 + t + 1)
                sq = srh[:, 1024 * (t % 2):1024 * (t % 2) + 512]
                p.pe.inc(e.matmul(p.pb[6][0:1, :],
                                  cb[:, CB_ONES_COL:CB_ONES_COL + 1], sq,
                                  start=(t == 0), stop=(t == DT - 1)))
        p.blk.tensor(pe3)
        assert p.pe.val == pe2 + DT, ("cnt", "pe", p.pe.val, pe2 + DT)
        p.fence()

        rs = rstd2_all[0:1, 512 * b:512 * (b + 1)]

        def vrms(e, rs=rs):
            e.tensor_scalar_mul(r_tmp, p.pb[6][0:1, :], 1.0 / D)
            return e.tensor_scalar_add(rs, r_tmp, float(RMS_EPS_MAMBA))

        def srms(e, rs=rs):
            e.activation(rs, rs, AF.Ln)
            return e.activation(rs, rs, AF.Exp, scale=-0.5)
        p.chain([("vector", vrms), ("scalar", srms)])

    for b in range(NB):
        def reprs(e, b=b):
            return e.matmul(p.pb[5][:, :], cf[0:1, CF_ONES_ROW:CF_ONES_ROW + 128],
                            rstd2_all[0:1, 512 * b:512 * (b + 1)],
                            start=True, stop=True)
        p.chain([("tensor", reprs),
                 ("vector", lambda e: e.tensor_copy(rrep, p.pb[5][:, :]))])
        ld0, dve0 = p.ld.val, p.dve.val

        def syncv(e, ld0=ld0, dve0=dve0, b=b):
            Plan.aw(e, p.ld, ld0)
            Plan.aw(e, p.dve, dve0)
            Plan.aw(e, p.st, p.st.val)
            for t in range(DT):
                if t >= 3:
                    Plan.aw(e, p.dve, dve0 + t - 2)
                p.ld.inc(e.dma_start(
                    xb[:, 512 * (t % 3):512 * (t % 3 + 1)],
                    dr["vbuf"].ap()[128 * t:128 * (t + 1), BLK * b:BLK * (b + 1)]), 16)
        p.blk.sync(syncv)
        assert p.ld.val == ld0 + 16 * DT, ("cnt", "ld", p.ld.val, ld0 + 16 * DT)

        def dvev(e, ld0=ld0, b=b):
            for t in range(DT):
                Plan.dwait(e, p.ld, ld0, t + 3, DT)
                s2 = xb[:, 512 * (t % 3):512 * (t % 3 + 1)]
                dstx = arb1[:, 2048 * t + 512 * b:2048 * t + 512 * (b + 1)]
                p.dve.inc(e.scalar_tensor_tensor(
                    dstx, s2, cf[:, CF_WNORM + t:CF_WNORM + t + 1], rrep,
                    ALU.mult, ALU.mult))
        p.blk.vector(dvev)
        assert p.dve.val == dve0 + DT, ("cnt", "dve", p.dve.val, dve0 + DT)
        p.fence()
    xn = lambda c, b: arb1[:, 2048 * c + 512 * b:2048 * c + 512 * (b + 1)]
    if upto == "xn":
        emit_dump(p, dst, lambda t: arb1[:, 2048 * t:2048 * (t + 1)])
        return

    NT = 69
    ld0, cast0, pe0, act0, dve0, st0 = (p.ld.val, p.cast.val, p.pe.val,
                                        p.act.val, p.dve.val, p.st.val)

    def is_xi(tt):
        return 32 <= tt < 64

    def n_xi(tt):
        return max(0, min(tt, 64) - 32)

    def dve_base(tt):
        return dve0 + 5 * n_xi(tt) + 4 * (tt - n_xi(tt))

    def act_base(tt):
        return act0 + n_xi(tt)

    def pe_base(tt):
        return pe0 + 4 * tt + n_xi(tt)

    def st_jz(tt):
        return 4 * (min(tt, 32) + max(0, tt - 64))

    def pj_cast(e):
        Plan.aw(e, p.cast, cast0)
        Plan.aw(e, p.pe, pe0)
        for tt in range(NT):
            if tt >= 2:
                Plan.aw(e, p.pe, pe_base(tt - 1))
            p.cast.inc(e.dma_start(
                wbuf[:, 2048 * (tt % 2):2048 * (tt % 2 + 1)],
                dr["w_inT"].ap()[128 * tt:128 * (tt + 1), :]), 16)
    p.blk.gpsimd(pj_cast)
    assert p.cast.val == cast0 + 16 * NT, ("cnt", "cast", p.cast.val, cast0 + 16 * NT)

    def pj_pe(e):
        p.fwait(e, "dve", "act")
        for tt in range(NT):
            if tt >= 33 and is_xi(tt - 1):
                i = tt - 1 - 32
                Plan.aw(e, p.act, act_base(tt - 1) + 1)
                sl = arb2[:, 2048 * ((tt - 1) % 2):2048 * ((tt - 1) % 2 + 1)]
                for b in range(NB):
                    mm = e.matmul(p.pb[2 + b][0:16, 0:512],
                                  cb[:, CB_XMEAN + 16 * i:CB_XMEAN + 16 * (i + 1)],
                                  sl[:, 512 * b:512 * (b + 1)],
                                  start=(tt - 1 == 32), stop=(tt - 1 == 63))
                p.pe.inc(mm)
            Plan.aw(e, p.cast, cast0 + 16 * min(tt + 2, NT))
            for b in range(NB):
                i = tt * NB + b
                if i >= 2:
                    pb_, bb_ = divmod(i - 2, NB)
                    Plan.aw(e, p.dve, dve_base(pb_) + bb_ + 1)
                for c in range(DT):
                    mm = e.matmul(p.pb[i % 2][:, :],
                                  wbuf[:, 2048 * (tt % 2) + 128 * c:
                                       2048 * (tt % 2) + 128 * (c + 1)],
                                  xn(c, b), start=(c == 0), stop=(c == DT - 1))
                p.pe.inc(mm)
    p.blk.tensor(pj_pe)
    assert p.pe.val == pe_base(NT), ("cnt", "pe", p.pe.val, pe_base(NT))

    def pj_dve(e):
        jz = 0
        for tt in range(NT):
            base = dve_base(tt)
            for b in range(NB):
                i = tt * NB + b
                Plan.aw(e, p.pe, pe_base(tt) + b + 1)
                if is_xi(tt):
                    if b == 0 and tt >= 34:
                        Plan.aw(e, p.act, act_base(tt - 1))
                    dstt = arf[:, 2048 * (tt % 2) + 512 * b:
                               2048 * (tt % 2) + 512 * (b + 1)]
                    p.dve.inc(e.tensor_copy(dstt, p.pb[i % 2][:, :]))
                else:
                    sl = p.smf[:, 512 * (jz % 2):512 * (jz % 2 + 1)]
                    Plan.aw(e, p.st, st0 + 16 * (st_jz(tt) + b - 1))
                    p.dve.inc(e.tensor_copy(sl, p.pb[i % 2][:, :]))
                    jz += 1
            if is_xi(tt):
                asm = arf[:, 2048 * (tt % 2):2048 * (tt % 2 + 1)]
                acc = arf[:, 4096:6144]
                it = tt - 32
                if tt > 32:
                    Plan.aw(e, p.act, act_base(tt))
                e.tensor_scalar(acc, asm, cf[:, CF_WCONV + 4 * it + 3:
                                             CF_WCONV + 4 * it + 4],
                                None, ALU.mult)
                last = None
                for tau in range(3):
                    sh = 3 - tau
                    last = e.scalar_tensor_tensor(
                        acc[:, sh:2048], asm[:, 0:2048 - sh],
                        cf[:, CF_WCONV + 4 * it + tau:CF_WCONV + 4 * it + tau + 1],
                        acc[:, sh:2048], ALU.mult, ALU.add)
                p.dve.inc(last)
    p.blk.vector(pj_dve)
    assert p.dve.val == dve_base(NT), ("cnt", "dve", p.dve.val, dve_base(NT))

    def pj_act(e):
        for tt in range(NT):
            base = dve_base(tt)
            if is_xi(tt):
                Plan.aw(e, p.dve, base + 5)
                if tt >= 34:
                    Plan.aw(e, p.pe, pe_base(tt - 1) + 1)
                acc = arf[:, 4096:6144]
                sl = arb2[:, 2048 * (tt % 2):2048 * (tt % 2 + 1)]
                p.act.inc(e.activation(sl, acc, AF.Silu))
            else:
                for b in range(NB):
                    Plan.aw(e, p.dve, base + b + 1)
                    jz2 = st_jz(tt) + b
                    sl = p.smf[:, 512 * (jz2 % 2):512 * (jz2 % 2 + 1)]
                    if tt < 32:
                        dd = dr["zbuf"].ap()[128 * tt:128 * (tt + 1),
                                             BLK * b:BLK * (b + 1)]
                    else:
                        dd = dr["dtbc"].ap()[128 * (tt - 64):128 * (tt - 63),
                                             BLK * b:BLK * (b + 1)]
                    p.st.inc(e.dma_start(dd, sl), 16)
    p.blk.scalar(pj_act)
    assert p.act.val == act_base(NT), ("cnt", "act", p.act.val, act_base(NT))
    assert p.st.val == st0 + 16 * st_jz(NT), ("cnt", "st", p.st.val, st0 + 16 * st_jz(NT))
    p.fence()

    def xm_ev(e):
        for b in range(NB):
            last = e.tensor_copy(arf[0:16, 6144 + 512 * b:6144 + 512 * (b + 1)],
                                 p.pb[2 + b][0:16, 0:512])
        return last
    p.chain([("vector", xm_ev)])
    x_mean = arf[0:16, 6144:8192]
    if upto == "proj":
        emit_dump(p, dst, lambda t: arf[0:16, 6144:8192], rows=16, tiles=1)
        return

    dtS = arf[0:16, 0:2048]
    v_s = arf[0:16, 2048:4096]
    y_all = arf[0:16, 4096:6144]
    absb = arb2.bitcast(F32)[0:16, 2048:4096]
    ld00 = p.ld.val

    def b4_sync0(e):
        p.fwait(e, "dve", "act", "pe")
        Plan.aw(e, p.st, p.st.val)
        p.ld.inc(e.dma_start(dtS, dr["dtbc"].ap()[0:16, :]), 16)
    p.blk.sync(b4_sync0)
    assert p.ld.val == ld00 + 16, ("cnt", "ld", p.ld.val, ld00 + 16)

    def v1(e):
        Plan.aw(e, p.ld, ld00 + 16)
        e.tensor_scalar(dtS, dtS, cf[0:16, CF_DTBIAS:CF_DTBIAS + 1],
                        None, ALU.add)
        e.tensor_scalar_max(v_s, dtS, 0.0)
        e.tensor_scalar_mul(absb, dtS, -1.0)
        return e.tensor_tensor(absb, absb, dtS, ALU.max)

    def s1(e):
        e.activation(absb, absb, AF.Exp, scale=-1.0)
        return e.activation(absb, absb, AF.Ln, bias=1.0)

    def v2(e):
        e.tensor_add(dtS, v_s, absb)
        e.tensor_mul(v_s, dtS, x_mean)
        return e.tensor_scalar(dtS, dtS, cf[0:16, CF_NALOGE:CF_NALOGE + 1],
                               None, ALU.mult)

    def s2(e):
        return e.activation(dtS, dtS, AF.Exp)

    p.chain([("vector", v1), ("scalar", s1), ("vector", v2), ("scalar", s2)])

    Bq = lambda k: arf[:, 6144 + 512 * k:6144 + 512 * (k + 1)]
    Cq = lambda k: arf[:, 7168 + 512 * k:7168 + 512 * (k + 1)]
    vrep = lambda k: arf[:, 8192 + 512 * k:8192 + 512 * (k + 1)]
    darep = lambda k: arf[:, 9216 + 512 * k:9216 + 512 * (k + 1)]
    hq = lambda k: arf[:, 10240 + 512 * k:10240 + 512 * (k + 1)]
    hstate = p.smf2[:, 512:514]

    p.chain([("vector", lambda e: e.memset(hstate, 0.0))])

    for q in range(4):
        ldq = p.ld.val

        def b4_sync(e, q=q, ldq=ldq):
            Plan.aw(e, p.dve, p.dve.val)
            for k in range(2):
                p.ld.inc(e.dma_start(
                    Bq(k), dr["dtbc"].ap()[128 + 128 * k:256 + 128 * k,
                                           512 * q:512 * (q + 1)]), 16)
                p.ld.inc(e.dma_start(
                    Cq(k), dr["dtbc"].ap()[384 + 128 * k:512 + 128 * k,
                                           512 * q:512 * (q + 1)]), 16)
        p.blk.sync(b4_sync)
        assert p.ld.val == ldq + 64, ("cnt", "ld", p.ld.val, ldq + 64)

        def mkrep(e, q=q, ldq=ldq):
            Plan.aw(e, p.ld, ldq + 64)
            for k in range(2):
                e.matmul(p.pb[2 + k][:, :],
                         cf[0:16, CF_HREP + 128 * k:CF_HREP + 128 * (k + 1)],
                         v_s[0:16, 512 * q:512 * (q + 1)],
                         start=True, stop=True)
                mm = e.matmul(p.pb[4 + k][:, :],
                              cf[0:16, CF_HREP + 128 * k:CF_HREP + 128 * (k + 1)],
                              dtS[0:16, 512 * q:512 * (q + 1)],
                              start=True, stop=True)
            return mm

        def evrep(e, q=q):
            for k in range(2):
                e.tensor_copy(vrep(k), p.pb[2 + k][:, :])
                e.tensor_copy(darep(k), p.pb[4 + k][:, :])
            for k in range(2):
                last = e.tensor_mul(Bq(k), Bq(k), vrep(k))
            return last

        def scan(e, q=q):
            e.tensor_tensor_scan(hq(0), darep(0), Bq(0), hstate[:, 0:1],
                                 ALU.mult, ALU.add)
            return e.tensor_tensor_scan(hq(1), darep(1), Bq(1), hstate[:, 1:2],
                                        ALU.mult, ALU.add)

        def post(e, q=q):
            e.tensor_copy(hstate[:, 0:1], hq(0)[:, 511:512])
            e.tensor_copy(hstate[:, 1:2], hq(1)[:, 511:512])
            e.tensor_mul(Cq(0), Cq(0), hq(0))
            return e.tensor_mul(Cq(1), Cq(1), hq(1))

        def ymm(e, q=q):
            e.matmul(p.pb[6][0:16, 0:512], cf[:, CF_HCOL:CF_HCOL + 16],
                     Cq(0), start=True, stop=False)
            return e.matmul(p.pb[6][0:16, 0:512],
                            cf[:, CF_HCOL + 16:CF_HCOL + 32],
                            Cq(1), start=False, stop=True)

        def yev(e, q=q):
            return e.tensor_copy(y_all[0:16, 512 * q:512 * (q + 1)],
                                 p.pb[6][0:16, 0:512])

        p.chain([("tensor", mkrep), ("vector", evrep), ("vector", scan),
                 ("vector", post), ("tensor", ymm), ("vector", yev)])
    if upto == "scan":
        emit_dump(p, dst, lambda t: y_all, rows=16, tiles=1)
        return

    wh2 = arf[0:16, 6144:10240]
    ldw = p.ld.val

    def b5_sync0(e):
        Plan.aw(e, p.dve, p.dve.val)
        p.ld.inc(e.dma_start(wh2, dr["w_h2iT"].ap()), 16)
    p.blk.sync(b5_sync0)
    assert p.ld.val == ldw + 16, ("cnt", "ld", p.ld.val, ldw + 16)

    for half in range(2):
        cast0, pe0, act0, dve0, ld0, st0 = (p.cast.val, p.pe.val, p.act.val,
                                            p.dve.val, p.ld.val, p.st.val)
        PE_B5, DVE_B5, ACT_B5, LD_B5 = 40, 40, 32, 40

        def h_cast(e, half=half, cast0=cast0, pe0=pe0):
            Plan.aw(e, p.cast, cast0)
            if half == 0:
                p.fwait(e, "pe")
            else:
                Plan.aw(e, p.pe, pe0)
            for k in range(8):
                p.cast.inc(e.dma_start(
                    arb1[:, 4096 * k:4096 * (k + 1)],
                    dr["w_outT"].ap()[128 * (8 * half + k):
                                      128 * (8 * half + k + 1), :]), 16)
        p.blk.gpsimd(h_cast)
        assert p.cast.val == cast0 + 128, ("cnt", "cast", p.cast.val, cast0 + 128)

        def b5_sync(e, half=half, ld0=ld0, dve0=dve0):
            Plan.aw(e, p.ld, ld0)
            if half == 1:
                Plan.aw(e, p.dve, dve0)
            for b in range(NB):
                for it in range(32):
                    if it >= 3:
                        Plan.aw(e, p.dve, dve0 + DVE_B5 * b + it - 2)
                    elif b > 0:
                        Plan.aw(e, p.dve, dve0 + DVE_B5 * b)
                    p.ld.inc(e.dma_start(
                        xb[:, 512 * (it % 3):512 * (it % 3 + 1)],
                        dr["zbuf"].ap()[128 * it:128 * (it + 1),
                                        BLK * b:BLK * (b + 1)]), 16)
                for dt_ in range(8):
                    j = 32 + dt_
                    Plan.aw(e, p.dve, dve0 + DVE_B5 * b + 30 + min(dt_, 2)
                            if dt_ < 3 else dve0 + DVE_B5 * b + 32 + dt_ - 2)
                    p.ld.inc(e.dma_start(
                        xb[:, 512 * (j % 3):512 * (j % 3 + 1)],
                        dr["r1buf"].ap()[128 * (8 * half + dt_):
                                         128 * (8 * half + dt_ + 1),
                                         BLK * b:BLK * (b + 1)]), 16)
        p.blk.sync(b5_sync)
        assert p.ld.val == ld0 + 16 * LD_B5 * NB, ("cnt", "ld", p.ld.val, ld0 + 16 * LD_B5 * NB)

        def b5_pe(e, half=half, cast0=cast0, pe0=pe0, act0=act0, dve0=dve0,
                  ld0=ld0):
            if half == 0:
                p.fwait(e, "dve", "act")
                Plan.aw(e, p.ld, ldw + 16)
            Plan.aw(e, p.cast, cast0 + 128)
            for b in range(NB):
                for it in range(32):
                    if it >= 2:
                        Plan.aw(e, p.dve, dve0 + DVE_B5 * b + it - 1)
                    elif b + half > 0:
                        Plan.aw(e, p.dve, dve0 + DVE_B5 * b)
                    p.pe.inc(e.matmul(
                        p.pb[it % 2][:, :], wh2[0:16, 128 * it:128 * (it + 1)],
                        y_all[0:16, 512 * b:512 * (b + 1)],
                        start=True, stop=True))
                Plan.aw(e, p.dve, dve0 + DVE_B5 * b + 32)
                for dt_ in range(8):
                    if dt_ >= 2:
                        Plan.aw(e, p.dve, dve0 + DVE_B5 * b + 32 + dt_ - 1)
                    for c in range(32):
                        mm = e.matmul(p.pb[2 + dt_ % 2][:, :],
                                      arb1[:, 4096 * dt_ + 128 * c:
                                           4096 * dt_ + 128 * (c + 1)],
                                      arb2[:, 512 * c:512 * (c + 1)],
                                      start=(c == 0), stop=(c == 31))
                    p.pe.inc(mm)
        p.blk.tensor(b5_pe)
        assert p.pe.val == pe0 + PE_B5 * NB, ("cnt", "pe", p.pe.val, pe0 + PE_B5 * NB)

        def b5_act(e, half=half, pe0=pe0, dve0=dve0, ld0=ld0, st0=st0):
            for b in range(NB):
                for it in range(32):
                    Plan.aw(e, p.pe, pe0 + PE_B5 * b + it + 1)
                    Plan.dwait(e, p.ld, ld0, min(LD_B5 * b + it + 3, LD_B5 * (b + 1)), LD_B5 * NB)
                    if it >= 2:
                        Plan.aw(e, p.dve, dve0 + DVE_B5 * b + it - 1)
                    sg = p.smf[:, 512 * (it % 2):512 * (it % 2 + 1)]
                    z = xb[:, 512 * (it % 3):512 * (it % 3 + 1)]
                    p.act.inc(e.activation(sg, z, AF.Silu))
                for dt_ in range(8):
                    Plan.aw(e, p.dve, dve0 + DVE_B5 * b + 32 + dt_ + 1)
                    out = p.smf2[:, 512 * (dt_ % 2):512 * (dt_ % 2 + 1)]
                    p.st.inc(e.dma_start(
                        dst.ap()[128 * (8 * half + dt_):
                                 128 * (8 * half + dt_ + 1),
                                 BLK * b:BLK * (b + 1)], out), 16)
        p.blk.scalar(b5_act)
        assert p.act.val == act0 + ACT_B5 * NB, ("cnt", "act", p.act.val, act0 + ACT_B5 * NB)
        assert p.st.val == st0 + 16 * 8 * NB

        def b5_dve(e, half=half, pe0=pe0, act0=act0, dve0=dve0, ld0=ld0,
                   st0=st0):
            for b in range(NB):
                for it in range(32):
                    Plan.aw(e, p.act, act0 + ACT_B5 * b + it + 1)
                    Plan.aw(e, p.pe, pe0 + PE_B5 * b + it + 1)
                    sg = p.smf[:, 512 * (it % 2):512 * (it % 2 + 1)]
                    p.dve.inc(e.tensor_mul(arb2[:, 512 * it:512 * (it + 1)],
                                           p.pb[it % 2][:, :], sg))
                for dt_ in range(8):
                    Plan.aw(e, p.pe, pe0 + PE_B5 * b + 32 + dt_ + 1)
                    Plan.dwait(e, p.ld, ld0, min(LD_B5 * b + 32 + dt_ + 3,
                               LD_B5 * (b + 1)), LD_B5 * NB)
                    if dt_ >= 2:
                        Plan.aw(e, p.st, st0 + 16 * (8 * b + dt_ - 1))
                    j = 32 + dt_
                    rp = xb[:, 512 * (j % 3):512 * (j % 3 + 1)]
                    out = p.smf2[:, 512 * (dt_ % 2):512 * (dt_ % 2 + 1)]
                    p.dve.inc(e.tensor_add(out, p.pb[2 + dt_ % 2][:, :], rp))
        p.blk.vector(b5_dve)
        assert p.dve.val == dve0 + DVE_B5 * NB, ("cnt", "dve", p.dve.val, dve0 + DVE_B5 * NB)

        p.fence()


def emit_attn_l1(p, dr, src, dst, upto=None):
    cb, cf = p.cbuf, p.cbuf_f
    arb1, arb2, arf, wbuf = p.arb1, p.arb2, p.arf, p.wbuf
    xb, smb, smf, smf2 = p.xbuf, p.smb, p.smf, p.smf2
    SC = float(1.0 / np.sqrt(HDIM))

    def tm_store(name, ot, b):
        base = dr[name].ap()[512 * b:512 * (b + 1), 128 * ot:128 * (ot + 1)]
        return mkap(base, [[T, 128], [128 * T, 4], [1, 128]])

    def tm_load(name, h):
        base = dr[name].ap()[0:T, 128 * h:128 * (h + 1)]
        return mkap(base, [[T, 128], [128 * T, 16], [1, 128]])

    def dump_dram(name):
        g0 = p.gp.val
        p.fence()

        def body(e):
            p.fwait(e, "pe", "dve", "act")
            Plan.aw(e, p.st, p.st.val)
            p.gp.inc(e.dma_start(dst.ap()[0:D, 0:T], dr[name].ap()), 16)
            e.wait_ge(p.gp.h, g0 + 16)
        p.blk.gpsimd(body)
        p.fence()

    emit_ln(p, src, arb1, 0)
    xn = lambda c, b: arb1[:, 2048 * c + 512 * b:2048 * c + 512 * (b + 1)]
    if upto == "attn_ln":
        emit_dump(p, dst, lambda t: arb1[:, 2048 * t:2048 * (t + 1)])
        return

    for wname, xsp, fsp in (("wqT", "qspill", "qfspill"),
                            ("wkT", "kspill", None)):
        isk = fsp is None
        PEb = 4 if isk else 2
        ACTb = 4 if isk else 2
        peW0 = p.pe.val
        for pr in range(8):
            cast0, pe0, act0, dve0, st0, ld0 = (p.cast.val, p.pe.val,
                                                p.act.val, p.dve.val,
                                                p.st.val, p.ld.val)
            ws = 4096 * (pr % 2)

            def c2_cast(e, pr=pr, wname=wname, cast0=cast0, ws=ws, peW0=peW0,
                        PEb=PEb):
                Plan.aw(e, p.cast, cast0)
                Plan.aw(e, p.pe, peW0 + PEb * NB * max(pr - 1, 0))
                p.cast.inc(e.dma_start(
                    wbuf[:, ws:ws + 2048],
                    dr[wname].ap()[128 * pr:128 * (pr + 1), :]), 16)
                p.cast.inc(e.dma_start(
                    wbuf[:, ws + 2048:ws + 4096],
                    dr[wname].ap()[128 * (pr + 8):128 * (pr + 9), :]), 16)
            p.blk.gpsimd(c2_cast)
            assert p.cast.val == cast0 + 32

            def c2_sync(e, pr=pr, ld0=ld0, dve0=dve0, pe0=pe0, act0=act0):
                Plan.aw(e, p.ld, ld0)
                Plan.aw(e, p.dve, dve0)
                Plan.aw(e, p.pe, pe0)
                Plan.aw(e, p.act, act0)
                for j, nm in enumerate(("rope_cos", "rope_sin")):
                    p.ld.inc(e.dma_start(
                        arf[:, 2048 * j:2048 * (j + 1)],
                        dr[nm].ap()[128 * pr:128 * (pr + 1), :]), 16)
                for j, nm in enumerate(("rope_cos", "rope_sin")):
                    p.ld.inc(e.dma_start(
                        arf[:, 4096 + 2048 * j:4096 + 2048 * (j + 1)],
                        dr[nm].ap()[128 * (pr + 8):128 * (pr + 9), :]), 16)
            p.blk.sync(c2_sync)
            assert p.ld.val == ld0 + 64

            def c2_pe(e, cast0=cast0, dve0=dve0, act0=act0, ws=ws, isk=isk):
                p.fwait(e, "dve", "act")
                Plan.aw(e, p.cast, cast0 + 32)
                for b in range(NB):
                    if b > 0:
                        Plan.aw(e, p.dve, dve0 + 14 * b)
                    else:
                        Plan.aw(e, p.dve, max(dve0 - 9, 0))
                    for c in range(DT):
                        mm = e.matmul(p.pb[0][:, :],
                                      wbuf[:, ws + 128 * c:ws + 128 * (c + 1)],
                                      xn(c, b), start=(c == 0), stop=(c == DT - 1))
                    p.pe.inc(mm)
                    for c in range(DT):
                        mm = e.matmul(
                            p.pb[1][:, :],
                            wbuf[:, ws + 2048 + 128 * c:ws + 2048 + 128 * (c + 1)],
                            xn(c, b), start=(c == 0), stop=(c == DT - 1))
                    p.pe.inc(mm)
                    if isk:
                        Plan.aw(e, p.dve, dve0 + 14 * b + 10)
                        Plan.aw(e, p.act, max(act0 + 4 * b, 0))
                        for j in range(4):
                            mm = e.transpose(
                                p.pbt[:, 128 * j:128 * (j + 1)],
                                smb[:, 1024 + 128 * j:1024 + 128 * (j + 1)],
                                cb[:, CB_IDN:CB_IDN + 128])
                        p.pe.inc(mm)
                        Plan.aw(e, p.dve, dve0 + 14 * b + 14)
                        Plan.aw(e, p.act, max(act0 + 4 * b + 3, 0))
                        for j in range(4):
                            mm = e.transpose(
                                p.pbt[:, 512 + 128 * j:512 + 128 * (j + 1)],
                                smb[:, 1536 + 128 * j:1536 + 128 * (j + 1)],
                                cb[:, CB_IDN:CB_IDN + 128])
                        p.pe.inc(mm)
            p.blk.tensor(c2_pe)
            assert p.pe.val == pe0 + PEb * NB

            def c2_dve(e, pe0=pe0, act0=act0, dve0=dve0, st0=st0, ld0=ld0,
                       PEb=PEb):
                for b in range(NB):
                    Plan.aw(e, p.pe, pe0 + PEb * b + 2)
                    Plan.aw(e, p.act, act0 + ACTb * b)
                    Plan.aw(e, p.st, st0 + 64 * b)
                    if b == 0:
                        Plan.aw(e, p.ld, ld0 + 64)
                    cA = arf[:, 512 * b:512 * (b + 1)]
                    sA = arf[:, 2048 + 512 * b:2048 + 512 * (b + 1)]
                    cB = arf[:, 4096 + 512 * b:4096 + 512 * (b + 1)]
                    sB = arf[:, 6144 + 512 * b:6144 + 512 * (b + 1)]
                    t1, t2 = smf[:, 0:512], smf[:, 512:1024]
                    xA, xB = smf2[:, 0:512], smf2[:, 512:1024]
                    p.dve.inc(e.tensor_mul(t1, p.pb[0][:, :], cA))
                    p.dve.inc(e.tensor_mul(t2, p.pb[1][:, :], sA))
                    p.dve.inc(e.tensor_sub(xA, t1, t2))
                    p.dve.inc(e.tensor_mul(t1, p.pb[1][:, :], cB))
                    p.dve.inc(e.tensor_mul(t2, p.pb[0][:, :], sB))
                    p.dve.inc(e.tensor_add(xB, t1, t2))
                    p.dve.inc(e.tensor_copy(smb[:, 0:512], xA))
                    p.dve.inc(e.tensor_scalar(t1, xA, 0.0, None, ALU.min))
                    Plan.aw(e, p.act, act0 + ACTb * b + 1)
                    p.dve.inc(e.tensor_scalar(t1, xA, 0.0, None, ALU.max))
                    p.dve.inc(e.tensor_add(smb[:, 1024:1536], t2, t1))
                    p.dve.inc(e.tensor_copy(smb[:, 512:1024], xB))
                    p.dve.inc(e.tensor_scalar(t1, xB, 0.0, None, ALU.min))
                    Plan.aw(e, p.act, act0 + ACTb * b + 2)
                    p.dve.inc(e.tensor_scalar(t1, xB, 0.0, None, ALU.max))
                    p.dve.inc(e.tensor_add(smb[:, 1536:2048], t2, t1))
            p.blk.vector(c2_dve)
            assert p.dve.val == dve0 + 14 * NB

            def c2_act(e, dve0=dve0, act0=act0, st0=st0, pe0=pe0, pr=pr,
                       isk=isk, xsp=xsp, fsp=fsp, PEb=PEb):
                for b in range(NB):
                    Plan.aw(e, p.st, st0 + 64 * b)
                    Plan.aw(e, p.dve, dve0 + 14 * b + 8)
                    p.act.inc(e.activation(smf[:, 512:1024], smf[:, 0:512],
                                           AF.Exp))
                    p.st.inc(e.dma_start(
                        dr[xsp].ap()[128 * pr:128 * (pr + 1),
                                     512 * b:512 * (b + 1)],
                        smb[:, 0:512]), 16)
                    Plan.aw(e, p.dve, dve0 + 14 * b + 12)
                    p.act.inc(e.activation(smf[:, 512:1024], smf[:, 0:512],
                                           AF.Exp))
                    p.st.inc(e.dma_start(
                        dr[xsp].ap()[128 * (pr + 8):128 * (pr + 9),
                                     512 * b:512 * (b + 1)],
                        smb[:, 512:1024]), 16)
                    if isk:
                        Plan.aw(e, p.pe, pe0 + PEb * b + 3)
                        p.act.inc(e.copy(arb2[:, 0:512], p.pbt[:, 0:512]))
                        p.st.inc(e.dma_start(tm_store("kfspill", pr, b),
                                             arb2[:, 0:512]), 16)
                        Plan.aw(e, p.pe, pe0 + PEb * b + 4)
                        p.act.inc(e.copy(arb2[:, 512:1024], p.pbt[:, 512:1024]))
                        p.st.inc(e.dma_start(tm_store("kfspill", pr + 8, b),
                                             arb2[:, 512:1024]), 16)
                    else:
                        p.st.inc(e.dma_start(
                            dr[fsp].ap()[128 * pr:128 * (pr + 1),
                                         512 * b:512 * (b + 1)],
                            smb[:, 1024:1536]), 16)
                        Plan.aw(e, p.dve, dve0 + 14 * b + 14)
                        p.st.inc(e.dma_start(
                            dr[fsp].ap()[128 * (pr + 8):128 * (pr + 9),
                                         512 * b:512 * (b + 1)],
                            smb[:, 1536:2048]), 16)
            p.blk.scalar(c2_act)
            assert p.act.val == act0 + ACTb * NB
            assert p.st.val == st0 + 16 * 4 * NB

    if upto in ("attn_v_slow", "attn_v_slowb"):
        pe_s, dve_s, act_s, st_s = p.pe.val, p.dve.val, p.act.val, p.st.val
        for ot in range(16):
            c0 = p.cast.val

            def vs_cast(e, ot=ot, c0=c0):
                p.fwait(e, "pe", "dve", "act")
                Plan.aw(e, p.pe, pe_s)
                Plan.aw(e, p.dve, dve_s)
                Plan.aw(e, p.act, act_s)
                Plan.aw(e, p.st, st_s)
                Plan.aw(e, p.cast, c0)
                p.cast.inc(e.dma_start(
                    wbuf[:, 0:2048],
                    dr["wvT"].ap()[128 * ot:128 * (ot + 1), :]), 16)
                e.wait_ge(p.cast.h, c0 + 16)
            p.blk.gpsimd(vs_cast)
            p.fence()
            for b in range(NB):
                def s_mm(e, ot=ot, b=b, c0=c0):
                    Plan.aw(e, p.cast, c0 + 16)
                    for c in range(DT):
                        mm = e.matmul(p.pb[0][:, :],
                                      wbuf[:, 128 * c:128 * (c + 1)],
                                      xn(c, b), start=(c == 0),
                                      stop=(c == DT - 1))
                    return mm

                def s_vbf(e):
                    return e.tensor_scalar_add(smb[:, 0:512], p.pb[0][:, :],
                                               0.0)

                def s_tr(e):
                    for j in range(4):
                        mm = e.transpose(p.pbt[:, 128 * j:128 * (j + 1)],
                                         smb[:, 128 * j:128 * (j + 1)],
                                         cb[:, CB_IDN:CB_IDN + 128])
                    return mm

                st_pre = p.st.val

                def s_ev(e, st_pre=st_pre):
                    Plan.aw(e, p.st, st_pre)
                    return e.tensor_copy(smb[:, 1024:1536], p.pbt[:, 0:512])

                def s_st(e, ot=ot, b=b):
                    p.st.inc(e.dma_start(tm_store("vtok", ot, b),
                                         smb[:, 1024:1536]), 16)
                    p.st.inc(e.dma_start(
                        dr["combspill"].ap()[128 * ot:128 * (ot + 1),
                                             512 * b:512 * (b + 1)],
                        smb[:, 0:512]), 16)
                    return e.activation(p.smf[0:1, 0:1], p.smf[0:1, 0:1],
                                        AF.Copy)
                p.chain([("tensor", s_mm), ("vector", s_vbf), ("tensor", s_tr),
                         ("vector", s_ev), ("scalar", s_st)])
        dump_dram("vtok" if upto == "attn_v_slow" else "combspill")
        return

    castV0, peV0, stV0 = p.cast.val, p.pe.val, p.st.val
    for ot in range(16):
        cast0, pe0, dve0, act0, st0 = (p.cast.val, p.pe.val, p.dve.val,
                                       p.act.val, p.st.val)

        def v_cast(e, ot=ot, cast0=cast0):
            Plan.aw(e, p.cast, cast0)
            Plan.aw(e, p.pe, peV0 + 8 * max(ot - 1, 0))
            p.cast.inc(e.dma_start(
                wbuf[:, 2048 * (ot % 2):2048 * (ot % 2 + 1)],
                dr["wvT"].ap()[128 * ot:128 * (ot + 1), :]), 16)
        p.blk.gpsimd(v_cast)

        def v_pe(e, cast0=cast0, dve0=dve0, ot=ot):
            Plan.aw(e, p.cast, cast0 + 16)
            for b in range(NB):
                Plan.aw(e, p.dve, max(dve0 + 2 * b - 3, 0))
                for c in range(DT):
                    mm = e.matmul(
                        p.pb[b % 2][:, :],
                        wbuf[:, 2048 * (ot % 2) + 128 * c:
                             2048 * (ot % 2) + 128 * (c + 1)],
                        xn(c, b), start=(c == 0), stop=(c == DT - 1))
                p.pe.inc(mm)
                Plan.aw(e, p.dve, dve0 + 2 * b + 1)
                for j in range(4):
                    mm = e.transpose(
                        p.pbt[:, 512 * (b % 2) + 128 * j:
                              512 * (b % 2) + 128 * (j + 1)],
                        smb[:, 512 * (b % 2) + 128 * j:
                            512 * (b % 2) + 128 * (j + 1)],
                        cb[:, CB_IDN:CB_IDN + 128])
                p.pe.inc(mm)
        p.blk.tensor(v_pe)
        assert p.pe.val == pe0 + 2 * NB

        def v_dve(e, pe0=pe0, dve0=dve0, ot=ot):
            if ot == 0:
                Plan.aw(e, p.st, stV0)
            for b in range(NB):
                Plan.aw(e, p.pe, pe0 + 2 * b + 1)
                p.dve.inc(e.tensor_copy(smb[:, 512 * (b % 2):512 * (b % 2 + 1)],
                                        p.pb[b % 2][:, :]))
                Plan.aw(e, p.pe, pe0 + 2 * b + 2)
                Plan.aw(e, p.st, stV0 + 16 * max(4 * ot + b - 1, 0))
                p.dve.inc(e.tensor_copy(smb[:, 1024 + 512 * (b % 2):
                                            1024 + 512 * (b % 2 + 1)],
                                        p.pbt[:, 512 * (b % 2):
                                              512 * (b % 2 + 1)]))
        p.blk.vector(v_dve)
        assert p.dve.val == dve0 + 2 * NB

        def v_act(e, dve0=dve0, ot=ot):
            for b in range(NB):
                Plan.aw(e, p.dve, dve0 + 2 * b + 2)
                p.st.inc(e.dma_start(tm_store("vtok", ot, b),
                                     smb[:, 1024 + 512 * (b % 2):
                                         1024 + 512 * (b % 2 + 1)]), 16)
        p.blk.scalar(v_act)
        assert p.st.val == st0 + 16 * NB

    pe_c2_end = p.pe.val
    st_c2 = p.st.val
    if upto in ("attn_q", "attn_k", "attn_qf", "attn_kft", "attn_v"):
        dump_dram({"attn_q": "qspill", "attn_k": "kspill",
                   "attn_qf": "qfspill", "attn_kft": "kfspill",
                   "attn_v": "vtok"}[upto])
        return

    QH, KH, QFH, KFT, VH = 0, 2048, 4096, 6144, 8192
    MEMB, NORMB = 10240, 10368
    for h in range(16):
        ld0, pe0, dve0, act0, st0 = (p.ld.val, p.pe.val, p.dve.val,
                                     p.act.val, p.st.val)

        def h_sync(e, h=h, ld0=ld0, pe0=pe0):
            Plan.aw(e, p.ld, ld0)
            Plan.aw(e, p.pe, pe0)
            Plan.aw(e, p.st, st_c2)
            p.ld.inc(e.dma_start(arb2[:, QH:QH + 2048],
                                 dr["qspill"].ap()[128 * h:128 * (h + 1), :]), 16)
            p.ld.inc(e.dma_start(arb2[:, KH:KH + 2048],
                                 dr["kspill"].ap()[128 * h:128 * (h + 1), :]), 16)
            p.ld.inc(e.dma_start(arb2[:, QFH:QFH + 2048],
                                 dr["qfspill"].ap()[128 * h:128 * (h + 1), :]), 16)
            p.ld.inc(e.dma_start(arb2[:, KFT:KFT + 2048],
                                 tm_load("kfspill", h)), 16)
            p.ld.inc(e.dma_start(arb2[:, VH:VH + 2048],
                                 tm_load("vtok", h)), 16)
        p.blk.sync(h_sync)
        assert p.ld.val == ld0 + 80

        def h_pe(e, ld0=ld0, pe0=pe0, dve0=dve0, act0=act0):
            Plan.aw(e, p.ld, ld0 + 80)
            Plan.aw(e, p.dve, dve0)
            Plan.aw(e, p.act, act0)
            for c in range(DT):
                mm = e.matmul(p.pb[2][:, 0:128],
                              arb2[:, KFT + 128 * c:KFT + 128 * (c + 1)],
                              arb2[:, VH + 128 * c:VH + 128 * (c + 1)],
                              start=(c == 0), stop=(c == DT - 1))
            p.pe.inc(mm)
            for c in range(DT):
                mm = e.matmul(p.pb[3][:, 0:1],
                              arb2[:, KFT + 128 * c:KFT + 128 * (c + 1)],
                              cb[:, CB_ONES_COL:CB_ONES_COL + 1],
                              start=(c == 0), stop=(c == DT - 1))
            p.pe.inc(mm)
            for qb in range(4):
                dve_q0 = dve0 + 2 + 9 * qb
                if qb == 0:
                    Plan.aw(e, p.dve, dve0 + 2)
                else:
                    Plan.aw(e, p.dve, dve_q0 - 9 + 6)
                    Plan.aw(e, p.act, act0 + 16 * qb)
                p.pe.inc(e.matmul(p.pb[4][:, :], arb2[:, MEMB:MEMB + 128],
                                  arb2[:, QFH + 512 * qb:QFH + 512 * (qb + 1)],
                                  start=True, stop=True))
                p.pe.inc(e.matmul(p.pb[5][0:1, :], arb2[:, NORMB:NORMB + 1],
                                  arb2[:, QFH + 512 * qb:QFH + 512 * (qb + 1)],
                                  start=True, stop=True))
                for kc in range(16):
                    Plan.aw(e, p.act, act0 + 16 * qb + max(kc - 1, 0))
                    p.pe.inc(e.matmul(
                        p.pb[kc % 2][:, :],
                        arb2[:, KH + 128 * kc:KH + 128 * (kc + 1)],
                        arb2[:, QH + 512 * qb:QH + 512 * (qb + 1)],
                        start=True, stop=True))
                for kc in range(16):
                    Plan.aw(e, p.act, act0 + 16 * qb + kc + 1)
                    pt = arb1[:, 8192 * (qb % 2) + 512 * kc:
                              8192 * (qb % 2) + 512 * (kc + 1)]
                    mmv = e.matmul(p.pb[6][:, :],
                                   arb2[:, VH + 128 * kc:VH + 128 * (kc + 1)],
                                   pt, start=(kc == 0), stop=(kc == 15))
                    mmd = e.matmul(p.pb[3][0:1, :],
                                   cb[:, CB_ONES_COL:CB_ONES_COL + 1],
                                   pt, start=(kc == 0), stop=(kc == 15))
                p.pe.inc(mmv)
                p.pe.inc(mmd)
                Plan.aw(e, p.act, act0 + 16 * (qb + 1))
                Plan.aw(e, p.dve, dve_q0 + 1)
                p.pe.inc(e.matmul(p.pb[0][:, :],
                                  cf[0:1, CF_ONES_ROW:CF_ONES_ROW + 128],
                                  p.rows[0:1, 0:512], start=True, stop=True))
                Plan.aw(e, p.dve, dve_q0 + 2)
                p.pe.inc(e.matmul(p.pb[1][:, :],
                                  cf[0:1, CF_ONES_ROW:CF_ONES_ROW + 128],
                                  p.rows[0:1, 512:1024], start=True, stop=True))
        p.blk.tensor(h_pe)
        assert p.pe.val == pe0 + 90

        def h_act(e, pe0=pe0, dve0=dve0, h=h):
            for qb in range(4):
                pe_q0 = pe0 + 2 + 22 * qb
                for kc in range(16):
                    Plan.aw(e, p.pe, pe_q0 + 3 + kc)
                    p.act.inc(e.activation(
                        arb1[:, 8192 * (qb % 2) + 512 * kc:
                             8192 * (qb % 2) + 512 * (kc + 1)],
                        p.pb[kc % 2][:, :], AF.Exp, scale=SC))
                Plan.aw(e, p.dve, dve0 + 2 + 9 * qb + 9)
                p.st.inc(e.dma_start(
                    dr["combspill"].ap()[128 * h:128 * (h + 1),
                                         512 * qb:512 * (qb + 1)],
                    smb[:, 512 * (qb % 2):512 * (qb % 2 + 1)]), 16)
        p.blk.scalar(h_act)
        assert p.act.val == act0 + 64
        assert p.st.val == st0 + 64

        def h_dve(e, pe0=pe0, dve0=dve0, st0=st0, act0=act0, h=h):
            Plan.aw(e, p.act, act0)
            Plan.aw(e, p.st, st_c2)
            Plan.aw(e, p.pe, pe0 + 1)
            p.dve.inc(e.tensor_copy(arb2[:, MEMB:MEMB + 128], p.pb[2][:, 0:128]))
            Plan.aw(e, p.pe, pe0 + 2)
            p.dve.inc(e.tensor_copy(arb2[:, NORMB:NORMB + 1], p.pb[3][:, 0:1]))
            for qb in range(4):
                pe_q0 = pe0 + 2 + 22 * qb
                Plan.aw(e, p.pe, pe_q0 + 20)
                p.dve.inc(e.reciprocal(p.rows[0:1, 0:512], p.pb[3][0:1, :]))
                p.dve.inc(e.reciprocal(p.rows[0:1, 512:1024], p.pb[5][0:1, :]))
                Plan.aw(e, p.pe, pe_q0 + 21)
                p.dve.inc(e.tensor_copy(smf2[:, 0:512], p.pb[0][:, :]))
                Plan.aw(e, p.pe, pe_q0 + 22)
                p.dve.inc(e.tensor_copy(smf2[:, 512:1024], p.pb[1][:, :]))
                p.dve.inc(e.tensor_mul(smf[:, 0:512], p.pb[4][:, :],
                                       smf2[:, 512:1024]))
                p.dve.inc(e.tensor_mul(smf[:, 512:1024], p.pb[6][:, :],
                                       smf2[:, 0:512]))
                p.dve.inc(e.tensor_sub(smf2[:, 0:512], smf[:, 0:512],
                                       smf[:, 512:1024]))
                p.dve.inc(e.tensor_scalar(
                    smf2[:, 512:1024], smf2[:, 0:512],
                    cf[:, CF_GREP + h:CF_GREP + h + 1], None, ALU.mult))
                if qb >= 2:
                    Plan.aw(e, p.st, st0 + 16 * (qb - 1))
                p.dve.inc(e.tensor_add(smb[:, 512 * (qb % 2):512 * (qb % 2 + 1)],
                                       smf2[:, 512:1024], smf[:, 512:1024]))
        p.blk.vector(h_dve)
        assert p.dve.val == dve0 + 38

    st_c3 = p.st.val
    pe_c3 = p.pe.val
    if upto == "attn_comb":
        dump_dram("combspill")
        return

    cast0, ld0, pe0, dve0, act0, st0 = (p.cast.val, p.ld.val, p.pe.val,
                                        p.dve.val, p.act.val, p.st.val)

    woW = lambda dt_: (16384 + 2048 * dt_) if dt_ < 8 else 2048 * (dt_ - 8)

    def wo_cast(e):
        Plan.aw(e, p.cast, cast0)
        Plan.aw(e, p.pe, pe_c2_end)
        for dt_ in range(16):
            if dt_ == 8:
                Plan.aw(e, p.pe, pe0)
            p.cast.inc(e.dma_start(
                arb1[:, woW(dt_):woW(dt_) + 2048],
                dr["woT"].ap()[128 * dt_:128 * (dt_ + 1), :]), 16)
    p.blk.gpsimd(wo_cast)
    assert p.cast.val == cast0 + 256

    def wo_sync(e):
        Plan.aw(e, p.ld, ld0)
        Plan.aw(e, p.act, act0)
        for b in range(NB):
            if b == 0:
                Plan.aw(e, p.st, st_c3)
                Plan.aw(e, p.pe, pe0)
            elif b >= 2:
                Plan.aw(e, p.pe, pe0 + 16 * (b - 1))
            for c in range(16):
                p.ld.inc(e.dma_start(
                    arb2[:, 8192 * (b % 2) + 512 * c:
                         8192 * (b % 2) + 512 * (c + 1)],
                    dr["combspill"].ap()[128 * c:128 * (c + 1),
                                         512 * b:512 * (b + 1)]), 16)
            for dt_ in range(16):
                i = 16 * b + dt_
                if i >= 3:
                    Plan.aw(e, p.dve, dve0 + i - 2)
                p.ld.inc(e.dma_start(
                    xb[:, 512 * (i % 3):512 * (i % 3 + 1)],
                    src.ap()[128 * dt_:128 * (dt_ + 1),
                             512 * b:512 * (b + 1)]), 16)
    p.blk.sync(wo_sync)
    assert p.ld.val == ld0 + 16 * 32 * NB

    def wo_pe(e):
        for b in range(NB):
            Plan.dwait(e, p.ld, ld0, min(32 * b + 16, 32 * (b + 1)), 32 * NB)
            for dt_ in range(16):
                i = 16 * b + dt_
                Plan.dwait(e, p.cast, cast0, dt_ + 1, 16)
                if i >= 2:
                    Plan.aw(e, p.dve, dve0 + i - 1)
                else:
                    Plan.aw(e, p.dve, dve0)
                for c in range(16):
                    mm = e.matmul(
                        p.pb[i % 2][:, :],
                        arb1[:, woW(dt_) + 128 * c:woW(dt_) + 128 * (c + 1)],
                        arb2[:, 8192 * (b % 2) + 512 * c:
                             8192 * (b % 2) + 512 * (c + 1)],
                        start=(c == 0), stop=(c == 15))
                p.pe.inc(mm)
    p.blk.tensor(wo_pe)
    assert p.pe.val == pe0 + 16 * NB

    def wo_dve(e):
        for b in range(NB):
            for dt_ in range(16):
                i = 16 * b + dt_
                Plan.aw(e, p.pe, pe0 + i + 1)
                Plan.dwait(e, p.ld, ld0, min(32 * b + 16 + dt_ + 1,
                                             32 * (b + 1)), 32 * NB)
                if i >= 2:
                    Plan.aw(e, p.st, st0 + 16 * (i - 1))
                p.dve.inc(e.tensor_add(smf2[:, 512 * (i % 2):512 * (i % 2 + 1)],
                                       p.pb[i % 2][:, :],
                                       xb[:, 512 * (i % 3):512 * (i % 3 + 1)]))
    p.blk.vector(wo_dve)
    assert p.dve.val == dve0 + 16 * NB

    def wo_act(e):
        for b in range(NB):
            for dt_ in range(16):
                i = 16 * b + dt_
                Plan.aw(e, p.dve, dve0 + i + 1)
                p.st.inc(e.dma_start(
                    dst.ap()[128 * dt_:128 * (dt_ + 1),
                             512 * b:512 * (b + 1)],
                    smf2[:, 512 * (i % 2):512 * (i % 2 + 1)]), 16)
    p.blk.scalar(wo_act)
    assert p.st.val == st0 + 16 * 16 * NB
    p.fence()



def emit_dequant(p, dr):
    dqb = p.wbuf
    dqi = p.wbuf.bitcast(I32)
    sfb = p.wbuf.bitcast(F16)
    SF32 = p.wbuf.bitcast(F32)[:, 5888:5932]
    X = lambda W: dqb[:, 4096:4096 + W]
    B1 = lambda W: dqb[:, 5120:5120 + W]
    B2 = lambda W: dqb[:, 6144:6144 + W]
    B3 = lambda W: dqb[:, 7168:7168 + W]
    OUT = lambda W: dqb[:, 6144:6144 + W]
    ASM = lambda s, W: dqb[:, 8192:8192 + W]

    slabs = []
    for o in range(88):
        for q in range(2):
            slabs.append(("gu", o, q, 1024, 128, 8))
    for o in range(16):
        for q in range(11):
            slabs.append(("dn", o, q, 512, 64, 4))
    N = len(slabs)

    ld0, act0, dve0, pe0, st0 = (p.ld.val, p.act.val, p.dve.val, p.pe.val,
                                 p.st.val)
    pstep_i = dqi.ap().ap[0][0]
    pstep_b = dqb.ap().ap[0][0]

    def d_sync(e):
        p.fwait(e, "pe", "dve", "act")
        Plan.aw(e, p.ld, ld0)
        for s, (kind, o, q, W, nw, ng) in enumerate(slabs):
            Plan.aw(e, p.dve, dve0 + 4 * (s - 2) + 1)
            if kind == "gu":
                psrc = dr["gu_packed"].ap()[128 * o:128 * (o + 1),
                                            128 * q:128 * q + nw]
                ssrc = dr["gu_scales_t"].ap()[128 * o:128 * (o + 1), :]
                nsc = 16
            else:
                psrc = dr["dn_packed"].ap()[128 * o:128 * (o + 1),
                                            64 * q:64 * q + nw]
                ssrc = dr["dn_scales_t"].ap()[128 * o:128 * (o + 1), :]
                nsc = 44
            p.ld.inc(e.dma_start(dqi[:, 4608 + 128 * (s % 2):4608 + 128 * (s % 2) + nw], psrc), 16)
            Plan.aw(e, p.act, act0 + 3 * (s - 1) + 1)
            p.ld.inc(e.dma_start(sfb[:, 11864:11864 + nsc], ssrc), 16)
    p.blk.sync(d_sync)
    assert p.ld.val == ld0 + 32 * N, ("cnt", "ld", p.ld.val, ld0 + 32 * N)

    def d_dve(e):
        for s, (kind, o, q, W, nw, ng) in enumerate(slabs):
            Plan.dwait(e, p.ld, ld0, 2 * s + 3, 2 * N)
            Plan.aw(e, p.act, act0 + 3 * (s - 1) + 1)
            for k in range(8):
                last = e.tensor_scalar(dqi[:, 4864 + k * nw:4864 + (k + 1) * nw],
                                       dqi[:, 4608 + 128 * (s % 2):
                                            4608 + 128 * (s % 2) + nw],
                                       4 * k, 15, ALU.logical_shift_right,
                                       ALU.bitwise_and)
            p.dve.inc(last)
            Plan.aw(e, p.act, act0 + 3 * s + 1)
            e.tensor_scalar(B1(W), X(W), 8.0, None, ALU.is_ge)
            e.scalar_tensor_tensor(X(W), B1(W), -8.0, X(W), ALU.mult, ALU.add)
            e.tensor_scalar_mul(B1(W), B1(W), -1.0)
            p.dve.inc(e.tensor_scalar_add(B1(W), B1(W), 0.5))
            Plan.aw(e, p.act, act0 + 3 * s + 2)
            e.tensor_add(X(W), X(W), B2(W))
            e.tensor_add(X(W), X(W), B3(W))
            p.dve.inc(e.tensor_mul(X(W), X(W), B1(W)))
            Plan.aw(e, p.pe, pe0 + s)
            for g in range(ng):
                sidx = (8 * q + g) if kind == "gu" else (4 * q + g)
                last = e.tensor_scalar(OUT(W)[:, 128 * g:128 * (g + 1)],
                                       X(W)[:, 128 * g:128 * (g + 1)],
                                       SF32[:, sidx:sidx + 1], None, ALU.mult)
            p.dve.inc(last)
    p.blk.vector(d_dve)
    assert p.dve.val == dve0 + 4 * N, ("cnt", "dve", p.dve.val, dve0 + 4 * N)

    nibsrc = lambda nw: bass.AP(dqi.ap().tensor, dqi.ap().offset + 4864,
                                [[pstep_i, 128], [nw, 8], [1, nw]])
    xdst = lambda nw: bass.AP(dqb.ap().tensor, dqb.ap().offset + 4096,
                              [[pstep_b, 128], [1, 8], [8, nw]])

    def d_act(e):
        for s, (kind, o, q, W, nw, ng) in enumerate(slabs):
            nsc = 16 if kind == "gu" else 44
            Plan.dwait(e, p.ld, ld0, 2 * s + 3, 2 * N)
            Plan.aw(e, p.dve, dve0 + 4 * (s - 1) + 4)
            e.copy(SF32[:, 0:nsc], sfb[:, 11864:11864 + nsc])
            Plan.aw(e, p.dve, dve0 + 4 * s + 1)
            p.act.inc(e.copy(xdst(nw), nibsrc(nw)))
            Plan.aw(e, p.dve, dve0 + 4 * s + 2)
            Plan.aw(e, p.pe, pe0 + s)
            e.activation(B2(W), X(W), AF.Relu, bias=-4.0)
            p.act.inc(e.activation(B3(W), X(W), AF.Relu, scale=2.0,
                                   bias=-12.0))
            Plan.aw(e, p.pe, pe0 + s + 1)
            Plan.aw(e, p.st, st0 + 16 * s)
            p.act.inc(e.copy(ASM(s, W), p.pbt[:, 0:W]))
            if kind == "gu":
                dstp = dr["wgu_scr"].ap()[128 * o:128 * (o + 1),
                                          1024 * q:1024 * q + W]
            else:
                dstp = dr["wdn_scr"].ap()[128 * o:128 * (o + 1),
                                          512 * q:512 * q + W]
            p.st.inc(e.dma_start(dstp, ASM(s, W)), 16)
    p.blk.scalar(d_act)
    assert p.act.val == act0 + 3 * N, ("cnt", "act", p.act.val, act0 + 3 * N)
    assert p.st.val == st0 + 16 * N, ("cnt", "st", p.st.val, st0 + 16 * N)

    def d_pe(e):
        p.fwait(e, "dve", "act")
        for s, (kind, o, q, W, nw, ng) in enumerate(slabs):
            Plan.aw(e, p.dve, dve0 + 4 * s + 4)
            Plan.aw(e, p.act, act0 + 3 * (s - 1) + 3)
            for c in range(ng):
                mm = e.transpose(p.pbt[:, 128 * c:128 * (c + 1)],
                                 OUT(W)[:, 128 * c:128 * (c + 1)],
                                 p.cbuf[:, CB_IDN:CB_IDN + 128])
            p.pe.inc(mm)
    p.blk.tensor(d_pe)
    assert p.pe.val == pe0 + N, ("cnt", "pe", p.pe.val, pe0 + N)
    p.fence()



def emit_ffn_l1(p, dr, src, dst, dequant):
    arb1, arf, wbuf = p.arb1, p.arf, p.wbuf
    if dequant:
        emit_dequant(p, dr)

    emit_ln(p, src, arb1, 0)
    xn = lambda c, b: arb1[:, 2048 * c + 512 * b:2048 * c + 512 * (b + 1)]

    NP = 44
    ld0, pe0, act0, dve0, st0 = (p.ld.val, p.pe.val, p.act.val, p.dve.val,
                                 p.st.val)

    def da_sync(e):
        p.fwait(e, "pe", "dve")
        Plan.aw(e, p.ld, ld0)
        Plan.aw(e, p.st, st0)
        for pp in range(NP):
            Plan.aw(e, p.pe, pe0 + 4 * (pp - 1))
            w0 = 4096 * (pp % 2)
            p.ld.inc(e.dma_start(
                wbuf[:, w0:w0 + 2048],
                dr["wgu_scr"].ap()[128 * pp:128 * (pp + 1), :]), 16)
            p.ld.inc(e.dma_start(
                wbuf[:, w0 + 2048:w0 + 4096],
                dr["wgu_scr"].ap()[128 * (pp + NP):128 * (pp + NP + 1), :]), 16)
    p.blk.sync(da_sync)
    assert p.ld.val == ld0 + 32 * NP, ("cnt", "ld", p.ld.val, ld0 + 32 * NP)

    def da_pe(e):
        p.fwait(e, "dve", "act")
        for pp in range(NP):
            Plan.dwait(e, p.ld, ld0, 2 * pp + 4, 2 * NP)
            w0 = 4096 * (pp % 2)
            for b in range(NB):
                i = pp * NB + b
                Plan.aw(e, p.dve, dve0 + i - 1)
                pg = p.pb[2 * (i % 2)]
                pu = p.pb[2 * (i % 2) + 1]
                for c in range(DT):
                    e.matmul(pg[:, :], wbuf[:, w0 + 128 * c:w0 + 128 * (c + 1)],
                             xn(c, b), start=(c == 0), stop=(c == DT - 1))
                for c in range(DT):
                    mm = e.matmul(pu[:, :],
                                  wbuf[:, w0 + 2048 + 128 * c:w0 + 2048 + 128 * (c + 1)],
                                  xn(c, b), start=(c == 0), stop=(c == DT - 1))
                p.pe.inc(mm)
    p.blk.tensor(da_pe)
    assert p.pe.val == pe0 + 4 * NP, ("cnt", "pe", p.pe.val, pe0 + 4 * NP)

    def da_act(e):
        for i in range(4 * NP):
            Plan.aw(e, p.pe, pe0 + i + 1)
            Plan.aw(e, p.dve, dve0 + i - 1)
            sg = p.smf[:, 512 * (i % 2):512 * (i % 2 + 1)]
            p.act.inc(e.activation(sg, p.pb[2 * (i % 2)][:, :], AF.Silu))
            if i > 0:
                pp1, b1 = divmod(i - 1, NB)
                Plan.aw(e, p.dve, dve0 + i)
                p.st.inc(e.dma_start(
                    dr["hbuf"].ap()[128 * pp1:128 * (pp1 + 1),
                                    512 * b1:512 * (b1 + 1)],
                    p.smb[:, 512 * ((i - 1) % 4):512 * ((i - 1) % 4 + 1)]), 16)
        pp1, b1 = divmod(4 * NP - 1, NB)
        Plan.aw(e, p.dve, dve0 + 4 * NP)
        p.st.inc(e.dma_start(
            dr["hbuf"].ap()[128 * pp1:128 * (pp1 + 1), 512 * b1:512 * (b1 + 1)],
            p.smb[:, 512 * ((4 * NP - 1) % 4):512 * ((4 * NP - 1) % 4 + 1)]), 16)
    p.blk.scalar(da_act)
    assert p.act.val == act0 + 4 * NP, ("cnt", "act", p.act.val, act0 + 4 * NP)
    assert p.st.val == st0 + 16 * 4 * NP, ("cnt", "st", p.st.val, st0 + 16 * 4 * NP)

    def da_dve(e):
        for i in range(4 * NP):
            Plan.aw(e, p.act, act0 + i + 1)
            Plan.aw(e, p.st, st0 + 16 * (i - 3))
            sg = p.smf[:, 512 * (i % 2):512 * (i % 2 + 1)]
            hh = p.smb[:, 512 * (i % 4):512 * (i % 4 + 1)]
            p.dve.inc(e.tensor_mul(hh, p.pb[2 * (i % 2) + 1][:, :], sg))
    p.blk.vector(da_dve)
    assert p.dve.val == dve0 + 4 * NP, ("cnt", "dve", p.dve.val, dve0 + 4 * NP)
    p.fence()

    NI = 44
    ld0, pe0, act0, dve0, st0 = (p.ld.val, p.pe.val, p.act.val, p.dve.val,
                                 p.st.val)
    r_rms = p.rows[0:1, 0:512]
    rrep = p.smf[:, 0:512]

    PE_B, DVE_B, ACT_B = 33, 34, 17

    def db_sync(e):
        p.fwait(e, "pe", "dve")
        Plan.aw(e, p.ld, ld0)
        Plan.aw(e, p.st, st0)
        for b in range(NB):
            Plan.aw(e, p.pe, pe0 + PE_B * b - 1 if b > 0 else 0)
            for pp in range(NI):
                p.ld.inc(e.dma_start(
                    arb1[:, 512 * pp:512 * (pp + 1)],
                    dr["hbuf"].ap()[128 * pp:128 * (pp + 1),
                                    512 * b:512 * (b + 1)]), 16)
            for ot in range(DT):
                if ot >= 3:
                    Plan.aw(e, p.dve, dve0 + DVE_B * b + 18 + ot - 2)
                elif b > 0:
                    Plan.aw(e, p.dve, dve0 + DVE_B * b)
                p.ld.inc(e.dma_start(
                    p.xbuf[:, 512 * (ot % 3):512 * (ot % 3 + 1)],
                    src.ap()[128 * ot:128 * (ot + 1), BLK * b:BLK * (b + 1)]), 16)
    p.blk.sync(db_sync)
    assert p.ld.val == ld0 + 16 * 60 * NB, ("cnt", "ld", p.ld.val, ld0 + 16 * 60 * NB)

    cast0 = p.cast.val

    def db_cast(e):
        Plan.aw(e, p.cast, cast0)
        Plan.aw(e, p.pe, pe0)
        for b in range(NB):
            for ot in range(DT):
                i = b * DT + ot
                if i >= 2:
                    b2, ot2 = divmod(i - 2, DT)
                    Plan.aw(e, p.pe, pe0 + PE_B * b2 +
                            (1 if ot2 == 0 else 2 * ot2))
                p.cast.inc(e.dma_start(
                    wbuf[:, 5632 * (i % 2):5632 * (i % 2) + 5632],
                    dr["wdn_scr"].ap()[128 * ot:128 * (ot + 1), :]), 16)
    p.blk.gpsimd(db_cast)
    assert p.cast.val == cast0 + 16 * 4 * DT, ("cnt", "cast", p.cast.val, cast0 + 16 * 4 * DT)

    def db_pe(e):
        p.fwait(e, "dve", "act")
        for b in range(NB):
            Plan.dwait(e, p.ld, ld0, 60 * b + 44, 60 * NB)
            for ot in range(DT):
                i = b * DT + ot
                Plan.aw(e, p.cast, cast0 + 16 * min(i + 2, 4 * DT))
                Plan.aw(e, p.dve, dve0 + DVE_B * b + ot - 1)
                pbk = p.pb[i % 2]
                for c in range(NI):
                    mm = e.matmul(pbk[:, :],
                                  wbuf[:, 5632 * (i % 2) + 128 * c:
                                       5632 * (i % 2) + 128 * (c + 1)],
                                  arb1[:, 512 * c:512 * (c + 1)],
                                  start=(c == 0), stop=(c == NI - 1))
                p.pe.inc(mm)
                if ot > 0:
                    Plan.aw(e, p.act, act0 + ACT_B * b + ot)
                    p.pe.inc(e.matmul(p.pb[6][0:1, :],
                                      p.cbuf[:, CB_ONES_COL:CB_ONES_COL + 1],
                                      p.smb[:, 512 * ((ot - 1) % 2):
                                            512 * ((ot - 1) % 2 + 1)],
                                      start=(ot - 1 == 0), stop=False))
            Plan.aw(e, p.act, act0 + ACT_B * b + 16)
            p.pe.inc(e.matmul(p.pb[6][0:1, :],
                              p.cbuf[:, CB_ONES_COL:CB_ONES_COL + 1],
                              p.smb[:, 512 * (15 % 2):512 * (15 % 2 + 1)],
                              start=False, stop=True))
            Plan.aw(e, p.act, act0 + ACT_B * b + 17)
            p.pe.inc(e.matmul(p.pb[5][:, :],
                              p.cbuf_f[0:1, CF_ONES_ROW:CF_ONES_ROW + 128],
                              r_rms, start=True, stop=True))
    p.blk.tensor(db_pe)
    assert p.pe.val == pe0 + PE_B * NB, ("cnt", "pe", p.pe.val, pe0 + PE_B * NB)

    def db_dve(e):
        for b in range(NB):
            for ot in range(DT):
                i = b * DT + ot
                Plan.aw(e, p.pe, pe0 + PE_B * b + 2 * ot + 1 - (1 if ot > 0 else 0))
                p.dve.inc(e.tensor_copy(arf[:, 512 * ot:512 * (ot + 1)],
                                        p.pb[i % 2][:, :]))
            Plan.aw(e, p.pe, pe0 + PE_B * b + 32)
            e.tensor_scalar_mul(r_rms, p.pb[6][0:1, :], 1.0 / D)
            p.dve.inc(e.tensor_scalar_add(r_rms, r_rms, float(RMS_EPS_L1FFN)))
            Plan.aw(e, p.pe, pe0 + PE_B * b + 33)
            p.dve.inc(e.tensor_copy(rrep, p.pb[5][:, :]))
            for ot in range(DT):
                Plan.dwait(e, p.ld, ld0, min(60 * b + 44 + ot + 3, 60 * (b + 1)), 60 * NB)
                Plan.aw(e, p.st, st0 + 16 * (b * DT + ot - 1))
                t2 = p.smf[:, 512:1024]
                out = p.smf2[:, 512 * (ot % 2):512 * (ot % 2 + 1)]
                e.tensor_mul(t2, arf[:, 512 * ot:512 * (ot + 1)], rrep)
                xsrc = p.xbuf[:, 512 * (ot % 3):512 * (ot % 3 + 1)]
                p.dve.inc(e.scalar_tensor_tensor(out, t2, 0.5, xsrc,
                                                 ALU.mult, ALU.add))
    p.blk.vector(db_dve)
    assert p.dve.val == dve0 + DVE_B * NB, ("cnt", "dve", p.dve.val, dve0 + DVE_B * NB)

    def db_act(e):
        for b in range(NB):
            for ot in range(DT):
                Plan.aw(e, p.dve, dve0 + DVE_B * b + ot + 1)
                ysq = p.smb[:, 512 * (ot % 2):512 * (ot % 2 + 1)]
                if ot >= 2:
                    Plan.aw(e, p.pe, pe0 + PE_B * b + 2 * ot - 1)
                p.act.inc(e.activation(ysq, arf[:, 512 * ot:512 * (ot + 1)],
                                       AF.Square))
            Plan.aw(e, p.dve, dve0 + DVE_B * b + 17)
            e.activation(r_rms, r_rms, AF.Ln)
            p.act.inc(e.activation(r_rms, r_rms, AF.Exp, scale=-0.5))
            for ot in range(DT):
                Plan.aw(e, p.dve, dve0 + DVE_B * b + 18 + ot + 1)
                out = p.smf2[:, 512 * (ot % 2):512 * (ot % 2 + 1)]
                p.st.inc(e.dma_start(
                    dst.ap()[128 * ot:128 * (ot + 1), BLK * b:BLK * (b + 1)],
                    out), 16)
    p.blk.scalar(db_act)
    assert p.act.val == act0 + ACT_B * NB, ("cnt", "act", p.act.val, act0 + ACT_B * NB)
    assert p.st.val == st0 + 16 * DT * NB, ("cnt", "st", p.st.val, st0 + 16 * DT * NB)
    p.fence()


ALL_SUBLAYERS = ["ffn0a", "mamba", "ffn0b", "ffn1a", "attn", "ffn1b"]


def _run_chained(x, weights, groups):
    cur = [np.ascontiguousarray(x[c].T) for c in range(N_CORES)]
    for gi, g in enumerate(groups):
        nc = build_program(g)
        in_maps = []
        for c in range(N_CORES):
            m = {"x_in": cur[c]}
            m.update(weights)
            in_maps.append(m)
        res = run_bass_kernel_spmd(nc, in_maps, core_ids=list(range(N_CORES)))
        cur = [np.asarray(res.results[c]["out"]) for c in range(N_CORES)]
    return np.stack([c.T for c in cur])


def kernel(**inputs):
    x = np.asarray(inputs["L0_attn_x"], np.float32)
    weights = prep_weights(inputs)
    out = _run_chained(x, weights, [ALL_SUBLAYERS])
    return out.astype(np.float32)


# version 03 (final)
# speedup vs baseline: 1.1676x; 1.0243x over previous
import contextlib
import numpy as np
import ml_dtypes as md

import concourse.bass as bass
import concourse.mybir as mybir
from concourse.bass_utils import run_bass_kernel_spmd

F32 = mybir.dt.float32
BF16 = mybir.dt.bfloat16
I32 = mybir.dt.int32
F16 = mybir.dt.float16
AF = mybir.ActivationFunctionType
ALU = mybir.AluOpType
AX = mybir.AxisListType

D = 2048
T = 2048
BLK = 512
NB = T // BLK
DT = D // 128
N_CORES = 8

D_INNER = 4096
D_STATE = 16
D_HEAD = 256
PROJ_PAD = 8832
HID = 1024
LAT = 512
NE = 16
MOE_SCALE = 2.5
I_FF = 5632
GU = 2 * I_FF
NHEADS = 16
HDIM = 128

LN_EPS = 1e-5
RMS_EPS_MAMBA = 1.1920929e-07
RMS_EPS_L1FFN = 1e-6

CONSTV_VALUES = [-4.0, -12.0, 0.5, -6.0]



def _mmtile(w):
    out_f, in_f = w.shape
    assert out_f % 128 == 0 and in_f % 128 == 0, (out_f, in_f)
    ot, kc = out_f // 128, in_f // 128
    h = w.reshape(ot, 128, kc, 128).transpose(0, 3, 2, 1)
    return np.ascontiguousarray(h.reshape(ot * 128, in_f))


def _sbimg(w, part=128):
    r, c = w.shape
    k = r // part
    return np.ascontiguousarray(
        w.reshape(k, part, c).transpose(1, 0, 2).reshape(part, k * c))


def mkap(ap, newdims):
    return bass.AP(ap.tensor, ap.offset, newdims)


def _host_consts():
    f32 = np.float32
    c = {}
    c["idn_bf"] = np.eye(128, dtype=f32).astype(md.bfloat16)
    c["idn_f32"] = np.eye(128, dtype=f32)
    c["ones_col_bf"] = np.ones((128, 1), f32).astype(md.bfloat16)
    c["ones_row_bf"] = np.ones((1, 128), f32).astype(md.bfloat16)
    c["ones_row_f32"] = np.ones((1, 128), f32)
    pat = np.zeros((32, 128, 16), f32)
    for i in range(32):
        pat[i, :, i >> 1] = 1.0 / D_HEAD
    c["xmean_pat_bf"] = np.ascontiguousarray(
        pat.transpose(1, 0, 2).reshape(128, 512)).astype(md.bfloat16)
    rep = np.zeros((16, 256), f32)
    for h in range(16):
        rep[h, 16 * h:16 * h + 16] = 1.0
    c["hrep_pat"] = rep
    col = np.zeros((256, 16), f32)
    for h in range(16):
        col[16 * h:16 * h + 16, h] = 1.0
    c["hcol_pat"] = _sbimg(col)
    sel = np.zeros((16, 16, 128), f32)
    for e in range(16):
        sel[e, e, :] = 1.0
    c["sel16_bf"] = np.ascontiguousarray(
        sel.transpose(1, 0, 2).reshape(16, 2048)).astype(md.bfloat16)
    inv = 1.0 / (10000.0 ** (np.arange(0, D, 2, dtype=np.float64) / D))
    tpos = np.arange(T, dtype=np.float64)
    emb = np.concatenate([np.outer(inv, tpos)] * 2, axis=0)
    c["rope_cos"] = np.cos(emb).astype(f32)
    c["rope_sin"] = np.sin(emb).astype(f32)
    return c


def prep_weights(i):
    w = {}
    f32 = np.float32
    W_in = np.asarray(i["L0_attn_W_in"], f32)
    pad = np.zeros((112, D), f32)
    W_in_pad = np.concatenate([W_in[0:4096], W_in[4096:8192], W_in[8192:8208],
                               pad, W_in[8208:8464], W_in[8464:8720]], 0)
    w["w_inT"] = _mmtile(W_in_pad)
    w["w_outT"] = _mmtile(np.asarray(i["L0_attn_W_out"], f32))
    w["w_h2iT"] = np.ascontiguousarray(np.asarray(i["L0_attn_W_h2i"], f32).T)
    w["w_norm_cols"] = np.ascontiguousarray(
        np.asarray(i["L0_attn_w_norm"], f32).reshape(DT, 128).T)
    w["w_conv_cols"] = np.ascontiguousarray(
        np.asarray(i["L0_attn_w_conv"], f32)[:, 0, :].reshape(32, 128, 4)
        .transpose(1, 0, 2).reshape(128, 128))
    w["dt_bias"] = np.asarray(i["L0_attn_dt_bias"], f32).reshape(16, 1)
    w["a_log"] = np.asarray(i["L0_attn_A_log"], f32).reshape(16, 1)

    sw2t = _mmtile(np.asarray(i["L0_ffn_sw2"], f32))
    wupt = _mmtile(np.asarray(i["L0_ffn_Wup"], f32))
    w["comb_img"] = np.ascontiguousarray(np.concatenate([sw2t, wupt], 1))
    w["sw1T"] = _mmtile(np.asarray(i["L0_ffn_sw1"], f32))
    w["swgT"] = _mmtile(np.asarray(i["L0_ffn_swg"], f32))
    w["wdownT"] = _mmtile(np.asarray(i["L0_ffn_Wdown"], f32))
    w["wg_sb"] = _sbimg(np.ascontiguousarray(np.asarray(i["L0_ffn_Wg"], f32).T))
    rw1 = np.asarray(i["L0_ffn_rw1"], f32)
    rwg = np.asarray(i["L0_ffn_rwg"], f32)
    rw2 = np.asarray(i["L0_ffn_rw2"], f32)
    imgs = []
    for e in range(NE):
        imgs.append(np.concatenate([_sbimg(_mmtile(rw1[e])),
                                    _sbimg(_mmtile(rwg[e])),
                                    _sbimg(_mmtile(rw2[e]))], 1))
    w["rw_img"] = np.ascontiguousarray(np.stack(imgs).reshape(NE * 128, 12288))
    w["moe_bias"] = np.asarray(i["L0_ffn_biases"], f32).reshape(1, 16)

    for nm, key in (("wqT", "L1_attn_Wq"), ("wkT", "L1_attn_Wk"),
                    ("wvT", "L1_attn_Wv"), ("woT", "L1_attn_Wo")):
        w[nm] = _mmtile(np.asarray(i[key], f32))
    w["attn_gate"] = np.asarray(i["L1_attn_gate"], f32).reshape(1, 16)

    w["gu_packed"] = np.ascontiguousarray(np.asarray(i["L1_ffn_gate_up_packed"]))
    w["gu_scales_t"] = np.ascontiguousarray(
        np.asarray(i["L1_ffn_gate_up_scales"]).T)
    w["dn_packed"] = np.ascontiguousarray(np.asarray(i["L1_ffn_down_packed"]))
    w["dn_scales_t"] = np.ascontiguousarray(
        np.asarray(i["L1_ffn_down_scales"]).T)
    w.update(_host_consts())
    cv = np.zeros((128, 16), np.float32)
    cv[:, :len(CONSTV_VALUES)] = np.asarray(CONSTV_VALUES, np.float32)[None, :]
    w["constv"] = cv
    return w


WEIGHT_SPECS = {
    "w_inT": ((PROJ_PAD, D), F32), "w_outT": ((D, D_INNER), F32),
    "w_h2iT": ((16, D_INNER), F32), "w_norm_cols": ((128, 16), F32),
    "w_conv_cols": ((128, 128), F32), "dt_bias": ((16, 1), F32),
    "a_log": ((16, 1), F32),
    "comb_img": ((D, 1536), F32), "sw1T": ((HID, D), F32),
    "swgT": ((HID, D), F32), "wdownT": ((LAT, D), F32),
    "wg_sb": ((128, 256), F32), "rw_img": ((NE * 128, 12288), F32),
    "moe_bias": ((1, 16), F32),
    "wqT": ((D, D), F32), "wkT": ((D, D), F32), "wvT": ((D, D), F32),
    "woT": ((D, D), F32), "attn_gate": ((1, 16), F32),
    "gu_packed": ((GU, 256), I32), "gu_scales_t": ((GU, 16), F16),
    "dn_packed": ((D, 704), I32), "dn_scales_t": ((D, 44), F16),
    "idn_bf": ((128, 128), BF16), "idn_f32": ((128, 128), F32),
    "ones_col_bf": ((128, 1), BF16), "ones_row_bf": ((1, 128), BF16),
    "ones_row_f32": ((1, 128), F32), "xmean_pat_bf": ((128, 512), BF16),
    "hrep_pat": ((16, 256), F32), "hcol_pat": ((128, 32), F32),
    "sel16_bf": ((16, 2048), BF16),
    "rope_cos": ((D, T), F32), "rope_sin": ((D, T), F32),
    "constv": ((128, 16), F32),
}

SCRATCH_SPECS = {
    "resid": ((D, T), F32),
    "zbuf": ((D_INNER, T), F32),
    "r1buf": ((D, T), F32),
    "vbuf": ((D, T), F32),
    "dtbc": ((5 * 128, T), F32),
    "wgu_scr": ((GU, D), BF16),
    "wdn_scr": ((D, I_FF), BF16),
    "hbuf": ((I_FF, T), BF16),
    "qspill": ((D, T), BF16),
    "kspill": ((D, T), BF16),
    "qfspill": ((D, T), BF16),
    "kfspill": ((D, T), BF16),
    "vtok": ((T, D), BF16),
    "combspill": ((D, T), BF16),
}



class Sem:
    __slots__ = ("h", "val", "name")

    def __init__(self, handle, name):
        self.h = handle
        self.val = 0
        self.name = name

    def inc(self, inst, n=1):
        inst.then_inc(self.h, n)
        self.val += n
        return self.val


ENG_SEM = {"sync": "ld", "gpsimd": "cast", "tensor": "pe", "scalar": "act",
           "vector": "dve"}


class Plan:
    def __init__(self, nc, ctx):
        self.nc = nc
        self.ctx = ctx
        self.blk = ctx.enter_context(nc.Block())
        for nm in ["ld", "cast", "st", "pe", "dve", "act", "gp"]:
            setattr(self, nm, Sem(ctx.enter_context(nc.semaphore(nm)), nm))
        self.pb = [ctx.enter_context(nc.psum_tensor(f"pb{k}", [128, BLK], F32))
                   for k in range(7)]
        self.pbt = ctx.enter_context(nc.psum_tensor("pbt", [128, 1024], BF16))
        self.arb1 = ctx.enter_context(nc.sbuf_tensor("arb1", [128, 32768], BF16))
        self.arb2 = ctx.enter_context(nc.sbuf_tensor("arb2", [128, 16384], BF16))
        self.arf = ctx.enter_context(nc.sbuf_tensor("arf", [128, 12288], F32))
        self.wbuf = ctx.enter_context(nc.sbuf_tensor("wbuf", [128, 12288], BF16))
        self.xbuf = ctx.enter_context(nc.sbuf_tensor("xbuf", [128, 1536], F32))
        self.smf = ctx.enter_context(nc.sbuf_tensor("smf", [128, 1024], F32))
        self.smf2 = ctx.enter_context(nc.sbuf_tensor("smf2", [128, 1024], F32))
        self.smb = ctx.enter_context(nc.sbuf_tensor("smb", [128, 2048], BF16))
        self.cbuf = ctx.enter_context(nc.sbuf_tensor("cbuf", [128, 784], BF16))
        self.cbuf_f = ctx.enter_context(nc.sbuf_tensor("cbuf_f", [128, 896], F32))
        self.cbuf2 = ctx.enter_context(nc.sbuf_tensor("cbuf2", [16, 2048], BF16))
        self.rows = self.arf[0:1, 10496:12288]
        self.fence_v = {}

    def fence(self):
        for nm in ["ld", "cast", "st", "pe", "dve", "act", "gp"]:
            self.fence_v[nm] = getattr(self, nm).val

    def fwait(self, e, *names):
        for nm in names:
            v = self.fence_v.get(nm, 0)
            if v > 0:
                e.wait_ge(getattr(self, nm).h, v)

    @staticmethod
    def aw(e, sem, val):
        if val > 0:
            e.wait_ge(sem.h, val)

    @staticmethod
    def dwait(e, sem, base, k, total):
        Plan.aw(e, sem, base + 16 * min(k, total))

    def chain(self, steps, first_wait=("pe", "dve", "act")):
        prev = None
        for k, (engname, fn) in enumerate(steps):
            sem = getattr(self, ENG_SEM[engname])

            def body(e, fn=fn, prev=prev, k=k, sem=sem):
                if k == 0:
                    self.fwait(e, *first_wait)
                if prev is not None:
                    Plan.aw(e, prev[0], prev[1])
                last = fn(e)
                if isinstance(last, (list, tuple)):
                    last = [x for x in last if x is not None][-1]
                sem.inc(last)
            getattr(self.blk, engname)(body)
            prev = (sem, sem.val)
        self.fence()


def selfstep(p, e, sem, inst):
    sem.inc(inst)
    e.wait_ge(sem.h, sem.val)
    return inst


def emit_dump(p, dst, region, rows=128, tiles=DT, tile_cols=2048):
    gp0 = p.gp.val

    def body(e):
        p.fwait(e, "pe", "dve", "act")
        Plan.aw(e, p.st, p.st.val)
        for t in range(tiles):
            p.gp.inc(e.dma_start(
                dst.ap()[128 * t:128 * t + rows, 0:tile_cols], region(t)), 16)
        e.wait_ge(p.gp.h, gp0 + 16 * tiles)
    p.blk.gpsimd(body)
    p.fence()


def make_plan(nc, ctx):
    p = Plan(nc, ctx)
    for j, v in enumerate(CONSTV_VALUES):
        nc.const_aps.register(F32, v, p.cbuf_f[:, CF_CONSTV + j:CF_CONSTV + j + 1])
    return p


CB_IDN = 0
CB_ONES_COL = 128
CB_XMEAN = 136
CB_ONES_ROW = 648
CF_IDN = 0
CF_ONES_ROW = 128
CF_HREP = 256
CF_HCOL = 512
CF_WNORM = 544
CF_WCONV = 560
CF_DTBIAS = 688
CF_ALOG = 689
CF_MOEB = 690
CF_GATE = 706
CF_GREP = 722
CF_MOEB_REP = 738
CF_NALOGE = 754
CF_CONSTV = 770


def emit_load_consts(p, dr):
    cb, cf = p.cbuf, p.cbuf_f

    def body(e):
        pairs = [
            (cb[:, CB_IDN:CB_IDN + 128], dr["idn_bf"].ap()),
            (cb[:, CB_ONES_COL:CB_ONES_COL + 1], dr["ones_col_bf"].ap()),
            (cb[0:1, CB_ONES_ROW:CB_ONES_ROW + 128], dr["ones_row_bf"].ap()),
            (cb[:, CB_XMEAN:CB_XMEAN + 512], dr["xmean_pat_bf"].ap()),
            (cf[:, CF_IDN:CF_IDN + 128], dr["idn_f32"].ap()),
            (cf[0:1, CF_ONES_ROW:CF_ONES_ROW + 128], dr["ones_row_f32"].ap()),
            (cf[0:16, CF_HREP:CF_HREP + 256], dr["hrep_pat"].ap()),
            (cf[:, CF_HCOL:CF_HCOL + 32], dr["hcol_pat"].ap()),
            (cf[:, CF_WNORM:CF_WNORM + 16], dr["w_norm_cols"].ap()),
            (cf[:, CF_WCONV:CF_WCONV + 128], dr["w_conv_cols"].ap()),
            (cf[0:16, CF_DTBIAS:CF_DTBIAS + 1], dr["dt_bias"].ap()),
            (cf[0:16, CF_ALOG:CF_ALOG + 1], dr["a_log"].ap()),
            (cf[0:1, CF_MOEB:CF_MOEB + 16], dr["moe_bias"].ap()),
            (cf[0:1, CF_GATE:CF_GATE + 16], dr["attn_gate"].ap()),
            (p.cbuf2[:, :], dr["sel16_bf"].ap()),
            (cf[:, CF_CONSTV:CF_CONSTV + 16], dr["constv"].ap()),
        ]
        for dst, src in pairs:
            p.ld.inc(e.dma_start(dst, src), 16)
    p.blk.sync(body)

    p.fence()


def emit_prep_small(p):
    cf = p.cbuf_f
    p.chain([
        ("scalar", lambda e: [
            p.fwait(e, "ld", "cast"),
            e.activation(cf[0:1, CF_GATE:CF_GATE + 16],
                         cf[0:1, CF_GATE:CF_GATE + 16], AF.Sigmoid),
            e.activation(cf[0:16, CF_NALOGE:CF_NALOGE + 1],
                         cf[0:16, CF_ALOG:CF_ALOG + 1], AF.Exp)][-1]),
        ("tensor", lambda e: [
            e.matmul(p.pb[6][:, 0:16], cf[0:1, CF_ONES_ROW:CF_ONES_ROW + 128],
                     cf[0:1, CF_GATE:CF_GATE + 16], start=True, stop=True),
            e.matmul(p.pb[6][:, 16:32], cf[0:1, CF_ONES_ROW:CF_ONES_ROW + 128],
                     cf[0:1, CF_MOEB:CF_MOEB + 16], start=True, stop=True)][-1]),
        ("vector", lambda e: [
            e.tensor_copy(cf[:, CF_GREP:CF_GREP + 16], p.pb[6][:, 0:16]),
            e.tensor_copy(cf[:, CF_MOEB_REP:CF_MOEB_REP + 16], p.pb[6][:, 16:32]),
            e.tensor_scalar_mul(cf[0:16, CF_NALOGE:CF_NALOGE + 1],
                                cf[0:16, CF_NALOGE:CF_NALOGE + 1], -1.0)][-1]),
    ], first_wait=("ld", "cast"))



def emit_ln(p, x_src, arena, col0, router=None):
    xb, srh = p.xbuf, p.smb
    mrep = p.smf[:, 0:512]
    rrep = p.smf[:, 512:1024]
    r_mean = p.rows[0:1, 0:512]
    r_rstd = p.rows[0:1, 512:1024]
    r_sq = p.rows[0:1, 1024:1536]
    r_tmp = p.smf2[0:1, 0:512]

    for b in range(NB):
        ld0, act0, pe0, st0 = p.ld.val, p.act.val, p.pe.val, p.st.val

        def sync1(e, ld0=ld0, act0=act0, b=b, st0=st0):
            p.fwait(e, "dve", "act")
            Plan.aw(e, p.ld, ld0)
            Plan.aw(e, p.st, st0)
            for t in range(DT):
                Plan.aw(e, p.act, act0 + t - 2)
                p.ld.inc(e.dma_start(
                    xb[:, 512 * (t % 3):512 * (t % 3 + 1)],
                    x_src.ap()[128 * t:128 * (t + 1), BLK * b:BLK * (b + 1)]), 16)
        p.blk.sync(sync1)
        assert p.ld.val == ld0 + 16 * DT, ("cnt", "ld", p.ld.val, ld0 + 16 * DT)

        def act1(e, ld0=ld0, pe0=pe0, b=b):
            p.fwait(e, "dve", "pe")
            for t in range(DT):
                Plan.dwait(e, p.ld, ld0, t + 3, DT)
                Plan.aw(e, p.pe, pe0 + t - 1)
                src = xb[:, 512 * (t % 3):512 * (t % 3 + 1)]
                dst = srh[:, 1024 * (t % 2):1024 * (t % 2 + 1)]
                e.copy(dst[:, 0:512], src)
                p.act.inc(e.activation(dst[:, 512:1024], src, AF.Square))
        p.blk.scalar(act1)
        assert p.act.val == act0 + DT, ("cnt", "act", p.act.val, act0 + DT)

        def pe1(e, act0=act0, b=b):
            p.fwait(e, "dve")
            for t in range(DT):
                Plan.aw(e, p.act, act0 + t + 1)
                dst = srh[:, 1024 * (t % 2):1024 * (t % 2 + 1)]
                first, last = (t == 0), (t == DT - 1)
                e.matmul(p.pb[4][0:1, :], p.cbuf[:, CB_ONES_COL:CB_ONES_COL + 1],
                         dst[:, 0:512], start=first, stop=last)
                p.pe.inc(e.matmul(p.pb[5][0:1, :],
                                  p.cbuf[:, CB_ONES_COL:CB_ONES_COL + 1],
                                  dst[:, 512:1024], start=first, stop=last))
        p.blk.tensor(pe1)
        assert p.pe.val == pe0 + DT, ("cnt", "pe", p.pe.val, pe0 + DT)
        p.fence()

        p.chain([
            ("vector", lambda e: [
                selfstep(p, e, p.dve,
                         e.tensor_scalar_mul(r_mean, p.pb[4][0:1, :], 1.0 / D)),
                selfstep(p, e, p.dve,
                         e.tensor_scalar_mul(r_sq, p.pb[5][0:1, :], 1.0 / D)),
                selfstep(p, e, p.dve,
                         e.tensor_tensor(r_tmp, r_mean, r_mean, ALU.mult)),
                selfstep(p, e, p.dve, e.tensor_sub(r_rstd, r_sq, r_tmp)),
                e.tensor_scalar_add(r_rstd, r_rstd, float(LN_EPS))][-1]),
            ("scalar", lambda e: [
                selfstep(p, e, p.act, e.activation(r_rstd, r_rstd, AF.Ln)),
                e.activation(r_rstd, r_rstd, AF.Exp, scale=-0.5)][-1]),
            ("tensor", lambda e: [
                e.matmul(p.pb[4][:, :], p.cbuf_f[0:1, CF_ONES_ROW:CF_ONES_ROW + 128],
                         r_mean, start=True, stop=True),
                e.matmul(p.pb[5][:, :], p.cbuf_f[0:1, CF_ONES_ROW:CF_ONES_ROW + 128],
                         r_rstd, start=True, stop=True)][-1]),
            ("vector", lambda e: [
                e.tensor_copy(mrep, p.pb[4][:, :]),
                e.tensor_copy(rrep, p.pb[5][:, :])][-1]),
        ])

        ld2, dve2 = p.ld.val, p.dve.val

        def sync2(e, ld2=ld2, dve2=dve2, b=b, st0=st0):
            Plan.aw(e, p.ld, ld2)
            Plan.aw(e, p.act, p.act.val)
            Plan.aw(e, p.st, st0)
            for t in range(DT):
                Plan.aw(e, p.dve, dve2 + t - 2)
                p.ld.inc(e.dma_start(
                    xb[:, 512 * (t % 3):512 * (t % 3 + 1)],
                    x_src.ap()[128 * t:128 * (t + 1), BLK * b:BLK * (b + 1)]), 16)
        p.blk.sync(sync2)
        assert p.ld.val == ld2 + 16 * DT, ("cnt", "ld", p.ld.val, ld2 + 16 * DT)

        per0 = p.pe.val

        def dve3(e, ld2=ld2, b=b, per0=per0):
            for t in range(DT):
                Plan.dwait(e, p.ld, ld2, t + 3, DT)
                src = xb[:, 512 * (t % 3):512 * (t % 3 + 1)]
                dst = arena[:, col0 + 2048 * t + 512 * b:
                            col0 + 2048 * t + 512 * (b + 1)]
                if router is None:
                    tmp = p.smf2[:, 0:512]
                    e.tensor_sub(tmp, src, mrep)
                    p.dve.inc(e.tensor_mul(dst, tmp, rrep))
                else:
                    xf = p.smf2[:, 512 * (t % 2):512 * (t % 2 + 1)]
                    if t >= 2:
                        Plan.aw(e, p.pe, per0 + t - 1)
                    e.tensor_sub(xf, src, mrep)
                    e.tensor_mul(xf, xf, rrep)
                    p.dve.inc(e.tensor_copy(dst, xf))
        p.blk.vector(dve3)
        assert p.dve.val == dve2 + DT, ("cnt", "dve", p.dve.val, dve2 + DT)

        if router is not None:
            dve3_0 = dve2

            def pe_rt(e, b=b, dve3_0=dve3_0):
                if b > 0:
                    Plan.aw(e, p.act, p.act.val)
                for t in range(DT):
                    Plan.aw(e, p.dve, dve3_0 + t + 1)
                    xf = p.smf2[:, 512 * (t % 2):512 * (t % 2 + 1)]
                    p.pe.inc(e.matmul(p.pb[6][0:16, :],
                                      router["wg"][:, 16 * t:16 * (t + 1)],
                                      xf, start=(t == 0), stop=(t == DT - 1)))
            p.blk.tensor(pe_rt)
            rt_pe_end = p.pe.val

            def act_rt(e, b=b, rt_pe_end=rt_pe_end):
                Plan.aw(e, p.pe, rt_pe_end)
                p.act.inc(e.activation(router["gl"][:, 512 * b:512 * (b + 1)],
                                       p.pb[6][0:16, :], AF.Sigmoid))
            p.blk.scalar(act_rt)
        p.fence()



def emit_ffn_l0(p, dr, src, dst, upto=None):
    cb, cf = p.cbuf, p.cbuf_f
    arb1, arb2, arf, wbuf = p.arb1, p.arb2, p.arf, p.wbuf
    xn = lambda c, b: arb1[:, 2048 * c + 512 * b:2048 * c + 512 * (b + 1)]

    gl = arf[0:16, 8192:10240]
    wgf = arf[:, 10240:10496]
    wn_tok = p.smf2[:, 0:256]
    wT16 = p.smb[0:16, 0:2048]

    ldw0 = p.ld.val

    def wg_ld(e):
        p.fwait(e, "pe", "dve")
        p.ld.inc(e.dma_start(wgf, dr["wg_sb"].ap()), 16)
    p.blk.sync(wg_ld)

    def wg_gate(e):
        Plan.aw(e, p.ld, ldw0 + 16)
        p.pe.inc(e.nop())
    p.blk.tensor(wg_gate)
    p.fence()

    emit_ln(p, src, arb1, 0, router={"wg": wgf, "gl": gl})
    if upto == "ln":
        emit_dump(p, dst, lambda t: arb1[:, 2048 * t:2048 * (t + 1)])
        return
    if upto == "router_gl":
        emit_dump(p, dst, lambda t: gl[0:16, 0:2048], rows=16, tiles=1)
        return

    def tr_fwd(e):
        for c in range(16):
            mm = e.transpose(p.pb[4][:, 16 * c:16 * (c + 1)],
                             gl[:, 128 * c:128 * (c + 1)],
                             cf[0:16, CF_IDN:CF_IDN + 16])
        return mm

    def rank(e):
        dn0 = p.dve.val

        def step(inst):
            p.dve.inc(inst)
            e.wait_ge(p.dve.h, p.dve.val)

        pbi = p.smf[:, 0:16]
        cnt = p.smf[:, 16:32]
        msk = p.smf[:, 32:48]
        wv = p.smf[:, 48:64]
        ws = p.smf[:, 64:66]
        scrap = p.smf[:, 256:512]
        for c in range(16):
            ptk = p.pb[4][:, 16 * c:16 * (c + 1)]
            step(e.tensor_add(pbi, ptk, cf[:, CF_MOEB_REP:CF_MOEB_REP + 16]))
            pstep = pbi.ap[0][0]
            gt0 = mkap(pbi, [[pstep, 128], [1, 16], [0, 16]])
            gt1 = mkap(pbi, [[pstep, 128], [0, 16], [1, 16]])
            step(e.tensor_tensor(scrap.rearrange("p (a b) -> p a b", b=16),
                                 gt0, gt1, ALU.is_gt))
            step(e.reduce_sum(cnt, scrap.rearrange("p (a b) -> p b a", b=16),
                              axis=AX.X))
            step(e.tensor_scalar(msk, cnt, 7.5, None, ALU.is_le))
            step(e.tensor_mul(wv, ptk, msk))
            step(e.reduce_sum(ws[:, 0:1], wv, axis=AX.X))
            step(e.reciprocal(ws[:, 1:2], ws[:, 0:1]))
            wib = mkap(ws[:, 1:2], [[ws.ap[0][0], 128], [0, 16]])
            last = e.scalar_tensor_tensor(wn_tok[:, 16 * c:16 * (c + 1)],
                                          wv, MOE_SCALE, wib,
                                          ALU.mult, ALU.mult)
            if c < 15:
                step(last)
        return last

    TRB = [4, 5, 6, 0]

    def tr_bwd(e):
        for g in range(4):
            for j in range(4):
                c = 4 * g + j
                mm = e.transpose(p.pb[TRB[g]][0:16, 128 * j:128 * (j + 1)],
                                 wn_tok[:, 16 * c:16 * (c + 1)],
                                 cf[:, CF_IDN:CF_IDN + 128])
        return mm

    def tr_ev(e):
        for g in range(4):
            last = e.copy(wT16[:, 512 * g:512 * (g + 1)],
                          p.pb[TRB[g]][0:16, 0:512])
        return last

    p.chain([("tensor", tr_fwd), ("vector", rank), ("tensor", tr_bwd),
             ("scalar", tr_ev)])
    if upto == "router":
        emit_dump(p, dst, lambda t: wT16[0:16, 0:2048], rows=16, tiles=1)
        return

    cast0, pe0, act0, dve0 = p.cast.val, p.pe.val, p.act.val, p.dve.val

    def sh_cast(e):
        Plan.aw(e, p.cast, cast0)
        Plan.aw(e, p.pe, pe0)
        for ht in range(8):
            Plan.aw(e, p.pe, pe0 + 4 * (ht - 1))
            p.cast.inc(e.dma_start(wbuf[:, 2048 * (ht % 2):2048 * (ht % 2 + 1)],
                                   dr["sw1T"].ap()[128 * ht:128 * (ht + 1), :]), 16)
            p.cast.inc(e.dma_start(
                wbuf[:, 4096 + 2048 * (ht % 2):4096 + 2048 * (ht % 2 + 1)],
                dr["swgT"].ap()[128 * ht:128 * (ht + 1), :]), 16)
    p.blk.gpsimd(sh_cast)

    def sh_pe(e):
        p.fwait(e, "act", "dve")
        for ht in range(8):
            Plan.aw(e, p.cast, cast0 + 32 * min(ht + 2, 8))
            for b in range(NB):
                i = ht * NB + b
                Plan.aw(e, p.dve, dve0 + max(i - 1, 0) if i >= 2 else 0)
                pa = p.pb[2 * (i % 2)]
                pg = p.pb[2 * (i % 2) + 1]
                for c in range(DT):
                    e.matmul(pa[:, :], wbuf[:, 2048 * (ht % 2) + 128 * c:
                                            2048 * (ht % 2) + 128 * (c + 1)],
                             xn(c, b), start=(c == 0), stop=(c == DT - 1))
                for c in range(DT):
                    mm = e.matmul(pg[:, :], wbuf[:, 4096 + 2048 * (ht % 2) + 128 * c:
                                                 4096 + 2048 * (ht % 2) + 128 * (c + 1)],
                                  xn(c, b), start=(c == 0), stop=(c == DT - 1))
                p.pe.inc(mm)
    p.blk.tensor(sh_pe)

    def sh_act(e):
        for i in range(32):
            Plan.aw(e, p.pe, pe0 + i + 1)
            if i >= 2:
                Plan.aw(e, p.dve, dve0 + i - 1)
            pg = p.pb[2 * (i % 2) + 1]
            tmp = p.smf[:, 512 * (i % 2):512 * (i % 2 + 1)]
            e.activation(tmp, pg[:, :], AF.Relu)
            p.act.inc(e.activation(tmp, tmp, AF.Square))
    p.blk.scalar(sh_act)

    def sh_dve(e):
        for i in range(32):
            ht, b = divmod(i, NB)
            Plan.aw(e, p.act, act0 + i + 1)
            pa = p.pb[2 * (i % 2)]
            tmp = p.smf[:, 512 * (i % 2):512 * (i % 2 + 1)]
            dsth = arb2[:, 2048 * ht + 512 * b:2048 * ht + 512 * (b + 1)]
            p.dve.inc(e.tensor_mul(dsth, pa[:, :], tmp))
    p.blk.vector(sh_dve)
    p.fence()
    if upto == "shared":
        emit_dump(p, dst, lambda t: arb2[:, 2048 * t:2048 * (t + 1)], tiles=8)
        return

    cast0, pe0, dve0 = p.cast.val, p.pe.val, p.dve.val

    def xl_cast(e):
        Plan.aw(e, p.cast, cast0)
        Plan.aw(e, p.pe, pe0)
        for lt in range(4):
            Plan.aw(e, p.pe, pe0 + 4 * (lt - 1))
            p.cast.inc(e.dma_start(wbuf[:, 2048 * (lt % 2):2048 * (lt % 2 + 1)],
                                   dr["wdownT"].ap()[128 * lt:128 * (lt + 1), :]), 16)
    p.blk.gpsimd(xl_cast)

    def xl_pe(e):
        p.fwait(e, "dve", "act")
        for lt in range(4):
            Plan.aw(e, p.cast, cast0 + 16 * min(lt + 2, 4))
            for b in range(NB):
                i = lt * NB + b
                if i >= 2:
                    Plan.aw(e, p.dve, dve0 + i - 1)
                pbk = p.pb[i % 2]
                for c in range(DT):
                    mm = e.matmul(pbk[:, :], wbuf[:, 2048 * (lt % 2) + 128 * c:
                                                  2048 * (lt % 2) + 128 * (c + 1)],
                                  xn(c, b), start=(c == 0), stop=(c == DT - 1))
                p.pe.inc(mm)
    p.blk.tensor(xl_pe)

    def xl_dve(e):
        for i in range(16):
            lt, b = divmod(i, NB)
            Plan.aw(e, p.pe, pe0 + i + 1)
            dxl = wbuf[:, 4096 + 2048 * lt + 512 * b:4096 + 2048 * lt + 512 * (b + 1)]
            p.dve.inc(e.tensor_copy(dxl, p.pb[i % 2][:, :]))
    p.blk.vector(xl_dve)
    if upto == "xl":
        p.fence()
        emit_dump(p, dst, lambda t: wbuf[:, 4096 + 2048 * t:4096 + 2048 * (t + 1)],
                  tiles=4)
        return
    p.fence()
    xl = lambda c, b: wbuf[:, 4096 + 2048 * c + 512 * b:4096 + 2048 * c + 512 * (b + 1)]

    h_eb = arf.bitcast(BF16)
    H0 = 20480
    acc_t = lambda c, b: arf[:, 2048 * c + 512 * b:2048 * c + 512 * (b + 1)]
    wrep_b = lambda b: arf[:, 8192 + 512 * b:8192 + 512 * (b + 1)]

    cast0, pe0, act0, dve0 = p.cast.val, p.pe.val, p.act.val, p.dve.val
    PE_E, DVE_E, ACT_E = 40, 52, 32

    def d_wrep(ee, b):
        return dve0 + DVE_E * ee + b + 1

    def d_h(ee, b, ht):
        return dve0 + DVE_E * ee + 4 + 12 * b + ht + 1

    def d_acc(ee, b):
        return dve0 + DVE_E * ee + 4 + 12 * b + 12

    def g_pe(ee, b, ht):
        return pe0 + PE_E * ee + 4 + 9 * b + ht + 1

    def ex_cast(e):
        Plan.aw(e, p.cast, cast0)
        Plan.aw(e, p.pe, pe0)
        for ee in range(NE):
            if ee >= 2:
                Plan.aw(e, p.pe, pe0 + (ee - 1) * PE_E)
            p.cast.inc(e.dma_start(
                arb1[:, 12288 * (ee % 2):12288 * (ee % 2 + 1)],
                dr["rw_img"].ap()[128 * ee:128 * (ee + 1), :]), 16)
    p.blk.gpsimd(ex_cast)

    def ex_pe(e):
        p.fwait(e, "dve", "act")
        for ee in range(NE):
            s = 12288 * (ee % 2)
            Plan.aw(e, p.cast, cast0 + 16 * min(ee + 2, NE))
            if ee > 0:
                Plan.aw(e, p.dve, dve0 + DVE_E * ee)
            for b in range(NB):
                if b > 0 or ee > 0:
                    pass
                Plan.aw(e, p.dve, d_wrep(ee, b) - 1 if (ee, b) != (0, 0) else 0)
                p.pe.inc(e.matmul(p.pb[6][:, :],
                                  p.cbuf2[:, 128 * ee:128 * (ee + 1)],
                                  wT16[:, 512 * b:512 * (b + 1)],
                                  start=True, stop=True))
            for b in range(NB):
                if b > 0:
                    Plan.aw(e, p.dve, d_acc(ee, b - 1))
                for ht in range(8):
                    g = 8 * b + ht
                    pr = g % 3
                    if ht >= 3:
                        Plan.aw(e, p.dve, d_h(ee, b, ht - 3))
                    pa, pg = p.pb[2 * pr], p.pb[2 * pr + 1]
                    for c in range(4):
                        e.matmul(pa[:, :], arb1[:, s + 512 * ht + 128 * c:
                                                s + 512 * ht + 128 * (c + 1)],
                                 xl(c, b), start=(c == 0), stop=(c == 3))
                    for c in range(4):
                        mm = e.matmul(pg[:, :], arb1[:, s + 4096 + 512 * ht + 128 * c:
                                                     s + 4096 + 512 * ht + 128 * (c + 1)],
                                      xl(c, b), start=(c == 0), stop=(c == 3))
                    p.pe.inc(mm)
                Plan.aw(e, p.dve, d_h(ee, b, 7))
                for lt in range(4):
                    for ht in range(8):
                        hsrc = h_eb[:, H0 + 512 * ht:H0 + 512 * (ht + 1)]
                        mm = e.matmul(p.pb[lt][:, :],
                                      arb1[:, s + 8192 + 1024 * lt + 128 * ht:
                                           s + 8192 + 1024 * lt + 128 * (ht + 1)],
                                      hsrc, start=(ht == 0), stop=(ht == 7))
                p.pe.inc(mm)
    p.blk.tensor(ex_pe)

    def ex_act(e):
        for ee in range(NE):
            for b in range(NB):
                for ht in range(8):
                    g = 8 * b + ht
                    Plan.aw(e, p.pe, g_pe(ee, b, ht))
                    gb2 = g - 2
                    if gb2 >= 0:
                        Plan.aw(e, p.dve, d_h(ee, gb2 // 8, gb2 % 8))
                    elif ee > 0:
                        Plan.aw(e, p.dve, dve0 + DVE_E * ee)
                    pg = p.pb[2 * (g % 3) + 1]
                    tmp = p.smf[:, 512 * (g % 2):512 * (g % 2 + 1)]
                    e.activation(tmp, pg[:, :], AF.Relu)
                    p.act.inc(e.activation(tmp, tmp, AF.Square))
    p.blk.scalar(ex_act)

    def ex_dve(e):
        for ee in range(NE):
            for b in range(NB):
                Plan.aw(e, p.pe, pe0 + PE_E * ee + b + 1)
                p.dve.inc(e.tensor_copy(wrep_b(b), p.pb[6][:, :]))
            for b in range(NB):
                for ht in range(8):
                    g = 8 * b + ht
                    Plan.aw(e, p.act, act0 + ACT_E * ee + g + 1)
                    pa = p.pb[2 * (g % 3)]
                    tmp = p.smf[:, 512 * (g % 2):512 * (g % 2 + 1)]
                    hdst = h_eb[:, H0 + 512 * ht:H0 + 512 * (ht + 1)]
                    p.dve.inc(e.tensor_mul(hdst, pa[:, :], tmp))
                Plan.aw(e, p.pe, pe0 + PE_E * ee + 4 + 9 * b + 9)
                for lt in range(4):
                    t2 = p.smf2[:, 512:1024]
                    if ee == 0:
                        p.dve.inc(e.tensor_mul(acc_t(lt, b), p.pb[lt][:, :],
                                               wrep_b(b)))
                    else:
                        e.tensor_mul(t2, p.pb[lt][:, :], wrep_b(b))
                        p.dve.inc(e.tensor_add(acc_t(lt, b), acc_t(lt, b), t2))
    p.blk.vector(ex_dve)
    p.fence()
    if upto == "experts":
        emit_dump(p, dst, lambda t: arf[:, 2048 * t:2048 * (t + 1)], tiles=4)
        return

    def acc_cvt(e):
        for c in range(4):
            last = e.tensor_copy(wbuf[:, 4096 + 2048 * c:4096 + 2048 * (c + 1)],
                                 arf[:, 2048 * c:2048 * (c + 1)])
        return last
    p.chain([("vector", acc_cvt)])
    accb = lambda c, b: wbuf[:, 4096 + 2048 * c + 512 * b:
                             4096 + 2048 * c + 512 * (b + 1)]

    cast0, pe0, dve0, ld0, st0 = (p.cast.val, p.pe.val, p.dve.val,
                                  p.ld.val, p.st.val)

    def cm_cast(e):
        Plan.aw(e, p.cast, cast0)
        Plan.aw(e, p.pe, pe0)
        for dt_ in range(DT):
            if dt_ >= 2:
                Plan.aw(e, p.pe, pe0 + 4 * (dt_ - 1))
            p.cast.inc(e.dma_start(
                wbuf[:, 1536 * (dt_ % 2):1536 * (dt_ % 2 + 1)],
                dr["comb_img"].ap()[128 * dt_:128 * (dt_ + 1), :]), 16)
    p.blk.gpsimd(cm_cast)

    def cm_sync(e):
        p.fwait(e, "dve")
        Plan.aw(e, p.ld, ld0)
        Plan.aw(e, p.st, st0)
        for i in range(DT * NB):
            if i >= 3:
                Plan.aw(e, p.dve, dve0 + i - 2)
            dt_, b = divmod(i, NB)
            p.ld.inc(e.dma_start(
                p.xbuf[:, 512 * (i % 3):512 * (i % 3 + 1)],
                src.ap()[128 * dt_:128 * (dt_ + 1), BLK * b:BLK * (b + 1)]), 16)
    p.blk.sync(cm_sync)

    def cm_pe(e):
        p.fwait(e, "dve", "act")
        for dt_ in range(DT):
            Plan.aw(e, p.cast, cast0 + 16 * min(dt_ + 2, DT))
            for b in range(NB):
                i = dt_ * NB + b
                if i >= 2:
                    Plan.aw(e, p.dve, dve0 + i - 1)
                pbk = p.pb[i % 2]
                w0 = 1536 * (dt_ % 2)
                for c in range(8):
                    e.matmul(pbk[:, :], wbuf[:, w0 + 128 * c:w0 + 128 * (c + 1)],
                             arb2[:, 2048 * c + 512 * b:2048 * c + 512 * (b + 1)],
                             start=(c == 0), stop=False)
                for c in range(4):
                    mm = e.matmul(pbk[:, :],
                                  wbuf[:, w0 + 1024 + 128 * c:w0 + 1024 + 128 * (c + 1)],
                                  accb(c, b), start=False, stop=(c == 3))
                p.pe.inc(mm)
    p.blk.tensor(cm_pe)

    def cm_dve(e):
        for i in range(DT * NB):
            Plan.aw(e, p.pe, pe0 + i + 1)
            Plan.dwait(e, p.ld, ld0, i + 3, DT * NB)
            if i >= 2:
                Plan.aw(e, p.st, st0 + 16 * (i - 1))
            xsrc = p.xbuf[:, 512 * (i % 3):512 * (i % 3 + 1)]
            out = p.smf2[:, 512 * (i % 2):512 * (i % 2 + 1)]
            p.dve.inc(e.scalar_tensor_tensor(out, p.pb[i % 2][:, :], 0.5, xsrc,
                                             ALU.mult, ALU.add))
    p.blk.vector(cm_dve)

    def cm_st(e):
        for i in range(DT * NB):
            dt_, b = divmod(i, NB)
            Plan.aw(e, p.dve, dve0 + i + 1)
            out = p.smf2[:, 512 * (i % 2):512 * (i % 2 + 1)]
            p.st.inc(e.dma_start(
                dst.ap()[128 * dt_:128 * (dt_ + 1), BLK * b:BLK * (b + 1)], out), 16)
    p.blk.scalar(cm_st)
    p.fence()



def build_program(sublayers, final_dst="out", upto=None):
    nc = bass.Bass("TRN2", target_bir_lowering=False)
    dr = {}
    dr["x_in"] = nc.dram_tensor("x_in", [D, T], F32, kind="ExternalInput")
    dr["out"] = nc.dram_tensor("out", [D, T], F32, kind="ExternalOutput")
    for nm, (shape, dt_) in WEIGHT_SPECS.items():
        dr[nm] = nc.dram_tensor(nm, list(shape), dt_, kind="ExternalInput")
    for nm, (shape, dt_) in SCRATCH_SPECS.items():
        dr[nm] = nc.dram_tensor(nm, list(shape), dt_)

    with contextlib.ExitStack() as ctx:
        p = make_plan(nc, ctx)
        emit_load_consts(p, dr)
        emit_prep_small(p)
        srcs = ["x_in"] + ["resid"] * (len(sublayers) - 1)
        dsts = ["resid"] * (len(sublayers) - 1) + [final_dst]
        for sl, s, d in zip(sublayers, srcs, dsts):
            if sl == "ffn0a" or sl == "ffn0b":
                emit_ffn_l0(p, dr, dr[s], dr[d], upto=upto)
            elif sl == "mamba":
                emit_mamba(p, dr, dr[s], dr[d], upto=upto)
            elif sl == "attn":
                emit_attn_l1(p, dr, dr[s], dr[d], upto=upto)
            elif sl == "ffn1a":
                emit_ffn_l1(p, dr, dr[s], dr[d], dequant=True)
            elif sl == "ffn1b":
                emit_ffn_l1(p, dr, dr[s], dr[d], dequant=False)
            else:
                raise ValueError(sl)
        stv = p.st.val

        def fin(e):
            e.wait_ge(p.st.h, stv)
        p.blk.sync(fin)
    return nc


def emit_mamba(p, dr, src, dst, upto=None):
    cb, cf = p.cbuf, p.cbuf_f
    arb1, arb2, arf, wbuf = p.arb1, p.arb2, p.arf, p.wbuf
    xb, srh = p.xbuf, p.smb
    mrep = p.smf[:, 0:512]
    rrep = p.smf[:, 512:1024]
    r_mean = p.rows[0:1, 0:512]
    r_rstd = p.rows[0:1, 512:1024]
    r_sq = p.rows[0:1, 1024:1536]
    r_tmp = p.smf2[0:1, 0:512]
    rstd2_all = arf[0:1, 8192:10240]

    for b in range(NB):
        ld0, act0, pe0, dve0, st0 = (p.ld.val, p.act.val, p.pe.val,
                                     p.dve.val, p.st.val)

        def sync1(e, ld0=ld0, act0=act0, b=b, st0=st0):
            p.fwait(e, "dve", "act")
            Plan.aw(e, p.ld, ld0)
            Plan.aw(e, p.st, st0)
            for t in range(DT):
                Plan.aw(e, p.act, act0 + t - 2)
                p.ld.inc(e.dma_start(
                    xb[:, 512 * (t % 3):512 * (t % 3 + 1)],
                    src.ap()[128 * t:128 * (t + 1), BLK * b:BLK * (b + 1)]), 16)
        p.blk.sync(sync1)
        assert p.ld.val == ld0 + 16 * DT, ("cnt", "ld", p.ld.val, ld0 + 16 * DT)

        def act1(e, ld0=ld0, pe0=pe0, b=b):
            p.fwait(e, "dve", "pe")
            for t in range(DT):
                Plan.dwait(e, p.ld, ld0, t + 3, DT)
                Plan.aw(e, p.pe, pe0 + t - 1)
                s2 = xb[:, 512 * (t % 3):512 * (t % 3 + 1)]
                dstp = srh[:, 1024 * (t % 2):1024 * (t % 2 + 1)]
                e.copy(dstp[:, 0:512], s2)
                p.act.inc(e.activation(dstp[:, 512:1024], s2, AF.Square))
        p.blk.scalar(act1)
        assert p.act.val == act0 + DT, ("cnt", "act", p.act.val, act0 + DT)

        def pe1(e, act0=act0, b=b):
            p.fwait(e, "dve")
            for t in range(DT):
                Plan.aw(e, p.act, act0 + t + 1)
                dstp = srh[:, 1024 * (t % 2):1024 * (t % 2 + 1)]
                first, last = (t == 0), (t == DT - 1)
                e.matmul(p.pb[4][0:1, :], cb[:, CB_ONES_COL:CB_ONES_COL + 1],
                         dstp[:, 0:512], start=first, stop=last)
                p.pe.inc(e.matmul(p.pb[5][0:1, :],
                                  cb[:, CB_ONES_COL:CB_ONES_COL + 1],
                                  dstp[:, 512:1024], start=first, stop=last))
        p.blk.tensor(pe1)
        assert p.pe.val == pe0 + DT, ("cnt", "pe", p.pe.val, pe0 + DT)
        p.fence()

        p.chain([
            ("vector", lambda e: [
                selfstep(p, e, p.dve,
                         e.tensor_scalar_mul(r_mean, p.pb[4][0:1, :], 1.0 / D)),
                selfstep(p, e, p.dve,
                         e.tensor_scalar_mul(r_sq, p.pb[5][0:1, :], 1.0 / D)),
                selfstep(p, e, p.dve,
                         e.tensor_tensor(r_tmp, r_mean, r_mean, ALU.mult)),
                selfstep(p, e, p.dve, e.tensor_sub(r_rstd, r_sq, r_tmp)),
                e.tensor_scalar_add(r_rstd, r_rstd, float(LN_EPS))][-1]),
            ("scalar", lambda e: [
                selfstep(p, e, p.act, e.activation(r_rstd, r_rstd, AF.Ln)),
                e.activation(r_rstd, r_rstd, AF.Exp, scale=-0.5)][-1]),
            ("tensor", lambda e: [
                e.matmul(p.pb[4][:, :], cf[0:1, CF_ONES_ROW:CF_ONES_ROW + 128],
                         r_mean, start=True, stop=True),
                e.matmul(p.pb[5][:, :], cf[0:1, CF_ONES_ROW:CF_ONES_ROW + 128],
                         r_rstd, start=True, stop=True)][-1]),
            ("vector", lambda e: [
                e.tensor_copy(mrep, p.pb[4][:, :]),
                e.tensor_copy(rrep, p.pb[5][:, :])][-1]),
        ])

        ld2, dve2, act2, pe2, st2 = (p.ld.val, p.dve.val, p.act.val,
                                     p.pe.val, p.st.val)

        def sync2(e, ld2=ld2, dve2=dve2, b=b, st2=st2):
            Plan.aw(e, p.ld, ld2)
            Plan.aw(e, p.act, p.act.val)
            Plan.aw(e, p.st, st2)
            for t in range(DT):
                if t >= 3:
                    Plan.aw(e, p.dve, dve2 + 2 * (t - 2))
                p.ld.inc(e.dma_start(
                    xb[:, 512 * (t % 3):512 * (t % 3 + 1)],
                    src.ap()[128 * t:128 * (t + 1), BLK * b:BLK * (b + 1)]), 16)
        p.blk.sync(sync2)
        assert p.ld.val == ld2 + 16 * DT, ("cnt", "ld", p.ld.val, ld2 + 16 * DT)

        def dve3(e, ld2=ld2, st2=st2, b=b):
            for t in range(DT):
                Plan.dwait(e, p.ld, ld2, t + 3, DT)
                if t >= 2:
                    Plan.aw(e, p.st, st2 + 16 * 2 * (t - 1))
                s2 = xb[:, 512 * (t % 3):512 * (t % 3 + 1)]
                v = p.smf2[:, 512 * (t % 2):512 * (t % 2 + 1)]
                rp = arf[:, 512 * (t % 2):512 * (t % 2 + 1)]
                e.tensor_sub(v, s2, mrep)
                p.dve.inc(e.tensor_mul(v, v, rrep))
                p.dve.inc(e.tensor_add(rp, s2, v))
        p.blk.vector(dve3)
        assert p.dve.val == dve2 + 2 * DT, ("cnt", "dve", p.dve.val, dve2 + 2 * DT)

        def act3(e, dve2=dve2, pe2=pe2, st2=st2, b=b):
            for t in range(DT):
                Plan.aw(e, p.dve, dve2 + 2 * t + 1)
                if t >= 2:
                    Plan.aw(e, p.pe, pe2 + t - 1)
                v = p.smf2[:, 512 * (t % 2):512 * (t % 2 + 1)]
                sq = srh[:, 1024 * (t % 2):1024 * (t % 2) + 512]
                p.act.inc(e.activation(sq, v, AF.Square))
                rp = arf[:, 512 * (t % 2):512 * (t % 2 + 1)]
                Plan.aw(e, p.dve, dve2 + 2 * t + 2)
                p.st.inc(e.dma_start(
                    dr["vbuf"].ap()[128 * t:128 * (t + 1), BLK * b:BLK * (b + 1)],
                    v), 16)
                p.st.inc(e.dma_start(
                    dr["r1buf"].ap()[128 * t:128 * (t + 1), BLK * b:BLK * (b + 1)],
                    rp), 16)
        p.blk.scalar(act3)
        assert p.act.val == act2 + DT, ("cnt", "act", p.act.val, act2 + DT)
        assert p.st.val == st2 + 32 * DT, ("cnt", "st", p.st.val, st2 + 32 * DT)

        def pe3(e, act2=act2, b=b):
            for t in range(DT):
                Plan.aw(e, p.act, act2 + t + 1)
                sq = srh[:, 1024 * (t % 2):1024 * (t % 2) + 512]
                p.pe.inc(e.matmul(p.pb[6][0:1, :],
                                  cb[:, CB_ONES_COL:CB_ONES_COL + 1], sq,
                                  start=(t == 0), stop=(t == DT - 1)))
        p.blk.tensor(pe3)
        assert p.pe.val == pe2 + DT, ("cnt", "pe", p.pe.val, pe2 + DT)
        p.fence()

        rs = rstd2_all[0:1, 512 * b:512 * (b + 1)]

        def vrms(e, rs=rs):
            e.tensor_scalar_mul(r_tmp, p.pb[6][0:1, :], 1.0 / D)
            return e.tensor_scalar_add(rs, r_tmp, float(RMS_EPS_MAMBA))

        def srms(e, rs=rs):
            e.activation(rs, rs, AF.Ln)
            return e.activation(rs, rs, AF.Exp, scale=-0.5)
        p.chain([("vector", vrms), ("scalar", srms)])

    for b in range(NB):
        def reprs(e, b=b):
            return e.matmul(p.pb[5][:, :], cf[0:1, CF_ONES_ROW:CF_ONES_ROW + 128],
                            rstd2_all[0:1, 512 * b:512 * (b + 1)],
                            start=True, stop=True)
        p.chain([("tensor", reprs),
                 ("vector", lambda e: e.tensor_copy(rrep, p.pb[5][:, :]))])
        ld0, dve0 = p.ld.val, p.dve.val

        def syncv(e, ld0=ld0, dve0=dve0, b=b):
            Plan.aw(e, p.ld, ld0)
            Plan.aw(e, p.dve, dve0)
            Plan.aw(e, p.st, p.st.val)
            for t in range(DT):
                if t >= 3:
                    Plan.aw(e, p.dve, dve0 + t - 2)
                p.ld.inc(e.dma_start(
                    xb[:, 512 * (t % 3):512 * (t % 3 + 1)],
                    dr["vbuf"].ap()[128 * t:128 * (t + 1), BLK * b:BLK * (b + 1)]), 16)
        p.blk.sync(syncv)
        assert p.ld.val == ld0 + 16 * DT, ("cnt", "ld", p.ld.val, ld0 + 16 * DT)

        def dvev(e, ld0=ld0, b=b):
            for t in range(DT):
                Plan.dwait(e, p.ld, ld0, t + 3, DT)
                s2 = xb[:, 512 * (t % 3):512 * (t % 3 + 1)]
                dstx = arb1[:, 2048 * t + 512 * b:2048 * t + 512 * (b + 1)]
                p.dve.inc(e.scalar_tensor_tensor(
                    dstx, s2, cf[:, CF_WNORM + t:CF_WNORM + t + 1], rrep,
                    ALU.mult, ALU.mult))
        p.blk.vector(dvev)
        assert p.dve.val == dve0 + DT, ("cnt", "dve", p.dve.val, dve0 + DT)
        p.fence()
    xn = lambda c, b: arb1[:, 2048 * c + 512 * b:2048 * c + 512 * (b + 1)]
    if upto == "xn":
        emit_dump(p, dst, lambda t: arb1[:, 2048 * t:2048 * (t + 1)])
        return

    NT = 69
    ld0, cast0, pe0, act0, dve0, st0 = (p.ld.val, p.cast.val, p.pe.val,
                                        p.act.val, p.dve.val, p.st.val)

    def is_xi(tt):
        return 32 <= tt < 64

    def n_xi(tt):
        return max(0, min(tt, 64) - 32)

    def dve_base(tt):
        return dve0 + 5 * n_xi(tt) + 4 * (tt - n_xi(tt))

    def act_base(tt):
        return act0 + n_xi(tt)

    def pe_base(tt):
        return pe0 + 4 * tt + n_xi(tt)

    def st_jz(tt):
        return 4 * (min(tt, 32) + max(0, tt - 64))

    def pj_cast(e):
        Plan.aw(e, p.cast, cast0)
        Plan.aw(e, p.pe, pe0)
        for tt in range(NT):
            if tt >= 2:
                Plan.aw(e, p.pe, pe_base(tt - 1))
            p.cast.inc(e.dma_start(
                wbuf[:, 2048 * (tt % 2):2048 * (tt % 2 + 1)],
                dr["w_inT"].ap()[128 * tt:128 * (tt + 1), :]), 16)
    p.blk.gpsimd(pj_cast)
    assert p.cast.val == cast0 + 16 * NT, ("cnt", "cast", p.cast.val, cast0 + 16 * NT)

    def pj_pe(e):
        p.fwait(e, "dve", "act")
        for tt in range(NT):
            if tt >= 33 and is_xi(tt - 1):
                i = tt - 1 - 32
                Plan.aw(e, p.act, act_base(tt - 1) + 1)
                sl = arb2[:, 2048 * ((tt - 1) % 2):2048 * ((tt - 1) % 2 + 1)]
                for b in range(NB):
                    mm = e.matmul(p.pb[2 + b][0:16, 0:512],
                                  cb[:, CB_XMEAN + 16 * i:CB_XMEAN + 16 * (i + 1)],
                                  sl[:, 512 * b:512 * (b + 1)],
                                  start=(tt - 1 == 32), stop=(tt - 1 == 63))
                p.pe.inc(mm)
            Plan.aw(e, p.cast, cast0 + 16 * min(tt + 2, NT))
            for b in range(NB):
                i = tt * NB + b
                if i >= 2:
                    pb_, bb_ = divmod(i - 2, NB)
                    Plan.aw(e, p.dve, dve_base(pb_) + bb_ + 1)
                for c in range(DT):
                    mm = e.matmul(p.pb[i % 2][:, :],
                                  wbuf[:, 2048 * (tt % 2) + 128 * c:
                                       2048 * (tt % 2) + 128 * (c + 1)],
                                  xn(c, b), start=(c == 0), stop=(c == DT - 1))
                p.pe.inc(mm)
    p.blk.tensor(pj_pe)
    assert p.pe.val == pe_base(NT), ("cnt", "pe", p.pe.val, pe_base(NT))

    def pj_dve(e):
        jz = 0
        for tt in range(NT):
            base = dve_base(tt)
            for b in range(NB):
                i = tt * NB + b
                Plan.aw(e, p.pe, pe_base(tt) + b + 1)
                if is_xi(tt):
                    if b == 0 and tt >= 34:
                        Plan.aw(e, p.act, act_base(tt - 1))
                    dstt = arf[:, 2048 * (tt % 2) + 512 * b:
                               2048 * (tt % 2) + 512 * (b + 1)]
                    p.dve.inc(e.tensor_copy(dstt, p.pb[i % 2][:, :]))
                else:
                    sl = p.smf[:, 512 * (jz % 2):512 * (jz % 2 + 1)]
                    Plan.aw(e, p.st, st0 + 16 * (st_jz(tt) + b - 1))
                    p.dve.inc(e.tensor_copy(sl, p.pb[i % 2][:, :]))
                    jz += 1
            if is_xi(tt):
                asm = arf[:, 2048 * (tt % 2):2048 * (tt % 2 + 1)]
                acc = arf[:, 4096:6144]
                it = tt - 32
                if tt > 32:
                    Plan.aw(e, p.act, act_base(tt))
                e.tensor_scalar(acc, asm, cf[:, CF_WCONV + 4 * it + 3:
                                             CF_WCONV + 4 * it + 4],
                                None, ALU.mult)
                last = None
                for tau in range(3):
                    sh = 3 - tau
                    last = e.scalar_tensor_tensor(
                        acc[:, sh:2048], asm[:, 0:2048 - sh],
                        cf[:, CF_WCONV + 4 * it + tau:CF_WCONV + 4 * it + tau + 1],
                        acc[:, sh:2048], ALU.mult, ALU.add)
                p.dve.inc(last)
    p.blk.vector(pj_dve)
    assert p.dve.val == dve_base(NT), ("cnt", "dve", p.dve.val, dve_base(NT))

    def pj_act(e):
        for tt in range(NT):
            base = dve_base(tt)
            if is_xi(tt):
                Plan.aw(e, p.dve, base + 5)
                if tt >= 34:
                    Plan.aw(e, p.pe, pe_base(tt - 1) + 1)
                acc = arf[:, 4096:6144]
                sl = arb2[:, 2048 * (tt % 2):2048 * (tt % 2 + 1)]
                p.act.inc(e.activation(sl, acc, AF.Silu))
            else:
                for b in range(NB):
                    Plan.aw(e, p.dve, base + b + 1)
                    jz2 = st_jz(tt) + b
                    sl = p.smf[:, 512 * (jz2 % 2):512 * (jz2 % 2 + 1)]
                    if tt < 32:
                        dd = dr["zbuf"].ap()[128 * tt:128 * (tt + 1),
                                             BLK * b:BLK * (b + 1)]
                    else:
                        dd = dr["dtbc"].ap()[128 * (tt - 64):128 * (tt - 63),
                                             BLK * b:BLK * (b + 1)]
                    p.st.inc(e.dma_start(dd, sl), 16)
    p.blk.scalar(pj_act)
    assert p.act.val == act_base(NT), ("cnt", "act", p.act.val, act_base(NT))
    assert p.st.val == st0 + 16 * st_jz(NT), ("cnt", "st", p.st.val, st0 + 16 * st_jz(NT))
    p.fence()

    def xm_ev(e):
        for b in range(NB):
            last = e.tensor_copy(arf[0:16, 6144 + 512 * b:6144 + 512 * (b + 1)],
                                 p.pb[2 + b][0:16, 0:512])
        return last
    p.chain([("vector", xm_ev)])
    x_mean = arf[0:16, 6144:8192]
    if upto == "proj":
        emit_dump(p, dst, lambda t: arf[0:16, 6144:8192], rows=16, tiles=1)
        return

    dtS = arf[0:16, 0:2048]
    v_s = arf[0:16, 2048:4096]
    y_all = arf[0:16, 4096:6144]
    absb = arb2.bitcast(F32)[0:16, 2048:4096]
    ld00 = p.ld.val

    def b4_sync0(e):
        p.fwait(e, "dve", "act", "pe")
        Plan.aw(e, p.st, p.st.val)
        p.ld.inc(e.dma_start(dtS, dr["dtbc"].ap()[0:16, :]), 16)
    p.blk.sync(b4_sync0)
    assert p.ld.val == ld00 + 16, ("cnt", "ld", p.ld.val, ld00 + 16)

    def v1(e):
        Plan.aw(e, p.ld, ld00 + 16)
        e.tensor_scalar(dtS, dtS, cf[0:16, CF_DTBIAS:CF_DTBIAS + 1],
                        None, ALU.add)
        e.tensor_scalar_max(v_s, dtS, 0.0)
        e.tensor_scalar_mul(absb, dtS, -1.0)
        return e.tensor_tensor(absb, absb, dtS, ALU.max)

    def s1(e):
        e.activation(absb, absb, AF.Exp, scale=-1.0)
        return e.activation(absb, absb, AF.Ln, bias=1.0)

    def v2(e):
        e.tensor_add(dtS, v_s, absb)
        e.tensor_mul(v_s, dtS, x_mean)
        return e.tensor_scalar(dtS, dtS, cf[0:16, CF_NALOGE:CF_NALOGE + 1],
                               None, ALU.mult)

    def s2(e):
        return e.activation(dtS, dtS, AF.Exp)

    p.chain([("vector", v1), ("scalar", s1), ("vector", v2), ("scalar", s2)])

    Bq = lambda k: arf[:, 6144 + 512 * k:6144 + 512 * (k + 1)]
    Cq = lambda k: arf[:, 7168 + 512 * k:7168 + 512 * (k + 1)]
    vrep = lambda k: arf[:, 8192 + 512 * k:8192 + 512 * (k + 1)]
    darep = lambda k: arf[:, 9216 + 512 * k:9216 + 512 * (k + 1)]
    hq = lambda k: arf[:, 10240 + 512 * k:10240 + 512 * (k + 1)]
    hstate = p.smf2[:, 512:514]

    p.chain([("vector", lambda e: e.memset(hstate, 0.0))])

    for q in range(4):
        ldq = p.ld.val

        def b4_sync(e, q=q, ldq=ldq):
            Plan.aw(e, p.dve, p.dve.val)
            for k in range(2):
                p.ld.inc(e.dma_start(
                    Bq(k), dr["dtbc"].ap()[128 + 128 * k:256 + 128 * k,
                                           512 * q:512 * (q + 1)]), 16)
                p.ld.inc(e.dma_start(
                    Cq(k), dr["dtbc"].ap()[384 + 128 * k:512 + 128 * k,
                                           512 * q:512 * (q + 1)]), 16)
        p.blk.sync(b4_sync)
        assert p.ld.val == ldq + 64, ("cnt", "ld", p.ld.val, ldq + 64)

        def mkrep(e, q=q, ldq=ldq):
            Plan.aw(e, p.ld, ldq + 64)
            for k in range(2):
                e.matmul(p.pb[2 + k][:, :],
                         cf[0:16, CF_HREP + 128 * k:CF_HREP + 128 * (k + 1)],
                         v_s[0:16, 512 * q:512 * (q + 1)],
                         start=True, stop=True)
                mm = e.matmul(p.pb[4 + k][:, :],
                              cf[0:16, CF_HREP + 128 * k:CF_HREP + 128 * (k + 1)],
                              dtS[0:16, 512 * q:512 * (q + 1)],
                              start=True, stop=True)
            return mm

        def evrep(e, q=q):
            for k in range(2):
                e.tensor_copy(vrep(k), p.pb[2 + k][:, :])
                e.tensor_copy(darep(k), p.pb[4 + k][:, :])
            for k in range(2):
                last = e.tensor_mul(Bq(k), Bq(k), vrep(k))
            return last

        def scan(e, q=q):
            e.tensor_tensor_scan(hq(0), darep(0), Bq(0), hstate[:, 0:1],
                                 ALU.mult, ALU.add)
            return e.tensor_tensor_scan(hq(1), darep(1), Bq(1), hstate[:, 1:2],
                                        ALU.mult, ALU.add)

        def post(e, q=q):
            e.tensor_copy(hstate[:, 0:1], hq(0)[:, 511:512])
            e.tensor_copy(hstate[:, 1:2], hq(1)[:, 511:512])
            e.tensor_mul(Cq(0), Cq(0), hq(0))
            return e.tensor_mul(Cq(1), Cq(1), hq(1))

        def ymm(e, q=q):
            e.matmul(p.pb[6][0:16, 0:512], cf[:, CF_HCOL:CF_HCOL + 16],
                     Cq(0), start=True, stop=False)
            return e.matmul(p.pb[6][0:16, 0:512],
                            cf[:, CF_HCOL + 16:CF_HCOL + 32],
                            Cq(1), start=False, stop=True)

        def yev(e, q=q):
            return e.tensor_copy(y_all[0:16, 512 * q:512 * (q + 1)],
                                 p.pb[6][0:16, 0:512])

        p.chain([("tensor", mkrep), ("vector", evrep), ("vector", scan),
                 ("vector", post), ("tensor", ymm), ("vector", yev)])
    if upto == "scan":
        emit_dump(p, dst, lambda t: y_all, rows=16, tiles=1)
        return

    wh2 = arf[0:16, 6144:10240]
    ldw = p.ld.val

    def b5_sync0(e):
        Plan.aw(e, p.dve, p.dve.val)
        p.ld.inc(e.dma_start(wh2, dr["w_h2iT"].ap()), 16)
    p.blk.sync(b5_sync0)
    assert p.ld.val == ldw + 16, ("cnt", "ld", p.ld.val, ldw + 16)

    for half in range(2):
        cast0, pe0, act0, dve0, ld0, st0 = (p.cast.val, p.pe.val, p.act.val,
                                            p.dve.val, p.ld.val, p.st.val)
        PE_B5, DVE_B5, ACT_B5, LD_B5 = 40, 40, 32, 40

        def h_cast(e, half=half, cast0=cast0, pe0=pe0):
            Plan.aw(e, p.cast, cast0)
            if half == 0:
                p.fwait(e, "pe")
            else:
                Plan.aw(e, p.pe, pe0)
            for k in range(8):
                p.cast.inc(e.dma_start(
                    arb1[:, 4096 * k:4096 * (k + 1)],
                    dr["w_outT"].ap()[128 * (8 * half + k):
                                      128 * (8 * half + k + 1), :]), 16)
        p.blk.gpsimd(h_cast)
        assert p.cast.val == cast0 + 128, ("cnt", "cast", p.cast.val, cast0 + 128)

        def b5_sync(e, half=half, ld0=ld0, dve0=dve0):
            Plan.aw(e, p.ld, ld0)
            if half == 1:
                Plan.aw(e, p.dve, dve0)
            for b in range(NB):
                for it in range(32):
                    if it >= 3:
                        Plan.aw(e, p.dve, dve0 + DVE_B5 * b + it - 2)
                    elif b > 0:
                        Plan.aw(e, p.dve, dve0 + DVE_B5 * b)
                    p.ld.inc(e.dma_start(
                        xb[:, 512 * (it % 3):512 * (it % 3 + 1)],
                        dr["zbuf"].ap()[128 * it:128 * (it + 1),
                                        BLK * b:BLK * (b + 1)]), 16)
                for dt_ in range(8):
                    j = 32 + dt_
                    Plan.aw(e, p.dve, dve0 + DVE_B5 * b + 30 + min(dt_, 2)
                            if dt_ < 3 else dve0 + DVE_B5 * b + 32 + dt_ - 2)
                    p.ld.inc(e.dma_start(
                        xb[:, 512 * (j % 3):512 * (j % 3 + 1)],
                        dr["r1buf"].ap()[128 * (8 * half + dt_):
                                         128 * (8 * half + dt_ + 1),
                                         BLK * b:BLK * (b + 1)]), 16)
        p.blk.sync(b5_sync)
        assert p.ld.val == ld0 + 16 * LD_B5 * NB, ("cnt", "ld", p.ld.val, ld0 + 16 * LD_B5 * NB)

        def b5_pe(e, half=half, cast0=cast0, pe0=pe0, act0=act0, dve0=dve0,
                  ld0=ld0):
            if half == 0:
                p.fwait(e, "dve", "act")
                Plan.aw(e, p.ld, ldw + 16)
            Plan.aw(e, p.cast, cast0 + 128)
            for b in range(NB):
                for it in range(32):
                    if it >= 2:
                        Plan.aw(e, p.dve, dve0 + DVE_B5 * b + it - 1)
                    elif b + half > 0:
                        Plan.aw(e, p.dve, dve0 + DVE_B5 * b)
                    p.pe.inc(e.matmul(
                        p.pb[it % 2][:, :], wh2[0:16, 128 * it:128 * (it + 1)],
                        y_all[0:16, 512 * b:512 * (b + 1)],
                        start=True, stop=True))
                Plan.aw(e, p.dve, dve0 + DVE_B5 * b + 32)
                for dt_ in range(8):
                    if dt_ >= 2:
                        Plan.aw(e, p.dve, dve0 + DVE_B5 * b + 32 + dt_ - 1)
                    for c in range(32):
                        mm = e.matmul(p.pb[2 + dt_ % 2][:, :],
                                      arb1[:, 4096 * dt_ + 128 * c:
                                           4096 * dt_ + 128 * (c + 1)],
                                      arb2[:, 512 * c:512 * (c + 1)],
                                      start=(c == 0), stop=(c == 31))
                    p.pe.inc(mm)
        p.blk.tensor(b5_pe)
        assert p.pe.val == pe0 + PE_B5 * NB, ("cnt", "pe", p.pe.val, pe0 + PE_B5 * NB)

        def b5_act(e, half=half, pe0=pe0, dve0=dve0, ld0=ld0, st0=st0):
            for b in range(NB):
                for it in range(32):
                    Plan.aw(e, p.pe, pe0 + PE_B5 * b + it + 1)
                    Plan.dwait(e, p.ld, ld0, min(LD_B5 * b + it + 3, LD_B5 * (b + 1)), LD_B5 * NB)
                    if it >= 2:
                        Plan.aw(e, p.dve, dve0 + DVE_B5 * b + it - 1)
                    sg = p.smf[:, 512 * (it % 2):512 * (it % 2 + 1)]
                    z = xb[:, 512 * (it % 3):512 * (it % 3 + 1)]
                    p.act.inc(e.activation(sg, z, AF.Silu))
                for dt_ in range(8):
                    Plan.aw(e, p.dve, dve0 + DVE_B5 * b + 32 + dt_ + 1)
                    out = p.smf2[:, 512 * (dt_ % 2):512 * (dt_ % 2 + 1)]
                    p.st.inc(e.dma_start(
                        dst.ap()[128 * (8 * half + dt_):
                                 128 * (8 * half + dt_ + 1),
                                 BLK * b:BLK * (b + 1)], out), 16)
        p.blk.scalar(b5_act)
        assert p.act.val == act0 + ACT_B5 * NB, ("cnt", "act", p.act.val, act0 + ACT_B5 * NB)
        assert p.st.val == st0 + 16 * 8 * NB

        def b5_dve(e, half=half, pe0=pe0, act0=act0, dve0=dve0, ld0=ld0,
                   st0=st0):
            for b in range(NB):
                for it in range(32):
                    Plan.aw(e, p.act, act0 + ACT_B5 * b + it + 1)
                    Plan.aw(e, p.pe, pe0 + PE_B5 * b + it + 1)
                    sg = p.smf[:, 512 * (it % 2):512 * (it % 2 + 1)]
                    p.dve.inc(e.tensor_mul(arb2[:, 512 * it:512 * (it + 1)],
                                           p.pb[it % 2][:, :], sg))
                for dt_ in range(8):
                    Plan.aw(e, p.pe, pe0 + PE_B5 * b + 32 + dt_ + 1)
                    Plan.dwait(e, p.ld, ld0, min(LD_B5 * b + 32 + dt_ + 3,
                               LD_B5 * (b + 1)), LD_B5 * NB)
                    if dt_ >= 2:
                        Plan.aw(e, p.st, st0 + 16 * (8 * b + dt_ - 1))
                    j = 32 + dt_
                    rp = xb[:, 512 * (j % 3):512 * (j % 3 + 1)]
                    out = p.smf2[:, 512 * (dt_ % 2):512 * (dt_ % 2 + 1)]
                    p.dve.inc(e.tensor_add(out, p.pb[2 + dt_ % 2][:, :], rp))
        p.blk.vector(b5_dve)
        assert p.dve.val == dve0 + DVE_B5 * NB, ("cnt", "dve", p.dve.val, dve0 + DVE_B5 * NB)

        p.fence()


def emit_attn_l1(p, dr, src, dst, upto=None):
    cb, cf = p.cbuf, p.cbuf_f
    arb1, arb2, arf, wbuf = p.arb1, p.arb2, p.arf, p.wbuf
    xb, smb, smf, smf2 = p.xbuf, p.smb, p.smf, p.smf2
    SC = float(1.0 / np.sqrt(HDIM))

    def tm_store(name, ot, b):
        base = dr[name].ap()[512 * b:512 * (b + 1), 128 * ot:128 * (ot + 1)]
        return mkap(base, [[T, 128], [128 * T, 4], [1, 128]])

    def tm_load(name, h):
        base = dr[name].ap()[0:T, 128 * h:128 * (h + 1)]
        return mkap(base, [[T, 128], [128 * T, 16], [1, 128]])

    def dump_dram(name):
        g0 = p.gp.val
        p.fence()

        def body(e):
            p.fwait(e, "pe", "dve", "act")
            Plan.aw(e, p.st, p.st.val)
            p.gp.inc(e.dma_start(dst.ap()[0:D, 0:T], dr[name].ap()), 16)
            e.wait_ge(p.gp.h, g0 + 16)
        p.blk.gpsimd(body)
        p.fence()

    emit_ln(p, src, arb1, 0)
    xn = lambda c, b: arb1[:, 2048 * c + 512 * b:2048 * c + 512 * (b + 1)]
    if upto == "attn_ln":
        emit_dump(p, dst, lambda t: arb1[:, 2048 * t:2048 * (t + 1)])
        return

    for wname, xsp, fsp in (("wqT", "qspill", "qfspill"),
                            ("wkT", "kspill", None)):
        isk = fsp is None
        PEb = 4 if isk else 2
        ACTb = 4 if isk else 2
        peW0 = p.pe.val
        for pr in range(8):
            cast0, pe0, act0, dve0, st0, ld0 = (p.cast.val, p.pe.val,
                                                p.act.val, p.dve.val,
                                                p.st.val, p.ld.val)
            ws = 4096 * (pr % 2)

            def c2_cast(e, pr=pr, wname=wname, cast0=cast0, ws=ws, peW0=peW0,
                        PEb=PEb):
                Plan.aw(e, p.cast, cast0)
                Plan.aw(e, p.pe, peW0 + PEb * NB * max(pr - 1, 0))
                p.cast.inc(e.dma_start(
                    wbuf[:, ws:ws + 2048],
                    dr[wname].ap()[128 * pr:128 * (pr + 1), :]), 16)
                p.cast.inc(e.dma_start(
                    wbuf[:, ws + 2048:ws + 4096],
                    dr[wname].ap()[128 * (pr + 8):128 * (pr + 9), :]), 16)
            p.blk.gpsimd(c2_cast)
            assert p.cast.val == cast0 + 32

            def c2_sync(e, pr=pr, ld0=ld0, dve0=dve0, pe0=pe0, act0=act0):
                Plan.aw(e, p.ld, ld0)
                Plan.aw(e, p.dve, dve0)
                Plan.aw(e, p.pe, pe0)
                Plan.aw(e, p.act, act0)
                for j, nm in enumerate(("rope_cos", "rope_sin")):
                    p.ld.inc(e.dma_start(
                        arf[:, 2048 * j:2048 * (j + 1)],
                        dr[nm].ap()[128 * pr:128 * (pr + 1), :]), 16)
                for j, nm in enumerate(("rope_cos", "rope_sin")):
                    p.ld.inc(e.dma_start(
                        arf[:, 4096 + 2048 * j:4096 + 2048 * (j + 1)],
                        dr[nm].ap()[128 * (pr + 8):128 * (pr + 9), :]), 16)
            p.blk.sync(c2_sync)
            assert p.ld.val == ld0 + 64

            def c2_pe(e, cast0=cast0, dve0=dve0, act0=act0, ws=ws, isk=isk):
                p.fwait(e, "dve", "act")
                Plan.aw(e, p.cast, cast0 + 32)
                for b in range(NB):
                    if b > 0:
                        Plan.aw(e, p.dve, dve0 + 14 * b)
                    else:
                        Plan.aw(e, p.dve, max(dve0 - 9, 0))
                    for c in range(DT):
                        mm = e.matmul(p.pb[0][:, :],
                                      wbuf[:, ws + 128 * c:ws + 128 * (c + 1)],
                                      xn(c, b), start=(c == 0), stop=(c == DT - 1))
                    p.pe.inc(mm)
                    for c in range(DT):
                        mm = e.matmul(
                            p.pb[1][:, :],
                            wbuf[:, ws + 2048 + 128 * c:ws + 2048 + 128 * (c + 1)],
                            xn(c, b), start=(c == 0), stop=(c == DT - 1))
                    p.pe.inc(mm)
                    if isk:
                        Plan.aw(e, p.dve, dve0 + 14 * b + 10)
                        Plan.aw(e, p.act, max(act0 + 4 * b, 0))
                        for j in range(4):
                            mm = e.transpose(
                                p.pbt[:, 128 * j:128 * (j + 1)],
                                smb[:, 1024 + 128 * j:1024 + 128 * (j + 1)],
                                cb[:, CB_IDN:CB_IDN + 128])
                        p.pe.inc(mm)
                        Plan.aw(e, p.dve, dve0 + 14 * b + 14)
                        Plan.aw(e, p.act, max(act0 + 4 * b + 3, 0))
                        for j in range(4):
                            mm = e.transpose(
                                p.pbt[:, 512 + 128 * j:512 + 128 * (j + 1)],
                                smb[:, 1536 + 128 * j:1536 + 128 * (j + 1)],
                                cb[:, CB_IDN:CB_IDN + 128])
                        p.pe.inc(mm)
            p.blk.tensor(c2_pe)
            assert p.pe.val == pe0 + PEb * NB

            def c2_dve(e, pe0=pe0, act0=act0, dve0=dve0, st0=st0, ld0=ld0,
                       PEb=PEb):
                for b in range(NB):
                    Plan.aw(e, p.pe, pe0 + PEb * b + 2)
                    Plan.aw(e, p.act, act0 + ACTb * b)
                    Plan.aw(e, p.st, st0 + 64 * b)
                    if b == 0:
                        Plan.aw(e, p.ld, ld0 + 64)
                    cA = arf[:, 512 * b:512 * (b + 1)]
                    sA = arf[:, 2048 + 512 * b:2048 + 512 * (b + 1)]
                    cB = arf[:, 4096 + 512 * b:4096 + 512 * (b + 1)]
                    sB = arf[:, 6144 + 512 * b:6144 + 512 * (b + 1)]
                    t1, t2 = smf[:, 0:512], smf[:, 512:1024]
                    xA, xB = smf2[:, 0:512], smf2[:, 512:1024]
                    p.dve.inc(e.tensor_mul(t1, p.pb[0][:, :], cA))
                    p.dve.inc(e.tensor_mul(t2, p.pb[1][:, :], sA))
                    p.dve.inc(e.tensor_sub(xA, t1, t2))
                    p.dve.inc(e.tensor_mul(t1, p.pb[1][:, :], cB))
                    p.dve.inc(e.tensor_mul(t2, p.pb[0][:, :], sB))
                    p.dve.inc(e.tensor_add(xB, t1, t2))
                    p.dve.inc(e.tensor_copy(smb[:, 0:512], xA))
                    p.dve.inc(e.tensor_scalar(t1, xA, 0.0, None, ALU.min))
                    Plan.aw(e, p.act, act0 + ACTb * b + 1)
                    p.dve.inc(e.tensor_scalar(t1, xA, 0.0, None, ALU.max))
                    p.dve.inc(e.tensor_add(smb[:, 1024:1536], t2, t1))
                    p.dve.inc(e.tensor_copy(smb[:, 512:1024], xB))
                    p.dve.inc(e.tensor_scalar(t1, xB, 0.0, None, ALU.min))
                    Plan.aw(e, p.act, act0 + ACTb * b + 2)
                    p.dve.inc(e.tensor_scalar(t1, xB, 0.0, None, ALU.max))
                    p.dve.inc(e.tensor_add(smb[:, 1536:2048], t2, t1))
            p.blk.vector(c2_dve)
            assert p.dve.val == dve0 + 14 * NB

            def c2_act(e, dve0=dve0, act0=act0, st0=st0, pe0=pe0, pr=pr,
                       isk=isk, xsp=xsp, fsp=fsp, PEb=PEb):
                for b in range(NB):
                    Plan.aw(e, p.st, st0 + 64 * b)
                    Plan.aw(e, p.dve, dve0 + 14 * b + 8)
                    p.act.inc(e.activation(smf[:, 512:1024], smf[:, 0:512],
                                           AF.Exp))
                    p.st.inc(e.dma_start(
                        dr[xsp].ap()[128 * pr:128 * (pr + 1),
                                     512 * b:512 * (b + 1)],
                        smb[:, 0:512]), 16)
                    Plan.aw(e, p.dve, dve0 + 14 * b + 12)
                    p.act.inc(e.activation(smf[:, 512:1024], smf[:, 0:512],
                                           AF.Exp))
                    p.st.inc(e.dma_start(
                        dr[xsp].ap()[128 * (pr + 8):128 * (pr + 9),
                                     512 * b:512 * (b + 1)],
                        smb[:, 512:1024]), 16)
                    if isk:
                        Plan.aw(e, p.pe, pe0 + PEb * b + 3)
                        p.act.inc(e.copy(arb2[:, 0:512], p.pbt[:, 0:512]))
                        e.wait_ge(p.act.h, p.act.val)
                        p.st.inc(e.dma_start(tm_store("kfspill", pr, b),
                                             arb2[:, 0:512]), 16)
                        Plan.aw(e, p.pe, pe0 + PEb * b + 4)
                        p.act.inc(e.copy(arb2[:, 512:1024], p.pbt[:, 512:1024]))
                        e.wait_ge(p.act.h, p.act.val)
                        p.st.inc(e.dma_start(tm_store("kfspill", pr + 8, b),
                                             arb2[:, 512:1024]), 16)
                    else:
                        p.st.inc(e.dma_start(
                            dr[fsp].ap()[128 * pr:128 * (pr + 1),
                                         512 * b:512 * (b + 1)],
                            smb[:, 1024:1536]), 16)
                        Plan.aw(e, p.dve, dve0 + 14 * b + 14)
                        p.st.inc(e.dma_start(
                            dr[fsp].ap()[128 * (pr + 8):128 * (pr + 9),
                                         512 * b:512 * (b + 1)],
                            smb[:, 1536:2048]), 16)
            p.blk.scalar(c2_act)
            assert p.act.val == act0 + ACTb * NB
            assert p.st.val == st0 + 16 * 4 * NB

    if upto in ("attn_v_slow", "attn_v_slowb"):
        pe_s, dve_s, act_s, st_s = p.pe.val, p.dve.val, p.act.val, p.st.val
        for ot in range(16):
            c0 = p.cast.val

            def vs_cast(e, ot=ot, c0=c0):
                p.fwait(e, "pe", "dve", "act")
                Plan.aw(e, p.pe, pe_s)
                Plan.aw(e, p.dve, dve_s)
                Plan.aw(e, p.act, act_s)
                Plan.aw(e, p.st, st_s)
                Plan.aw(e, p.cast, c0)
                p.cast.inc(e.dma_start(
                    wbuf[:, 0:2048],
                    dr["wvT"].ap()[128 * ot:128 * (ot + 1), :]), 16)
                e.wait_ge(p.cast.h, c0 + 16)
            p.blk.gpsimd(vs_cast)
            p.fence()
            for b in range(NB):
                def s_mm(e, ot=ot, b=b, c0=c0):
                    Plan.aw(e, p.cast, c0 + 16)
                    for c in range(DT):
                        mm = e.matmul(p.pb[0][:, :],
                                      wbuf[:, 128 * c:128 * (c + 1)],
                                      xn(c, b), start=(c == 0),
                                      stop=(c == DT - 1))
                    return mm

                def s_vbf(e):
                    return e.tensor_scalar_add(smb[:, 0:512], p.pb[0][:, :],
                                               0.0)

                def s_tr(e):
                    for j in range(4):
                        mm = e.transpose(p.pbt[:, 128 * j:128 * (j + 1)],
                                         smb[:, 128 * j:128 * (j + 1)],
                                         cb[:, CB_IDN:CB_IDN + 128])
                    return mm

                st_pre = p.st.val

                def s_ev(e, st_pre=st_pre):
                    Plan.aw(e, p.st, st_pre)
                    return e.tensor_copy(smb[:, 1024:1536], p.pbt[:, 0:512])

                def s_st(e, ot=ot, b=b):
                    p.st.inc(e.dma_start(tm_store("vtok", ot, b),
                                         smb[:, 1024:1536]), 16)
                    p.st.inc(e.dma_start(
                        dr["combspill"].ap()[128 * ot:128 * (ot + 1),
                                             512 * b:512 * (b + 1)],
                        smb[:, 0:512]), 16)
                    return e.activation(p.smf[0:1, 0:1], p.smf[0:1, 0:1],
                                        AF.Copy)
                p.chain([("tensor", s_mm), ("vector", s_vbf), ("tensor", s_tr),
                         ("vector", s_ev), ("scalar", s_st)])
        dump_dram("vtok" if upto == "attn_v_slow" else "combspill")
        return

    castV0, peV0, stV0 = p.cast.val, p.pe.val, p.st.val
    for ot in range(16):
        cast0, pe0, dve0, act0, st0 = (p.cast.val, p.pe.val, p.dve.val,
                                       p.act.val, p.st.val)

        def v_cast(e, ot=ot, cast0=cast0):
            Plan.aw(e, p.cast, cast0)
            Plan.aw(e, p.pe, peV0 + 8 * max(ot - 1, 0))
            p.cast.inc(e.dma_start(
                wbuf[:, 2048 * (ot % 2):2048 * (ot % 2 + 1)],
                dr["wvT"].ap()[128 * ot:128 * (ot + 1), :]), 16)
        p.blk.gpsimd(v_cast)

        def v_pe(e, cast0=cast0, dve0=dve0, ot=ot):
            Plan.aw(e, p.cast, cast0 + 16)
            for b in range(NB):
                Plan.aw(e, p.dve, max(dve0 + 2 * b - 3, 0))
                for c in range(DT):
                    mm = e.matmul(
                        p.pb[b % 2][:, :],
                        wbuf[:, 2048 * (ot % 2) + 128 * c:
                             2048 * (ot % 2) + 128 * (c + 1)],
                        xn(c, b), start=(c == 0), stop=(c == DT - 1))
                p.pe.inc(mm)
                Plan.aw(e, p.dve, dve0 + 2 * b + 1)
                for j in range(4):
                    mm = e.transpose(
                        p.pbt[:, 512 * (b % 2) + 128 * j:
                              512 * (b % 2) + 128 * (j + 1)],
                        smb[:, 512 * (b % 2) + 128 * j:
                            512 * (b % 2) + 128 * (j + 1)],
                        cb[:, CB_IDN:CB_IDN + 128])
                p.pe.inc(mm)
        p.blk.tensor(v_pe)
        assert p.pe.val == pe0 + 2 * NB

        def v_dve(e, pe0=pe0, dve0=dve0, ot=ot):
            if ot == 0:
                Plan.aw(e, p.st, stV0)
            for b in range(NB):
                Plan.aw(e, p.pe, pe0 + 2 * b + 1)
                p.dve.inc(e.tensor_copy(smb[:, 512 * (b % 2):512 * (b % 2 + 1)],
                                        p.pb[b % 2][:, :]))
                Plan.aw(e, p.pe, pe0 + 2 * b + 2)
                Plan.aw(e, p.st, stV0 + 16 * max(4 * ot + b - 1, 0))
                p.dve.inc(e.tensor_copy(smb[:, 1024 + 512 * (b % 2):
                                            1024 + 512 * (b % 2 + 1)],
                                        p.pbt[:, 512 * (b % 2):
                                              512 * (b % 2 + 1)]))
        p.blk.vector(v_dve)
        assert p.dve.val == dve0 + 2 * NB

        def v_act(e, dve0=dve0, ot=ot):
            for b in range(NB):
                Plan.aw(e, p.dve, dve0 + 2 * b + 2)
                p.st.inc(e.dma_start(tm_store("vtok", ot, b),
                                     smb[:, 1024 + 512 * (b % 2):
                                         1024 + 512 * (b % 2 + 1)]), 16)
        p.blk.scalar(v_act)
        assert p.st.val == st0 + 16 * NB

    pe_c2_end = p.pe.val
    st_c2 = p.st.val
    if upto in ("attn_q", "attn_k", "attn_qf", "attn_kft", "attn_v"):
        dump_dram({"attn_q": "qspill", "attn_k": "kspill",
                   "attn_qf": "qfspill", "attn_kft": "kfspill",
                   "attn_v": "vtok"}[upto])
        return

    QH, KH, QFH, KFT, VH = 0, 2048, 4096, 6144, 8192
    MEMB, NORMB = 10240, 10368
    for h in range(16):
        ld0, pe0, dve0, act0, st0 = (p.ld.val, p.pe.val, p.dve.val,
                                     p.act.val, p.st.val)

        def h_sync(e, h=h, ld0=ld0, pe0=pe0):
            Plan.aw(e, p.ld, ld0)
            Plan.aw(e, p.pe, pe0)
            Plan.aw(e, p.st, st_c2)
            p.ld.inc(e.dma_start(arb2[:, QH:QH + 2048],
                                 dr["qspill"].ap()[128 * h:128 * (h + 1), :]), 16)
            p.ld.inc(e.dma_start(arb2[:, KH:KH + 2048],
                                 dr["kspill"].ap()[128 * h:128 * (h + 1), :]), 16)
            p.ld.inc(e.dma_start(arb2[:, QFH:QFH + 2048],
                                 dr["qfspill"].ap()[128 * h:128 * (h + 1), :]), 16)
            p.ld.inc(e.dma_start(arb2[:, KFT:KFT + 2048],
                                 tm_load("kfspill", h)), 16)
            p.ld.inc(e.dma_start(arb2[:, VH:VH + 2048],
                                 tm_load("vtok", h)), 16)
        p.blk.sync(h_sync)
        assert p.ld.val == ld0 + 80

        def h_pe(e, ld0=ld0, pe0=pe0, dve0=dve0, act0=act0):
            Plan.aw(e, p.ld, ld0 + 80)
            Plan.aw(e, p.dve, dve0)
            Plan.aw(e, p.act, act0)
            for c in range(DT):
                mm = e.matmul(p.pb[2][:, 0:128],
                              arb2[:, KFT + 128 * c:KFT + 128 * (c + 1)],
                              arb2[:, VH + 128 * c:VH + 128 * (c + 1)],
                              start=(c == 0), stop=(c == DT - 1))
            p.pe.inc(mm)
            for c in range(DT):
                mm = e.matmul(p.pb[3][:, 0:1],
                              arb2[:, KFT + 128 * c:KFT + 128 * (c + 1)],
                              cb[:, CB_ONES_COL:CB_ONES_COL + 1],
                              start=(c == 0), stop=(c == DT - 1))
            p.pe.inc(mm)
            for qb in range(4):
                dve_q0 = dve0 + 2 + 9 * qb
                if qb == 0:
                    Plan.aw(e, p.dve, dve0 + 2)
                else:
                    Plan.aw(e, p.dve, dve_q0 - 9 + 6)
                    Plan.aw(e, p.act, act0 + 16 * qb)
                p.pe.inc(e.matmul(p.pb[4][:, :], arb2[:, MEMB:MEMB + 128],
                                  arb2[:, QFH + 512 * qb:QFH + 512 * (qb + 1)],
                                  start=True, stop=True))
                p.pe.inc(e.matmul(p.pb[5][0:1, :], arb2[:, NORMB:NORMB + 1],
                                  arb2[:, QFH + 512 * qb:QFH + 512 * (qb + 1)],
                                  start=True, stop=True))
                for kc in range(16):
                    Plan.aw(e, p.act, act0 + 16 * qb + max(kc - 1, 0))
                    p.pe.inc(e.matmul(
                        p.pb[kc % 2][:, :],
                        arb2[:, KH + 128 * kc:KH + 128 * (kc + 1)],
                        arb2[:, QH + 512 * qb:QH + 512 * (qb + 1)],
                        start=True, stop=True))
                for kc in range(16):
                    Plan.aw(e, p.act, act0 + 16 * qb + kc + 1)
                    pt = arb1[:, 8192 * (qb % 2) + 512 * kc:
                              8192 * (qb % 2) + 512 * (kc + 1)]
                    mmv = e.matmul(p.pb[6][:, :],
                                   arb2[:, VH + 128 * kc:VH + 128 * (kc + 1)],
                                   pt, start=(kc == 0), stop=(kc == 15))
                    mmd = e.matmul(p.pb[3][0:1, :],
                                   cb[:, CB_ONES_COL:CB_ONES_COL + 1],
                                   pt, start=(kc == 0), stop=(kc == 15))
                p.pe.inc(mmv)
                p.pe.inc(mmd)
                Plan.aw(e, p.act, act0 + 16 * (qb + 1))
                Plan.aw(e, p.dve, dve_q0 + 1)
                p.pe.inc(e.matmul(p.pb[0][:, :],
                                  cf[0:1, CF_ONES_ROW:CF_ONES_ROW + 128],
                                  p.rows[0:1, 0:512], start=True, stop=True))
                Plan.aw(e, p.dve, dve_q0 + 2)
                p.pe.inc(e.matmul(p.pb[1][:, :],
                                  cf[0:1, CF_ONES_ROW:CF_ONES_ROW + 128],
                                  p.rows[0:1, 512:1024], start=True, stop=True))
        p.blk.tensor(h_pe)
        assert p.pe.val == pe0 + 90

        def h_act(e, pe0=pe0, dve0=dve0, h=h):
            for qb in range(4):
                pe_q0 = pe0 + 2 + 22 * qb
                for kc in range(16):
                    Plan.aw(e, p.pe, pe_q0 + 3 + kc)
                    p.act.inc(e.activation(
                        arb1[:, 8192 * (qb % 2) + 512 * kc:
                             8192 * (qb % 2) + 512 * (kc + 1)],
                        p.pb[kc % 2][:, :], AF.Exp, scale=SC))
                Plan.aw(e, p.dve, dve0 + 2 + 9 * qb + 9)
                p.st.inc(e.dma_start(
                    dr["combspill"].ap()[128 * h:128 * (h + 1),
                                         512 * qb:512 * (qb + 1)],
                    smb[:, 512 * (qb % 2):512 * (qb % 2 + 1)]), 16)
        p.blk.scalar(h_act)
        assert p.act.val == act0 + 64
        assert p.st.val == st0 + 64

        def h_dve(e, pe0=pe0, dve0=dve0, st0=st0, act0=act0, h=h):
            Plan.aw(e, p.act, act0)
            Plan.aw(e, p.st, st_c2)
            Plan.aw(e, p.pe, pe0 + 1)
            p.dve.inc(e.tensor_copy(arb2[:, MEMB:MEMB + 128], p.pb[2][:, 0:128]))
            Plan.aw(e, p.pe, pe0 + 2)
            p.dve.inc(e.tensor_copy(arb2[:, NORMB:NORMB + 1], p.pb[3][:, 0:1]))
            for qb in range(4):
                pe_q0 = pe0 + 2 + 22 * qb
                Plan.aw(e, p.pe, pe_q0 + 20)
                p.dve.inc(e.reciprocal(p.rows[0:1, 0:512], p.pb[3][0:1, :]))
                p.dve.inc(e.reciprocal(p.rows[0:1, 512:1024], p.pb[5][0:1, :]))
                Plan.aw(e, p.pe, pe_q0 + 21)
                p.dve.inc(e.tensor_copy(smf2[:, 0:512], p.pb[0][:, :]))
                Plan.aw(e, p.pe, pe_q0 + 22)
                p.dve.inc(e.tensor_copy(smf2[:, 512:1024], p.pb[1][:, :]))
                p.dve.inc(e.tensor_mul(smf[:, 0:512], p.pb[4][:, :],
                                       smf2[:, 512:1024]))
                p.dve.inc(e.tensor_mul(smf[:, 512:1024], p.pb[6][:, :],
                                       smf2[:, 0:512]))
                p.dve.inc(e.tensor_sub(smf2[:, 0:512], smf[:, 0:512],
                                       smf[:, 512:1024]))
                p.dve.inc(e.tensor_scalar(
                    smf2[:, 512:1024], smf2[:, 0:512],
                    cf[:, CF_GREP + h:CF_GREP + h + 1], None, ALU.mult))
                if qb >= 2:
                    Plan.aw(e, p.st, st0 + 16 * (qb - 1))
                p.dve.inc(e.tensor_add(smb[:, 512 * (qb % 2):512 * (qb % 2 + 1)],
                                       smf2[:, 512:1024], smf[:, 512:1024]))
        p.blk.vector(h_dve)
        assert p.dve.val == dve0 + 38

    st_c3 = p.st.val
    pe_c3 = p.pe.val
    if upto == "attn_comb":
        dump_dram("combspill")
        return

    cast0, ld0, pe0, dve0, act0, st0 = (p.cast.val, p.ld.val, p.pe.val,
                                        p.dve.val, p.act.val, p.st.val)

    woW = lambda dt_: (16384 + 2048 * dt_) if dt_ < 8 else 2048 * (dt_ - 8)

    def wo_cast(e):
        Plan.aw(e, p.cast, cast0)
        Plan.aw(e, p.pe, pe_c2_end)
        for dt_ in range(16):
            if dt_ == 8:
                Plan.aw(e, p.pe, pe0)
            p.cast.inc(e.dma_start(
                arb1[:, woW(dt_):woW(dt_) + 2048],
                dr["woT"].ap()[128 * dt_:128 * (dt_ + 1), :]), 16)
    p.blk.gpsimd(wo_cast)
    assert p.cast.val == cast0 + 256

    def wo_sync(e):
        Plan.aw(e, p.ld, ld0)
        Plan.aw(e, p.act, act0)
        for b in range(NB):
            if b == 0:
                Plan.aw(e, p.st, st_c3)
                Plan.aw(e, p.pe, pe0)
            elif b >= 2:
                Plan.aw(e, p.pe, pe0 + 16 * (b - 1))
            for c in range(16):
                p.ld.inc(e.dma_start(
                    arb2[:, 8192 * (b % 2) + 512 * c:
                         8192 * (b % 2) + 512 * (c + 1)],
                    dr["combspill"].ap()[128 * c:128 * (c + 1),
                                         512 * b:512 * (b + 1)]), 16)
            for dt_ in range(16):
                i = 16 * b + dt_
                if i >= 3:
                    Plan.aw(e, p.dve, dve0 + i - 2)
                p.ld.inc(e.dma_start(
                    xb[:, 512 * (i % 3):512 * (i % 3 + 1)],
                    src.ap()[128 * dt_:128 * (dt_ + 1),
                             512 * b:512 * (b + 1)]), 16)
    p.blk.sync(wo_sync)
    assert p.ld.val == ld0 + 16 * 32 * NB

    def wo_pe(e):
        for b in range(NB):
            Plan.dwait(e, p.ld, ld0, min(32 * b + 16, 32 * (b + 1)), 32 * NB)
            for dt_ in range(16):
                i = 16 * b + dt_
                Plan.dwait(e, p.cast, cast0, dt_ + 1, 16)
                if i >= 2:
                    Plan.aw(e, p.dve, dve0 + i - 1)
                else:
                    Plan.aw(e, p.dve, dve0)
                for c in range(16):
                    mm = e.matmul(
                        p.pb[i % 2][:, :],
                        arb1[:, woW(dt_) + 128 * c:woW(dt_) + 128 * (c + 1)],
                        arb2[:, 8192 * (b % 2) + 512 * c:
                             8192 * (b % 2) + 512 * (c + 1)],
                        start=(c == 0), stop=(c == 15))
                p.pe.inc(mm)
    p.blk.tensor(wo_pe)
    assert p.pe.val == pe0 + 16 * NB

    def wo_dve(e):
        for b in range(NB):
            for dt_ in range(16):
                i = 16 * b + dt_
                Plan.aw(e, p.pe, pe0 + i + 1)
                Plan.dwait(e, p.ld, ld0, min(32 * b + 16 + dt_ + 1,
                                             32 * (b + 1)), 32 * NB)
                if i >= 2:
                    Plan.aw(e, p.st, st0 + 16 * (i - 1))
                p.dve.inc(e.tensor_add(smf2[:, 512 * (i % 2):512 * (i % 2 + 1)],
                                       p.pb[i % 2][:, :],
                                       xb[:, 512 * (i % 3):512 * (i % 3 + 1)]))
    p.blk.vector(wo_dve)
    assert p.dve.val == dve0 + 16 * NB

    def wo_act(e):
        for b in range(NB):
            for dt_ in range(16):
                i = 16 * b + dt_
                Plan.aw(e, p.dve, dve0 + i + 1)
                p.st.inc(e.dma_start(
                    dst.ap()[128 * dt_:128 * (dt_ + 1),
                             512 * b:512 * (b + 1)],
                    smf2[:, 512 * (i % 2):512 * (i % 2 + 1)]), 16)
    p.blk.scalar(wo_act)
    assert p.st.val == st0 + 16 * 16 * NB
    p.fence()



def emit_dequant(p, dr):
    dqb = p.wbuf
    dqi = p.wbuf.bitcast(I32)
    sfb = p.wbuf.bitcast(F16)
    SF32a = p.wbuf.bitcast(F32)
    X = lambda pi, W: dqb[:, 1024 * pi:1024 * pi + W]
    B1 = lambda pi, W: dqb[:, 2048 + 1024 * pi:2048 + 1024 * pi + W]
    B2 = lambda pi, W: dqb[:, 4096 + 1024 * pi:4096 + 1024 * pi + W]
    B3 = lambda pi, W: dqb[:, 6144 + 1024 * pi:6144 + 1024 * pi + W]
    OUT = B2
    SF32 = lambda pi: SF32a[:, 5504 + 44 * pi:5504 + 44 * pi + 44]
    ASM = lambda pi, W: p.arb2[:, 1024 * pi:1024 * pi + W]
    pb5b = p.pb[5].bitcast(BF16)
    DQ_PB5 = False
    PBT = lambda pi, W: (p.pbt[:, 0:W] if (pi == 0 or not DQ_PB5)
                         else pb5b[:, 0:W])

    slabs = []
    for o in range(88):
        for q in range(2):
            slabs.append(("gu", o, q, 1024, 128, 8))
    for o in range(16):
        for q in range(11):
            slabs.append(("dn", o, q, 512, 64, 4))
    N = len(slabs)

    ld0, act0, dve0, pe0, st0, gp0 = (p.ld.val, p.act.val, p.dve.val,
                                      p.pe.val, p.st.val, p.gp.val)
    pstep_i = dqi.ap().ap[0][0]
    pstep_b = dqb.ap().ap[0][0]

    nibs_v = [dve0 + 3 * s + 1 for s in range(N)]
    stt_v = [dve0 + 3 * s + 2 for s in range(N)]
    scale_v = [dve0 + 3 * s + 3 for s in range(N)]
    transp_v = [pe0 + s + 1 for s in range(N)]
    store_v = [st0 + 16 * (s + 1) for s in range(N)]
    conv_v = [0] * N
    relu_v = [0] * N
    evac_v = [0] * N
    a = act0
    for s in range(N + 2):
        if s < N:
            a += 1
            conv_v[s] = a
            a += 1
            relu_v[s] = a
        if s >= 2:
            a += 1
            evac_v[s - 2] = a
    act_end = a

    def d_sync(e):
        p.fwait(e, "pe", "dve", "act")
        Plan.aw(e, p.ld, ld0)
        for s, (kind, o, q, W, nw, ng) in enumerate(slabs):
            if s >= 2:
                Plan.aw(e, p.dve, nibs_v[s - 2])
            if kind == "gu":
                psrc = dr["gu_packed"].ap()[128 * o:128 * (o + 1),
                                            128 * q:128 * q + nw]
                ssrc = dr["gu_scales_t"].ap()[128 * o:128 * (o + 1), :]
                nsc = 16
            else:
                psrc = dr["dn_packed"].ap()[128 * o:128 * (o + 1),
                                            64 * q:64 * q + nw]
                ssrc = dr["dn_scales_t"].ap()[128 * o:128 * (o + 1), :]
                nsc = 44
            p.ld.inc(e.dma_start(
                dqi[:, 4096 + 128 * (s % 2):4096 + 128 * (s % 2) + nw],
                psrc), 16)
            if s >= 1:
                Plan.aw(e, p.act, conv_v[s - 1])
            p.ld.inc(e.dma_start(sfb[:, 11184:11184 + nsc], ssrc), 16)
    p.blk.sync(d_sync)
    assert p.ld.val == ld0 + 32 * N, ("cnt", "ld", p.ld.val, ld0 + 32 * N)

    def d_dve(e):
        for s, (kind, o, q, W, nw, ng) in enumerate(slabs):
            pi = s % 2
            Plan.dwait(e, p.ld, ld0, 2 * s + 1, 2 * N)
            if s >= 1:
                Plan.aw(e, p.act, conv_v[s - 1])
            for k in range(8):
                last = e.tensor_scalar(dqi[:, 4480 + k * nw:4480 + (k + 1) * nw],
                                       dqi[:, 4096 + 128 * pi:
                                            4096 + 128 * pi + nw],
                                       4 * k, 15, ALU.logical_shift_right,
                                       ALU.bitwise_and)
            p.dve.inc(last)
            Plan.aw(e, p.act, conv_v[s])
            e.tensor_scalar(B1(pi, W), X(pi, W), 8.0, None, ALU.is_ge)
            p.dve.inc(e.scalar_tensor_tensor(X(pi, W), B1(pi, W), -8.0,
                                             X(pi, W), ALU.mult, ALU.add))
            e.tensor_scalar_mul(B1(pi, W), B1(pi, W), -1.0)
            e.tensor_scalar_add(B1(pi, W), B1(pi, W), 0.5)
            Plan.aw(e, p.act, relu_v[s])
            e.tensor_add(X(pi, W), X(pi, W), B2(pi, W))
            e.tensor_add(X(pi, W), X(pi, W), B3(pi, W))
            e.tensor_mul(X(pi, W), X(pi, W), B1(pi, W))
            if s >= 2:
                Plan.aw(e, p.pe, transp_v[s - 2])
            for g in range(ng):
                sidx = (8 * q + g) if kind == "gu" else (4 * q + g)
                last = e.tensor_scalar(OUT(pi, W)[:, 128 * g:128 * (g + 1)],
                                       X(pi, W)[:, 128 * g:128 * (g + 1)],
                                       SF32(pi)[:, sidx:sidx + 1], None,
                                       ALU.mult)
            p.dve.inc(last)
    p.blk.vector(d_dve)
    assert p.dve.val == dve0 + 3 * N, ("cnt", "dve", p.dve.val, dve0 + 3 * N)

    nibsrc = lambda nw: bass.AP(dqi.ap().tensor, dqi.ap().offset + 4480,
                                [[pstep_i, 128], [nw, 8], [1, nw]])
    xdst = lambda pi, nw: bass.AP(dqb.ap().tensor, dqb.ap().offset + 1024 * pi,
                                  [[pstep_b, 128], [1, 8], [8, nw]])

    def d_act(e):
        for s in range(N + 2):
            if s < N:
                kind, o, q, W, nw, ng = slabs[s]
                pi = s % 2
                nsc = 16 if kind == "gu" else 44
                Plan.dwait(e, p.ld, ld0, 2 * s + 2, 2 * N)
                if s >= 2:
                    Plan.aw(e, p.dve, scale_v[s - 2])
                e.copy(SF32(pi)[:, 0:nsc], sfb[:, 11184:11184 + nsc])
                Plan.aw(e, p.dve, nibs_v[s])
                p.act.inc(e.copy(xdst(pi, nw), nibsrc(nw)))
                Plan.aw(e, p.dve, stt_v[s])
                if s >= 2:
                    Plan.aw(e, p.pe, transp_v[s - 2])
                e.activation(B2(pi, W), X(pi, W), AF.Relu, bias=-4.0)
                p.act.inc(e.activation(B3(pi, W), X(pi, W), AF.Relu,
                                       scale=2.0, bias=-12.0))
            if s >= 2:
                k = s - 2
                kindk, ok, qk, Wk, nwk, ngk = slabs[k]
                pik = k % 2
                Plan.aw(e, p.pe, transp_v[k])
                if k >= 2:
                    Plan.aw(e, p.st, store_v[k - 2])
                p.act.inc(e.copy(ASM(pik, Wk), PBT(pik, Wk)))
                e.wait_ge(p.act.h, evac_v[k])
                if kindk == "gu":
                    dstp = dr["wgu_scr"].ap()[128 * ok:128 * (ok + 1),
                                              1024 * qk:1024 * qk + Wk]
                else:
                    dstp = dr["wdn_scr"].ap()[128 * ok:128 * (ok + 1),
                                              512 * qk:512 * qk + Wk]
                p.st.inc(e.dma_start(dstp, ASM(pik, Wk)), 16)
    p.blk.scalar(d_act)
    assert p.act.val == act_end, ("cnt", "act", p.act.val, act_end)
    assert p.st.val == st0 + 16 * N, ("cnt", "st", p.st.val, st0 + 16 * N)

    def d_pe(e):
        p.fwait(e, "dve", "act")
        for s, (kind, o, q, W, nw, ng) in enumerate(slabs):
            pi = s % 2
            Plan.aw(e, p.dve, scale_v[s])
            if DQ_PB5 and s >= 2:
                Plan.aw(e, p.act, evac_v[s - 2])
            elif not DQ_PB5 and s >= 1:
                Plan.aw(e, p.act, evac_v[s - 1])
            for c in range(ng):
                mm = e.transpose(PBT(pi, W)[:, 128 * c:128 * (c + 1)],
                                 OUT(pi, W)[:, 128 * c:128 * (c + 1)],
                                 p.cbuf[:, CB_IDN:CB_IDN + 128])
            p.pe.inc(mm)
    p.blk.tensor(d_pe)
    assert p.pe.val == pe0 + N, ("cnt", "pe", p.pe.val, pe0 + N)
    p.fence()



def emit_ffn_l1(p, dr, src, dst, dequant):
    arb1, arf, wbuf = p.arb1, p.arf, p.wbuf
    if dequant:
        emit_dequant(p, dr)

    emit_ln(p, src, arb1, 0)
    xn = lambda c, b: arb1[:, 2048 * c + 512 * b:2048 * c + 512 * (b + 1)]

    NP = 44
    ld0, pe0, act0, dve0, st0 = (p.ld.val, p.pe.val, p.act.val, p.dve.val,
                                 p.st.val)

    def da_sync(e):
        p.fwait(e, "pe", "dve")
        Plan.aw(e, p.ld, ld0)
        Plan.aw(e, p.st, st0)
        for pp in range(NP):
            Plan.aw(e, p.pe, pe0 + 4 * (pp - 1))
            w0 = 4096 * (pp % 2)
            p.ld.inc(e.dma_start(
                wbuf[:, w0:w0 + 2048],
                dr["wgu_scr"].ap()[128 * pp:128 * (pp + 1), :]), 16)
            p.ld.inc(e.dma_start(
                wbuf[:, w0 + 2048:w0 + 4096],
                dr["wgu_scr"].ap()[128 * (pp + NP):128 * (pp + NP + 1), :]), 16)
    p.blk.sync(da_sync)
    assert p.ld.val == ld0 + 32 * NP, ("cnt", "ld", p.ld.val, ld0 + 32 * NP)

    def da_pe(e):
        p.fwait(e, "dve", "act")
        for pp in range(NP):
            Plan.dwait(e, p.ld, ld0, 2 * pp + 4, 2 * NP)
            w0 = 4096 * (pp % 2)
            for b in range(NB):
                i = pp * NB + b
                Plan.aw(e, p.dve, dve0 + i - 1)
                pg = p.pb[2 * (i % 2)]
                pu = p.pb[2 * (i % 2) + 1]
                for c in range(DT):
                    e.matmul(pg[:, :], wbuf[:, w0 + 128 * c:w0 + 128 * (c + 1)],
                             xn(c, b), start=(c == 0), stop=(c == DT - 1))
                for c in range(DT):
                    mm = e.matmul(pu[:, :],
                                  wbuf[:, w0 + 2048 + 128 * c:w0 + 2048 + 128 * (c + 1)],
                                  xn(c, b), start=(c == 0), stop=(c == DT - 1))
                p.pe.inc(mm)
    p.blk.tensor(da_pe)
    assert p.pe.val == pe0 + 4 * NP, ("cnt", "pe", p.pe.val, pe0 + 4 * NP)

    def da_act(e):
        for i in range(4 * NP):
            Plan.aw(e, p.pe, pe0 + i + 1)
            Plan.aw(e, p.dve, dve0 + i - 1)
            sg = p.smf[:, 512 * (i % 2):512 * (i % 2 + 1)]
            p.act.inc(e.activation(sg, p.pb[2 * (i % 2)][:, :], AF.Silu))
            if i > 0:
                pp1, b1 = divmod(i - 1, NB)
                Plan.aw(e, p.dve, dve0 + i)
                p.st.inc(e.dma_start(
                    dr["hbuf"].ap()[128 * pp1:128 * (pp1 + 1),
                                    512 * b1:512 * (b1 + 1)],
                    p.smb[:, 512 * ((i - 1) % 4):512 * ((i - 1) % 4 + 1)]), 16)
        pp1, b1 = divmod(4 * NP - 1, NB)
        Plan.aw(e, p.dve, dve0 + 4 * NP)
        p.st.inc(e.dma_start(
            dr["hbuf"].ap()[128 * pp1:128 * (pp1 + 1), 512 * b1:512 * (b1 + 1)],
            p.smb[:, 512 * ((4 * NP - 1) % 4):512 * ((4 * NP - 1) % 4 + 1)]), 16)
    p.blk.scalar(da_act)
    assert p.act.val == act0 + 4 * NP, ("cnt", "act", p.act.val, act0 + 4 * NP)
    assert p.st.val == st0 + 16 * 4 * NP, ("cnt", "st", p.st.val, st0 + 16 * 4 * NP)

    def da_dve(e):
        for i in range(4 * NP):
            Plan.aw(e, p.act, act0 + i + 1)
            Plan.aw(e, p.st, st0 + 16 * (i - 3))
            sg = p.smf[:, 512 * (i % 2):512 * (i % 2 + 1)]
            hh = p.smb[:, 512 * (i % 4):512 * (i % 4 + 1)]
            p.dve.inc(e.tensor_mul(hh, p.pb[2 * (i % 2) + 1][:, :], sg))
    p.blk.vector(da_dve)
    assert p.dve.val == dve0 + 4 * NP, ("cnt", "dve", p.dve.val, dve0 + 4 * NP)
    p.fence()

    NI = 44
    ld0, pe0, act0, dve0, st0 = (p.ld.val, p.pe.val, p.act.val, p.dve.val,
                                 p.st.val)
    r_rms = p.rows[0:1, 0:512]
    rrep = p.smf[:, 0:512]

    PE_B, DVE_B, ACT_B = 33, 34, 17

    def db_sync(e):
        p.fwait(e, "pe", "dve")
        Plan.aw(e, p.ld, ld0)
        Plan.aw(e, p.st, st0)
        for b in range(NB):
            Plan.aw(e, p.pe, pe0 + PE_B * b - 1 if b > 0 else 0)
            for pp in range(NI):
                p.ld.inc(e.dma_start(
                    arb1[:, 512 * pp:512 * (pp + 1)],
                    dr["hbuf"].ap()[128 * pp:128 * (pp + 1),
                                    512 * b:512 * (b + 1)]), 16)
            for ot in range(DT):
                if ot >= 3:
                    Plan.aw(e, p.dve, dve0 + DVE_B * b + 18 + ot - 2)
                elif b > 0:
                    Plan.aw(e, p.dve, dve0 + DVE_B * b)
                p.ld.inc(e.dma_start(
                    p.xbuf[:, 512 * (ot % 3):512 * (ot % 3 + 1)],
                    src.ap()[128 * ot:128 * (ot + 1), BLK * b:BLK * (b + 1)]), 16)
    p.blk.sync(db_sync)
    assert p.ld.val == ld0 + 16 * 60 * NB, ("cnt", "ld", p.ld.val, ld0 + 16 * 60 * NB)

    cast0 = p.cast.val

    def db_cast(e):
        Plan.aw(e, p.cast, cast0)
        Plan.aw(e, p.pe, pe0)
        for b in range(NB):
            for ot in range(DT):
                i = b * DT + ot
                if i >= 2:
                    b2, ot2 = divmod(i - 2, DT)
                    Plan.aw(e, p.pe, pe0 + PE_B * b2 +
                            (1 if ot2 == 0 else 2 * ot2))
                p.cast.inc(e.dma_start(
                    wbuf[:, 5632 * (i % 2):5632 * (i % 2) + 5632],
                    dr["wdn_scr"].ap()[128 * ot:128 * (ot + 1), :]), 16)
    p.blk.gpsimd(db_cast)
    assert p.cast.val == cast0 + 16 * 4 * DT, ("cnt", "cast", p.cast.val, cast0 + 16 * 4 * DT)

    def db_pe(e):
        p.fwait(e, "dve", "act")
        for b in range(NB):
            Plan.dwait(e, p.ld, ld0, 60 * b + 44, 60 * NB)
            for ot in range(DT):
                i = b * DT + ot
                Plan.aw(e, p.cast, cast0 + 16 * min(i + 2, 4 * DT))
                Plan.aw(e, p.dve, dve0 + DVE_B * b + ot - 1)
                pbk = p.pb[i % 2]
                for c in range(NI):
                    mm = e.matmul(pbk[:, :],
                                  wbuf[:, 5632 * (i % 2) + 128 * c:
                                       5632 * (i % 2) + 128 * (c + 1)],
                                  arb1[:, 512 * c:512 * (c + 1)],
                                  start=(c == 0), stop=(c == NI - 1))
                p.pe.inc(mm)
                if ot > 0:
                    Plan.aw(e, p.act, act0 + ACT_B * b + ot)
                    p.pe.inc(e.matmul(p.pb[6][0:1, :],
                                      p.cbuf[:, CB_ONES_COL:CB_ONES_COL + 1],
                                      p.smb[:, 512 * ((ot - 1) % 2):
                                            512 * ((ot - 1) % 2 + 1)],
                                      start=(ot - 1 == 0), stop=False))
            Plan.aw(e, p.act, act0 + ACT_B * b + 16)
            p.pe.inc(e.matmul(p.pb[6][0:1, :],
                              p.cbuf[:, CB_ONES_COL:CB_ONES_COL + 1],
                              p.smb[:, 512 * (15 % 2):512 * (15 % 2 + 1)],
                              start=False, stop=True))
            Plan.aw(e, p.act, act0 + ACT_B * b + 17)
            p.pe.inc(e.matmul(p.pb[5][:, :],
                              p.cbuf_f[0:1, CF_ONES_ROW:CF_ONES_ROW + 128],
                              r_rms, start=True, stop=True))
    p.blk.tensor(db_pe)
    assert p.pe.val == pe0 + PE_B * NB, ("cnt", "pe", p.pe.val, pe0 + PE_B * NB)

    def db_dve(e):
        for b in range(NB):
            for ot in range(DT):
                i = b * DT + ot
                Plan.aw(e, p.pe, pe0 + PE_B * b + 2 * ot + 1 - (1 if ot > 0 else 0))
                p.dve.inc(e.tensor_copy(arf[:, 512 * ot:512 * (ot + 1)],
                                        p.pb[i % 2][:, :]))
            Plan.aw(e, p.pe, pe0 + PE_B * b + 32)
            e.tensor_scalar_mul(r_rms, p.pb[6][0:1, :], 1.0 / D)
            p.dve.inc(e.tensor_scalar_add(r_rms, r_rms, float(RMS_EPS_L1FFN)))
            Plan.aw(e, p.pe, pe0 + PE_B * b + 33)
            p.dve.inc(e.tensor_copy(rrep, p.pb[5][:, :]))
            for ot in range(DT):
                Plan.dwait(e, p.ld, ld0, min(60 * b + 44 + ot + 3, 60 * (b + 1)), 60 * NB)
                Plan.aw(e, p.st, st0 + 16 * (b * DT + ot - 1))
                t2 = p.smf[:, 512:1024]
                out = p.smf2[:, 512 * (ot % 2):512 * (ot % 2 + 1)]
                e.tensor_mul(t2, arf[:, 512 * ot:512 * (ot + 1)], rrep)
                xsrc = p.xbuf[:, 512 * (ot % 3):512 * (ot % 3 + 1)]
                p.dve.inc(e.scalar_tensor_tensor(out, t2, 0.5, xsrc,
                                                 ALU.mult, ALU.add))
    p.blk.vector(db_dve)
    assert p.dve.val == dve0 + DVE_B * NB, ("cnt", "dve", p.dve.val, dve0 + DVE_B * NB)

    def db_act(e):
        for b in range(NB):
            for ot in range(DT):
                Plan.aw(e, p.dve, dve0 + DVE_B * b + ot + 1)
                ysq = p.smb[:, 512 * (ot % 2):512 * (ot % 2 + 1)]
                if ot >= 2:
                    Plan.aw(e, p.pe, pe0 + PE_B * b + 2 * ot - 1)
                p.act.inc(e.activation(ysq, arf[:, 512 * ot:512 * (ot + 1)],
                                       AF.Square))
            Plan.aw(e, p.dve, dve0 + DVE_B * b + 17)
            e.activation(r_rms, r_rms, AF.Ln)
            p.act.inc(e.activation(r_rms, r_rms, AF.Exp, scale=-0.5))
            for ot in range(DT):
                Plan.aw(e, p.dve, dve0 + DVE_B * b + 18 + ot + 1)
                out = p.smf2[:, 512 * (ot % 2):512 * (ot % 2 + 1)]
                p.st.inc(e.dma_start(
                    dst.ap()[128 * ot:128 * (ot + 1), BLK * b:BLK * (b + 1)],
                    out), 16)
    p.blk.scalar(db_act)
    assert p.act.val == act0 + ACT_B * NB, ("cnt", "act", p.act.val, act0 + ACT_B * NB)
    assert p.st.val == st0 + 16 * DT * NB, ("cnt", "st", p.st.val, st0 + 16 * DT * NB)
    p.fence()


ALL_SUBLAYERS = ["ffn0a", "mamba", "ffn0b", "ffn1a", "attn", "ffn1b"]


def _run_chained(x, weights, groups):
    cur = [np.ascontiguousarray(x[c].T) for c in range(N_CORES)]
    for gi, g in enumerate(groups):
        nc = build_program(g)
        in_maps = []
        for c in range(N_CORES):
            m = {"x_in": cur[c]}
            m.update(weights)
            in_maps.append(m)
        res = run_bass_kernel_spmd(nc, in_maps, core_ids=list(range(N_CORES)))
        cur = [np.asarray(res.results[c]["out"]) for c in range(N_CORES)]
    return np.stack([c.T for c in cur])


def kernel(**inputs):
    x = np.asarray(inputs["L0_attn_x"], np.float32)
    weights = prep_weights(inputs)
    out = _run_chained(x, weights, [ALL_SUBLAYERS])
    return out.astype(np.float32)
